# version 06
# baseline (speedup 1.0000x reference)
import contextlib
import math
import numpy as np
import ml_dtypes
import concourse.bass as bass
import concourse.mybir as mybir
from concourse.bass_utils import run_bass_kernel_spmd
from concourse.aot_env import allow_ici_all_to_all


def _cc_alltoall(e, iap, oap):
    with allow_ici_all_to_all():
        return e.collective_compute(
            "AllToAll", mybir.AluOpType.bypass, replica_groups=[list(range(8))],
            ins=[iap], outs=[oap])

F32 = mybir.dt.float32
BF16 = mybir.dt.bfloat16
AF = mybir.ActivationFunctionType
OP = mybir.AluOpType

B, L, D = 4, 4096, 1024
TT, CH = 512, 64
NTT, NCH = L // TT, TT // CH
NRND = 4
DKI = 1.0 / math.sqrt(128.0)
NIT = 31
_DBG_TILES = None
_DBG_P2 = True
_DBG_DUMP = False
_DBG_CHUNKS = None
_DBG_STAGE = 3
_DBG_PREP = 99
_DBG_BSET = (0, 1, 2, 3)


def build_nc():
    nc = bass.Bass("TRN2", target_bir_lowering=False)

    xfm = nc.dram_tensor("xfm", [B, 8, 128, L], F32, kind="ExternalInput").ap()
    xre = nc.dram_tensor("xre", [8, 128, NRND * 512], F32, kind="ExternalInput").ap()
    w5 = nc.dram_tensor("w5", [128, 40 * 128], F32, kind="ExternalInput").ap()
    wbcol = nc.dram_tensor("wbcol", [128, 8], F32, kind="ExternalInput").ap()
    convw = nc.dram_tensor("convw", [128, 12], F32, kind="ExternalInput").ap()
    convb = nc.dram_tensor("convb", [128, 3], F32, kind="ExternalInput").ap()
    bneg = nc.dram_tensor("bneg", [128, 1], F32, kind="ExternalInput").ap()
    bbv = nc.dram_tensor("bbv", [1, 1], F32, kind="ExternalInput").ap()
    lngb = nc.dram_tensor("lngb", [128, 2], F32, kind="ExternalInput").ap()
    woT = nc.dram_tensor("woT", [8, 128, 8 * 128], F32, kind="ExternalInput").ap()
    w1T = nc.dram_tensor("w1T", [32, 128, 8 * 128], BF16, kind="ExternalInput").ap()
    w2T = nc.dram_tensor("w2T", [32, 128, 8 * 128], BF16, kind="ExternalInput").ap()
    h1T = nc.dram_tensor("h1T", [32, 128, 32 * 128], BF16, kind="ExternalInput").ap()
    h2T = nc.dram_tensor("h2T", [32, 128, 32 * 128], BF16, kind="ExternalInput").ap()
    w3T = nc.dram_tensor("w3T", [8, 128, 32 * 128], BF16, kind="ExternalInput").ap()
    fscl = nc.dram_tensor("fscl", [128, 5 * 32], F32, kind="ExternalInput").ap()
    identb = nc.dram_tensor("identb", [128, 128], BF16, kind="ExternalInput").ap()
    identf = nc.dram_tensor("identf", [128, 128], F32, kind="ExternalInput").ap()
    onecf = nc.dram_tensor("onecf", [128, 1], F32, kind="ExternalInput").ap()
    onerf = nc.dram_tensor("onerf", [1, 128], F32, kind="ExternalInput").ap()
    mk32q = nc.dram_tensor("mk32q", [128, 32], F32, kind="ExternalInput").ap()
    mq0 = nc.dram_tensor("mq0", [128, 64], F32, kind="ExternalInput").ap()
    mq1 = nc.dram_tensor("mq1", [128, 64], F32, kind="ExternalInput").ap()
    ibs = [nc.dram_tensor(f"ib{e}", [8, 128, 512], BF16) for e in range(NRND)]
    obs = [nc.dram_tensor(f"ob{e}", [8, 128, 512], BF16) for e in range(NRND)]
    x1d = nc.dram_tensor("x1d", [128, 8 * 512], F32)
    r2d = nc.dram_tensor("r2d", [128, 8 * 512], F32)
    hd = nc.dram_tensor("hd", [128, 32 * 512], BF16)
    outp = nc.dram_tensor("outp", [8, 128, NRND * 512], F32, kind="ExternalOutput").ap()
    if _DBG_DUMP:
        dbgo = nc.dram_tensor("dbgo", [128, 8 * 512], BF16, kind="ExternalOutput").ap()
        dbgs = nc.dram_tensor("dbgs", [128, B * 128], F32, kind="ExternalOutput").ap()
        dbgq = nc.dram_tensor("dbgq", [128, B * 512], BF16, kind="ExternalOutput").ap()
        dbgk = nc.dram_tensor("dbgk", [128, B * 512], BF16, kind="ExternalOutput").ap()
        dbgv = nc.dram_tensor("dbgv", [128, B * 512], BF16, kind="ExternalOutput").ap()
        dbgsp = nc.dram_tensor("dbgsp", [128, B * 512], F32, kind="ExternalOutput").ap()
        dbgln = nc.dram_tensor("dbgln", [128, B * 512], F32, kind="ExternalOutput").ap()
        dbgbe = nc.dram_tensor("dbgbe", [1, B * 512], F32, kind="ExternalOutput").ap()
        dbgu = nc.dram_tensor("dbgu", [128, 512], BF16, kind="ExternalOutput").ap()
        dbgos = nc.dram_tensor("dbgos", [64, 256], F32, kind="ExternalOutput").ap()
        dbgoc = nc.dram_tensor("dbgoc", [64, 256], F32, kind="ExternalOutput").ap()
        dbggq = nc.dram_tensor("dbggq", [128, 256], BF16, kind="ExternalOutput").ap()
        dbgmk = nc.dram_tensor("dbgmk", [128, 128], F32, kind="ExternalOutput").ap()
        dbgp3 = nc.dram_tensor("dbgp3", [128, 256], F32, kind="ExternalOutput").ap()
        dbge = nc.dram_tensor("dbge", [64, 8 * 128], F32, kind="ExternalOutput").ap()

    es = contextlib.ExitStack()
    with es:
        T = lambda name, shape, dt=F32: es.enter_context(nc.sbuf_tensor(name, shape, dt))
        wq = T("wq", [128, 40 * 128 + 8], BF16)
        cdiag = T("cdiag", [128, 12 * 128], BF16)
        cbias = T("cbias", [128, 3], F32)
        bnegc = T("bnegc", [128, 1], F32)
        bbc = T("bbc", [1, 1], F32)
        lng = T("lng", [128, 2], F32)
        idb = T("idb", [128, 128], BF16)
        idf = T("idf", [128, 128], F32)
        onc = T("onc", [128, 1], F32)
        onr = T("onr", [1, 128], F32)
        mks = T("mks", [128, 32], F32)
        mg0 = T("mg0", [128, 64], F32)
        mg1 = T("mg1", [128, 64], F32)
        fsc = T("fsc", [128, 5 * 32], F32)
        xf = T("xf", [128, 8 * TT], F32)
        xb = T("xb", [128, 8 * TT], BF16)
        qT_ = T("qT_", [128, B * TT], BF16)
        kT_ = T("kT_", [128, B * TT], BF16)
        vT_ = T("vT_", [128, B * TT], BF16)
        gT_ = T("gT_", [128, B * TT], BF16)
        spT_ = T("spT_", [128, B * TT], F32)
        lnT_ = T("lnT_", [128, B * TT], F32)
        beta_ = T("beta_", [1, B * TT], F32)
        pre = T("pre", [128, TT + 3], F32)
        preb = T("preb", [128, TT + 3], BF16)
        halo = T("halo", [128, 9 * B * 2], F32)
        st_f = T("st_f", [128, B * 128], F32)
        st_b = T("st_b", [128, B * 128], BF16)
        sbf = T("sbf", [128, 128], BF16)
        kct = T("kct", [128, 256], BF16)
        gq01 = T("gq01", [128, 128], BF16)
        gbd0 = T("gbd0", [128, 128], BF16)
        gbd1 = T("gbd1", [128, 128], BF16)
        gbdx = T("gbdx", [128, 128], BF16)
        efm = T("efm", [128, 3 * 64], F32)
        rhsp = T("rhsp", [128, 64], F32)
        e1t = T("e1t", [64, B * 128], F32)
        e2t = T("e2t", [64, B * 128], F32)
        cp0 = T("cp0", [128, 256], F32)
        cp1 = T("cp1", [128, 256], F32)
        invs = T("invs", [128, 256], F32)
        bstk = T("bstk", [128, 2], F32)
        bstn = T("bstn", [128, 2], F32)
        rstk = T("rstk", [128, 512], BF16)
        delta = T("delta", [128, 256], BF16)
        u0 = T("u0", [128, 256], BF16)
        u1 = T("u1", [128, 256], BF16)
        oS = T("oS", [64, 128], F32)
        oS2 = T("oS2", [64, 128], F32)
        ocol = T("ocol", [64, 2], F32)
        mcol = T("mcol", [128, 2], F32)
        ecolP = T("ecolP", [128, B], F32)
        ecolN = T("ecolN", [128, 2 * B], F32)
        ognm = T("ognm", [128, 64], F32)
        send = T("send", [128, 8 * 512], BF16)
        recv = T("recv", [128, 8 * 512], BF16)
        x1f = T("x1f", [128, 512], F32)
        x1b = T("x1b", [128, 8 * 512], BF16)
        r2f = T("r2f", [128, 512], F32)
        sqs = T("sqs", [128, 512], F32)
        rowa = T("rowa", [1, 512], F32)
        rowb = T("rowb", [1, 512], F32)
        hs0 = T("hs0", [128, 512], BF16)
        hs1 = T("hs1", [128, 512], BF16)
        ht = T("ht", [128, 32 * 512], BF16)
        wst = T("wst", [128, 2 * 4096], BF16)
        outf = T("outf", [128, 512], F32)
        sscr_state = T("sscr_state", [128, 128], F32)
        epsc = T("epsc", [128, 1], F32)
        ocdump = T("ocdump", [64, 256], F32)
        p3dump = T("p3dump", [128, 256], F32)

        pb = [es.enter_context(nc.psum_tensor(f"pb{i}", [128, 512], F32)) for i in range(6)]
        pbar = es.enter_context(nc.psum_tensor("pbar", [128, 512], F32))
        ptb = es.enter_context(nc.psum_tensor("ptb", [128, 512], BF16))

        sems = {}
        for k in ("ld", "x", "w", "mm", "ve", "sc", "cc", "sd", "rv", "st", "gp"):
            sems[k] = es.enter_context(nc.semaphore("s_" + k))
        cnt = {k: 0 for k in sems}
        prog = {k: [] for k in ("sync", "tensor", "vector", "scalar", "gpsimd")}

        def emit(eng, fn):
            prog[eng].append(fn)

        def dma(eng, out, in_, key, serial=True):
            if serial:
                wait(eng, key, cnt[key])
            cnt[key] += 16
            emit(eng, lambda e, out=out, in_=in_, key=key: e.dma_start(out=out, in_=in_).then_inc(sems[key], 16))
            return cnt[key]

        def wait(eng, key, val=None):
            if val is None:
                val = cnt[key]
            emit(eng, lambda e, key=key, val=val: e.wait_ge(sems[key], val))

        def op(eng, fn, key=None):
            if eng in ("vector", "scalar", "gpsimd"):
                emit(eng, lambda e: e.drain())
            if key:
                cnt[key] += 1
                emit(eng, lambda e, fn=fn, key=key: fn(e).then_inc(sems[key], 1))
                return cnt[key]
            emit(eng, lambda e, fn=fn: fn(e))
            return None

        TP = [(0, 0), (0, 32), (0, 64), (0, 96)]

        def bar_mm():
            return op("tensor", lambda e: e.matmul(pbar[:, 0:4], idb[:, :], idb[:, 0:4],
                                                   start=True, stop=True, skip_group_check=True), "mm")

        dma("sync", idb[:, :], identb[:, :], "ld")
        dma("sync", idf[:, :], identf[:, :], "ld")
        dma("sync", onc[:, :], onecf[:, :], "ld")
        dma("sync", onr[:, :], onerf[:, :], "ld")
        dma("sync", mks[:, :], mk32q[:, :], "ld")
        dma("sync", mg0[:, :], mq0[:, :], "ld")
        dma("sync", mg1[:, :], mq1[:, :], "ld")
        dma("sync", bnegc[:, :], bneg[:, :], "ld")
        dma("sync", bbc[:, :], bbv[:, :], "ld")
        dma("sync", lng[:, :], lngb[:, :], "ld")
        dma("sync", cbias[:, :], convb[:, :], "ld")
        dma("sync", fsc[:, :], fscl[:, :], "ld")
        for g in range(5):
            wait("sync", "ve")
            v = dma("sync", xf[:, 0:8 * 128], w5[:, g * 1024:(g + 1) * 1024], "x")
            wait("vector", "x", v)
            op("vector", lambda e, g=g: e.tensor_copy(wq[:, g * 1024:(g + 1) * 1024], xf[:, 0:1024]), "ve")
        wait("sync", "ve")
        v = dma("sync", xf[:, 0:8], wbcol[:, :], "x")
        wait("vector", "x", v)
        op("vector", lambda e: e.tensor_copy(wq[:, 5120:5128], xf[:, 0:8]), "ve")
        v = dma("sync", xf[:, 8:20], convw[:, :], "x")
        wait("vector", "x", v)
        for t in range(12):
            op("vector", lambda e, t=t: e.tensor_scalar(
                cdiag[:, t * 128:(t + 1) * 128], idb[:, :], xf[:, 8 + t:9 + t], 0.0, OP.mult, OP.add))
        op("vector", lambda e: e.memset(epsc[:, :], 1e-5))
        op("vector", lambda e: e.memset(send[:, :], 0.0))
        op("vector", lambda e: e.memset(recv[:, :], 0.0))
        op("vector", lambda e: e.memset(halo[:, :], 0.0))
        op("vector", lambda e: e.memset(st_f[:, :], 0.0))
        op("vector", lambda e: e.memset(st_b[:, :], 0.0))
        op("vector", lambda e: e.memset(gbd0[:, :], 0.0))
        op("vector", lambda e: e.memset(gbd1[:, :], 0.0))
        op("vector", lambda e: e.memset(gbdx[:, :], 0.0))
        op("vector", lambda e: e.memset(gq01[:, :], 0.0))
        op("vector", lambda e: e.memset(kct[:, :], 0.0))
        op("vector", lambda e: e.memset(rstk[:, :], 0.0))
        op("vector", lambda e: e.memset(invs[:, :], 1.0))
        op("vector", lambda e: e.memset(bstk[:, :], 0.0))
        op("vector", lambda e: e.memset(bstn[:, :], 0.0), "ve")
        wait("tensor", "ve")
        wait("scalar", "ve")

        q_t = lambda b: qT_[:, b * TT:(b + 1) * TT]
        k_t = lambda b: kT_[:, b * TT:(b + 1) * TT]
        v_t = lambda b: vT_[:, b * TT:(b + 1) * TT]
        g_t = lambda b: gT_[:, b * TT:(b + 1) * TT]
        sp_t = lambda b: spT_[:, b * TT:(b + 1) * TT]
        ln_t = lambda b: lnT_[:, b * TT:(b + 1) * TT]
        be_t = lambda b: beta_[:, b * TT:(b + 1) * TT]
        st_fb = lambda b: st_f[:, b * 128:(b + 1) * 128]
        st_bb = lambda b: st_b[:, b * 128:(b + 1) * 128]

        def emit_proj(tt, b):
            t0 = tt * TT
            wait("sync", "ve")
            xv = 0
            for c in range(8):
                xv = dma("sync", xf[:, c * TT:(c + 1) * TT], xfm[b, c, :, t0:t0 + TT], "x", serial=(c == 0))
            wait("vector", "x", xv)
            wait("vector", "mm")
            for c in range(8):
                vx = op("vector", lambda e, c=c: e.tensor_copy(xb[:, c * TT:(c + 1) * TT], xf[:, c * TT:(c + 1) * TT]),
                        "ve" if c == 7 else None)
            wait("tensor", "ve", vx)
            for ti in range(3):
                for c in range(8):
                    mm = op("tensor", lambda e, c=c, ti=ti: e.matmul(
                        pb[0][:, 0:TT], wq[:, (ti * 8 + c) * 128:(ti * 8 + c + 1) * 128],
                        xb[:, c * TT:(c + 1) * TT], start=(c == 0), stop=(c == 7)),
                        "mm" if c == 7 else None)
                wait("scalar", "mm", mm)
                wait("scalar", "ve")
                sc = op("scalar", lambda e: e.copy(pre[:, 3:3 + TT], pb[0][:, 0:TT]), "sc")
                wait("vector", "sc", sc)
                hi = (ti * B + b) * 3 + (tt % 2) * 36
                ho = (ti * B + b) * 3 + ((tt + 1) % 2) * 36
                op("vector", lambda e, hi=hi: e.tensor_copy(pre[:, 0:3], halo[:, hi:hi + 3]))
                op("vector", lambda e, ho=ho: e.tensor_copy(halo[:, ho:ho + 3], pre[:, TT:TT + 3]))
                cv = op("vector", lambda e: e.tensor_copy(preb[:, :], pre[:, :]), "ve")
                wait("tensor", "ve", cv)
                for j in range(4):
                    mm = op("tensor", lambda e, ti=ti, j=j: e.matmul(
                        pb[1][:, 0:TT], cdiag[:, (ti * 4 + j) * 128:(ti * 4 + j + 1) * 128],
                        preb[:, j:j + TT], start=(j == 0), stop=(j == 3)),
                        "mm" if j == 3 else None)
                wait("scalar", "mm", mm)
                dstt = (q_t(b), k_t(b), v_t(b))[ti]
                scv = op("scalar", lambda e, ti=ti, dstt=dstt: e.activation(
                    dstt, pb[1][:, 0:TT], AF.Silu, bias=cbias[:, ti:ti + 1], scale=1.0), "sc")
                if ti == 1:
                    wait("vector", "sc", scv)
                    op("vector", lambda e, dstt=dstt: e.tensor_scalar(dstt, dstt, DKI, 0.0, OP.mult, OP.add), "ve")
            for c in range(8):
                mm = op("tensor", lambda e, c=c: e.matmul(
                    pb[0][:, 0:TT], wq[:, (24 + c) * 128:(24 + c + 1) * 128],
                    xb[:, c * TT:(c + 1) * TT], start=(c == 0), stop=(c == 7)),
                    "mm" if c == 7 else None)
            wait("scalar", "mm", mm)
            op("scalar", lambda e: e.activation(xf[:, 0:TT], pb[0][:, 0:TT], AF.Sigmoid,
                                                bias=bnegc[:, 0:1], scale=1.0))
            op("scalar", lambda e, b=b: e.activation(sp_t(b), xf[:, 0:TT], AF.Ln), "sc")
            wait("tensor", "sc")
            for c in range(8):
                mm = op("tensor", lambda e, c=c: e.matmul(
                    pb[1][:, 0:TT], wq[:, (32 + c) * 128:(32 + c + 1) * 128],
                    xb[:, c * TT:(c + 1) * TT], start=(c == 0), stop=(c == 7)),
                    "mm" if c == 7 else None)
            wait("scalar", "mm", mm)
            op("scalar", lambda e, b=b: e.activation(g_t(b), pb[1][:, 0:TT], AF.Sigmoid), "sc")
            wait("tensor", "sc")
            for c in range(8):
                mm = op("tensor", lambda e, c=c: e.matmul(
                    pb[0][0:1, 0:TT], wq[:, 5120 + c:5121 + c],
                    xb[:, c * TT:(c + 1) * TT], start=(c == 0), stop=(c == 7)),
                    "mm" if c == 7 else None)
            wait("scalar", "mm", mm)
            scb = op("scalar", lambda e, b=b: e.activation(be_t(b), pb[0][0:1, 0:TT], AF.Sigmoid,
                                                           bias=bbc[0:1, 0:1], scale=1.0), "sc")
            wait("vector", "sc", scb)
            for c in range(NCH):
                op("vector", lambda e, b=b, c=c: e.tensor_tensor_scan(
                    out=ln_t(b)[:, c * CH:(c + 1) * CH], data0=sp_t(b)[:, c * CH:(c + 1) * CH],
                    data1=sp_t(b)[:, c * CH:(c + 1) * CH], initial=0.0, op0=OP.add, op1=OP.bypass),
                   "ve" if c == NCH - 1 else None)

        def emit_prep(tt, b, c):
            tp = TP[b]
            r0, r1 = 32 * b, 32 * b + 32
            c0 = c * CH
            Kc = k_t(b)[:, c0:c0 + CH]
            Qc = q_t(b)[:, c0:c0 + CH]
            lnc = ln_t(b)[:, c0:c0 + CH]
            op("vector", lambda e, lnc=lnc: e.tensor_scalar(mcol[:, 0:1], lnc[:, CH - 1:CH], 0.5, 0.0, OP.mult, OP.add))
            mv = op("vector", lambda e, lnc=lnc: e.tensor_scalar(mcol[:, 1:2], lnc[:, CH - 1:CH], -0.5, 0.0, OP.mult, OP.add), "ve")
            wait("scalar", "ve", mv)
            wait("scalar", "mm")
            op("scalar", lambda e, lnc=lnc: e.activation(efm[:, 0:64], lnc, AF.Exp, scale=1.0))
            e2v = op("scalar", lambda e, lnc=lnc: e.activation(efm[:, 64:128], lnc, AF.Exp, bias=mcol[:, 1:2], scale=1.0), "sc")
            wait("vector", "sc", e2v)
            tsub = op("vector", lambda e, b=b, c0=c0, lnc=lnc: e.tensor_tensor(
                rhsp[:, :], lnc, sp_t(b)[:, c0:c0 + CH], OP.subtract), "ve")
            wait("scalar", "ve", tsub)
            e3 = op("scalar", lambda e: e.activation(efm[:, 128:192], rhsp[:, :], AF.Exp, bias=mcol[:, 0:1], scale=-1.0), "sc")
            wait("vector", "sc", e3)
            op("vector", lambda e, b=b, c0=c0: e.tensor_tensor(rhsp[:, :], efm[:, 128:192], v_t(b)[:, c0:c0 + CH], OP.mult))
            op("vector", lambda e: e.tensor_scalar(rhsp[:, :], rhsp[:, :], -1.0, 0.0, OP.mult, OP.add))
            op("vector", lambda e, b=b: e.tensor_copy(ecolP[:, b:b + 1], efm[:, 63:64]))
            op("vector", lambda e, b=b: e.tensor_scalar(ecolN[:, 2 * b:2 * b + 1], efm[:, 63:64], -1.0, 0.0, OP.mult, OP.add))
            rr = op("vector", lambda e, b=b: e.tensor_scalar(ecolN[:, 2 * b + 1:2 * b + 2], efm[:, 127:128], -1.0, 0.0, OP.mult, OP.add), "ve")
            if _DBG_PREP < 2:
                return
            wait("tensor", "ve", rr)
            pbt = pb[b % 2]
            op("tensor", lambda e, pbt=pbt, r0=r0, r1=r1, b=b, c0=c0, tp=tp: e.matmul(
                pbt[r0:r1, 0:128], sp_t(b)[:, c0:c0 + 32], idf[:, :], start=True, stop=True, tile_position=tp, skip_group_check=True))
            op("tensor", lambda e, pbt=pbt, r0=r0, r1=r1, b=b, c0=c0, tp=tp: e.matmul(
                pbt[r0:r1, 128:256], sp_t(b)[:, c0 + 32:c0 + 64], idf[:, :], start=True, stop=True, tile_position=tp, skip_group_check=True))
            op("tensor", lambda e, pbt=pbt, r0=r0, r1=r1, tp=tp: e.matmul(
                pbt[r0:r1, 256:384], rhsp[:, 0:32], idf[:, :], start=True, stop=True, tile_position=tp, skip_group_check=True))
            op("tensor", lambda e, pbt=pbt, r0=r0, r1=r1, tp=tp: e.matmul(
                pbt[r0:r1, 384:512], rhsp[:, 32:64], idf[:, :], start=True, stop=True, tile_position=tp, skip_group_check=True))
            if _DBG_PREP < 3:
                return
            op("tensor", lambda e, b=b, c0=c0, tp=tp: e.matmul(
                pb[4][32 * b:32 * b + 32, 0:1], be_t(b)[0:1, c0:c0 + 32], onc[0:1, 0:1], start=True, stop=True, tile_position=tp, skip_group_check=True))
            tb = op("tensor", lambda e, b=b, c0=c0, tp=tp: e.matmul(
                pb[4][32 * b:32 * b + 32, 1:2], be_t(b)[0:1, c0 + 32:c0 + 64], onc[0:1, 0:1], start=True, stop=True, tile_position=tp, skip_group_check=True), "mm")
            if _DBG_PREP < 4:
                return
            op("tensor", lambda e: e.matmul(pb[2][0:64, 0:128], efm[:, 0:64], idf[:, :], start=True, stop=True, skip_group_check=True))
            op("tensor", lambda e: e.matmul(pb[2][0:64, 128:256], efm[:, 64:128], idf[:, :], start=True, stop=True, skip_group_check=True))
            if _DBG_PREP < 5:
                return
            op("tensor", lambda e, Kc=Kc, b=b, tp=tp: e.matmul(
                pb[3][32 * b:32 * b + 32, 0:64], Kc[:, 0:32], Kc[:, :], start=True, stop=True, tile_position=(0, 32 * b), skip_group_check=True))
            op("tensor", lambda e, Kc=Kc, b=b: e.matmul(
                pb[3][32 * b:32 * b + 32, 64:96], Kc[:, 32:64], Kc[:, 32:64], start=True, stop=True, tile_position=(0, 32 * b), skip_group_check=True))
            op("tensor", lambda e, Kc=Kc, Qc=Qc, b=b: e.matmul(
                pb[3][32 * b:32 * b + 32, 128:192], Kc[:, 0:32], Qc[:, :], start=True, stop=True, tile_position=(0, 32 * b), skip_group_check=True))
            op("tensor", lambda e, Kc=Kc, Qc=Qc, b=b: e.matmul(
                pb[3][32 * b:32 * b + 32, 192:256], Kc[:, 32:64], Qc[:, :], start=True, stop=True, tile_position=(0, 32 * b), skip_group_check=True))
            if _DBG_PREP < 6:
                return
            op("tensor", lambda e, Kc=Kc, b=b, tp=tp: e.matmul(
                pb[4][32 * b:32 * b + 32, 128:256], Kc[:, 0:32], idb[:, :], start=True, stop=True, tile_position=tp, skip_group_check=True))
            op("tensor", lambda e, Kc=Kc, b=b, tp=tp: e.matmul(
                pb[4][32 * b:32 * b + 32, 256:384], Kc[:, 32:64], idb[:, :], start=True, stop=True, tile_position=tp, skip_group_check=True))
            tv = bar_mm()
            if _DBG_PREP < 7:
                return
            wait("scalar", "mm", tv)
            op("scalar", lambda e, pbt=pbt, r0=r0, r1=r1: e.activation(invs[r0:r1, 0:128], pbt[r0:r1, 0:128], AF.Exp, scale=-1.0))
            op("scalar", lambda e, pbt=pbt, r0=r0, r1=r1: e.activation(invs[r0:r1, 128:256], pbt[r0:r1, 128:256], AF.Exp, scale=-1.0))
            sv = op("scalar", lambda e, b=b: e.copy(kct[32 * b:32 * b + 32, :], pb[4][32 * b:32 * b + 32, 128:384]), "sc")
            if _DBG_PREP < 8:
                wait("tensor", "sc", sv)
                return
            op("scalar", lambda e, r0=r0, r1=r1: e.copy(bstk[r0:r1, 0:2], pb[4][r0:r1, 0:2]))
            op("scalar", lambda e, r0=r0, r1=r1: e.mul(bstn[r0:r1, 0:2], pb[4][r0:r1, 0:2], -1.0))
            op("scalar", lambda e, pbt=pbt, r0=r0, r1=r1: e.copy(rstk[r0:r1, 128:256], pbt[r0:r1, 256:384]))
            sv = op("scalar", lambda e, pbt=pbt, r0=r0, r1=r1: e.copy(rstk[r0:r1, 384:512], pbt[r0:r1, 384:512]), "sc")
            wait("vector", "mm", tv)
            op("vector", lambda e, b=b: e.tensor_copy(e1t[:, b * 128:(b + 1) * 128], pb[2][0:64, 0:128]))
            esv = op("vector", lambda e, b=b: e.tensor_copy(e2t[:, b * 128:(b + 1) * 128], pb[2][0:64, 128:256]), "ve")

            op("vector", lambda e, r0=r0, r1=r1: e.tensor_tensor(gbd0[r0:r1, r0:r1], pb[3][r0:r1, 0:32], mks[r0:r1, :], OP.mult))
            op("vector", lambda e, r0=r0, r1=r1: e.tensor_scalar(gbdx[r0:r1, r0:r1], pb[3][r0:r1, 32:64], -1.0, 0.0, OP.mult, OP.add))
            op("vector", lambda e, r0=r0, r1=r1: e.tensor_tensor(gbd1[r0:r1, r0:r1], pb[3][r0:r1, 64:96], mks[r0:r1, :], OP.mult))
            op("vector", lambda e, r0=r0, r1=r1: e.tensor_tensor(gq01[r0:r1, 0:64], pb[3][r0:r1, 128:192], mg0[r0:r1, :], OP.mult))
            vv = op("vector", lambda e, r0=r0, r1=r1: e.tensor_tensor(gq01[r0:r1, 64:128], pb[3][r0:r1, 192:256], mg1[r0:r1, :], OP.mult), "ve")
            wait("tensor", "ve", vv)
            wait("tensor", "sc", sv)

        def emit_chain(tt, c):
            c0 = c * CH
            wait("vector", "sc")
            op("vector", lambda e: e.tensor_scalar(cp0[:, 0:128], invs[:, 0:128], bstk[:, 0:1], 0.0, OP.mult, OP.add))
            op("vector", lambda e: e.tensor_copy(cp0[:, 128:256], cp0[:, 0:128]))
            op("vector", lambda e: e.tensor_scalar(cp1[:, 0:128], invs[:, 128:256], bstk[:, 1:2], 0.0, OP.mult, OP.add))
            cpv = op("vector", lambda e: e.tensor_copy(cp1[:, 128:256], cp1[:, 0:128]), "ve")
            wait("tensor", "ve", cpv)
            op("tensor", lambda e: e.matmul(pb[2][:, 0:256], idb[:, :], rstk[:, 0:256], start=True, stop=True))
            for b in range(B):
                op("tensor", lambda e, b=b, c0=c0: e.matmul(
                    pb[2][32 * b:32 * b + 32, 0:128], k_t(b)[:, c0:c0 + 32], st_bb(b),
                    start=False, stop=False, tile_position=(0, 32 * b),
                    skip_group_check=True))
            mmv = bar_mm()
            wait("vector", "mm", mmv)
            dv = op("vector", lambda e: e.tensor_tensor(delta[:, :], pb[2][:, 0:256], cp0[:, :], OP.mult), "ve")
            wait("tensor", "ve", dv)
            op("tensor", lambda e: e.matmul(pb[4][:, 0:256], idb[:, :], delta[:, :], start=True, stop=False))
            for j in range(NIT):
                mmv = op("tensor", lambda e: e.matmul(pb[3][:, 0:256], gbd0[:, :], delta[:, :], start=True, stop=True), "mm")
                wait("vector", "mm", mmv)
                dv = op("vector", lambda e: e.tensor_tensor(delta[:, :], pb[3][:, 0:256], cp0[:, :], OP.mult), "ve")
                wait("tensor", "ve", dv)
                op("tensor", lambda e, j=j: e.matmul(pb[4][:, 0:256], idb[:, :], delta[:, :], start=False, stop=(j == NIT - 1)),
                   "mm" if j == NIT - 1 else None)
            mmv = cnt["mm"]
            wait("scalar", "mm", mmv)
            uv = op("scalar", lambda e: e.copy(u0[:, :], pb[4][:, 0:256]), "sc")
            wait("tensor", "sc", uv)
            op("tensor", lambda e: e.matmul(pb[5][:, 0:256], idb[:, :], rstk[:, 256:512], start=True, stop=True))
            for b in range(B):
                op("tensor", lambda e, b=b, c0=c0: e.matmul(
                    pb[5][32 * b:32 * b + 32, 0:128], k_t(b)[:, c0 + 32:c0 + 64], st_bb(b),
                    start=False, stop=False, tile_position=(0, 32 * b),
                    skip_group_check=True))
            mmv = op("tensor", lambda e: e.matmul(pb[5][:, 0:256], gbdx[:, :], u0[:, :], start=False, stop=True,
                                                  skip_group_check=True), "mm")
            wait("vector", "mm", mmv)
            dv = op("vector", lambda e: e.tensor_tensor(delta[:, :], pb[5][:, 0:256], cp1[:, :], OP.mult), "ve")
            wait("tensor", "ve", dv)
            op("tensor", lambda e: e.matmul(pb[4][:, 0:256], idb[:, :], delta[:, :], start=True, stop=False))
            for j in range(NIT):
                mmv = op("tensor", lambda e: e.matmul(pb[3][:, 0:256], gbd1[:, :], delta[:, :], start=True, stop=True), "mm")
                wait("vector", "mm", mmv)
                dv = op("vector", lambda e: e.tensor_tensor(delta[:, :], pb[3][:, 0:256], cp1[:, :], OP.mult), "ve")
                wait("tensor", "ve", dv)
                op("tensor", lambda e, j=j: e.matmul(pb[4][:, 0:256], idb[:, :], delta[:, :], start=False, stop=(j == NIT - 1)),
                   "mm" if j == NIT - 1 else None)
            mmv = cnt["mm"]
            wait("scalar", "mm", mmv)
            uv = op("scalar", lambda e: e.copy(u1[:, :], pb[4][:, 0:256]), "sc")
            wait("tensor", "sc", uv)
            wait("vector", "sc", uv)

        def emit_post(tt, b, c):
            tp = TP[b]
            r0, r1 = 32 * b, 32 * b + 32
            c0 = c * CH
            slot = 2 * b + (tt % 2)
            op("tensor", lambda e, b=b, c0=c0: e.matmul(
                pb[2][0:64, 0:128], q_t(b)[:, c0:c0 + CH], st_bb(b), start=True, stop=True, skip_group_check=True))
            op("tensor", lambda e, r0=r0, r1=r1: e.matmul(
                pb[2][0:64, 0:128], gq01[r0:r1, 0:64], u0[r0:r1, 0:128], start=False, stop=False, tile_position=(r0, 0), skip_group_check=True))
            op("tensor", lambda e, r0=r0, r1=r1: e.matmul(
                pb[2][0:64, 0:128], gq01[r0:r1, 64:128], u1[r0:r1, 0:128], start=False, stop=False, tile_position=(r0, 0), skip_group_check=True))
            op("tensor", lambda e, r0=r0, r1=r1: e.matmul(
                pb[2][0:64, 128:256], gq01[r0:r1, 0:64], u0[r0:r1, 128:256], start=True, stop=True, tile_position=(r0, 0), skip_group_check=True))
            op("tensor", lambda e, r0=r0, r1=r1: e.matmul(
                pb[2][0:64, 128:256], gq01[r0:r1, 64:128], u1[r0:r1, 128:256], start=False, stop=False, tile_position=(r0, 0), skip_group_check=True))
            om = bar_mm()
            op("tensor", lambda e, r0=r0, r1=r1: e.matmul(
                pb[1][:, 0:128], u0[r0:r1, 0:128], kct[r0:r1, 0:128], start=True, stop=True, tile_position=(r0, 0), skip_group_check=True))
            op("tensor", lambda e, r0=r0, r1=r1: e.matmul(
                pb[1][:, 0:128], u1[r0:r1, 0:128], kct[r0:r1, 128:256], start=False, stop=False, tile_position=(r0, 0), skip_group_check=True))
            op("tensor", lambda e, r0=r0, r1=r1: e.matmul(
                pb[1][:, 128:256], u0[r0:r1, 128:256], kct[r0:r1, 0:128], start=True, stop=True, tile_position=(r0, 0), skip_group_check=True))
            op("tensor", lambda e, r0=r0, r1=r1: e.matmul(
                pb[1][:, 128:256], u1[r0:r1, 128:256], kct[r0:r1, 128:256], start=False, stop=False, tile_position=(r0, 0), skip_group_check=True))
            sm = bar_mm()
            wait("vector", "mm", om)
            if _DBG_DUMP and b == 3:
                op("vector", lambda e: e.tensor_copy(ocdump[0:64, 0:128], pb[2][0:64, 0:128]))
                op("vector", lambda e: e.tensor_copy(ocdump[0:64, 128:256], pb[2][0:64, 128:256]))
            op("vector", lambda e, b=b: e.tensor_tensor(oS[:, :], pb[2][0:64, 0:128], e1t[:, b * 128:(b + 1) * 128], OP.mult))
            op("vector", lambda e, b=b: e.tensor_tensor(oS2[:, :], pb[2][0:64, 128:256], e2t[:, b * 128:(b + 1) * 128], OP.mult))
            op("vector", lambda e: e.tensor_tensor(oS[:, :], oS[:, :], oS2[:, :], OP.add))
            op("vector", lambda e: e.tensor_reduce(ocol[:, 0:1], oS[:, :], mybir.AxisListType.X, OP.add))
            op("vector", lambda e: e.tensor_scalar(ocol[:, 0:1], ocol[:, 0:1], 1.0 / 128.0, 0.0, OP.mult, OP.add))
            ov = op("vector", lambda e: e.tensor_scalar(oS[:, :], oS[:, :], ocol[:, 0:1], 0.0, OP.subtract, OP.add), "ve")
            wait("scalar", "ve", ov)
            sq = op("scalar", lambda e: e.activation(oS2[:, :], oS[:, :], AF.Square), "sc")
            wait("vector", "sc", sq)
            rv = op("vector", lambda e: e.tensor_reduce(ocol[:, 1:2], oS2[:, :], mybir.AxisListType.X, OP.add), "ve")
            wait("scalar", "ve", rv)
            sq = op("scalar", lambda e: e.activation(ocol[:, 1:2], ocol[:, 1:2], AF.Sqrt, bias=epsc[0:64, 0:1], scale=1.0 / 128.0), "sc")
            wait("vector", "sc", sq)
            op("vector", lambda e: e.reciprocal(ocol[:, 1:2], ocol[:, 1:2]))
            onv = op("vector", lambda e: e.tensor_scalar(oS[:, :], oS[:, :], ocol[:, 1:2], 0.0, OP.mult, OP.add), "ve")
            wait("tensor", "ve", onv)
            ot = op("tensor", lambda e: e.matmul(pb[2][0:128, 256:320], oS[:, :], idf[0:64, 0:64], start=True, stop=True, skip_group_check=True), "mm")
            wait("vector", "mm", ot)
            og = op("vector", lambda e: e.tensor_scalar(ognm[:, :], pb[2][0:128, 256:320],
                                                        lng[:, 0:1], lng[:, 1:2], OP.mult, OP.add), "ve")
            gv = op("vector", lambda e, b=b, c0=c0, slot=slot, c=c: e.tensor_tensor(
                send[:, slot * 512 + c * 64:slot * 512 + (c + 1) * 64], ognm[:, :],
                g_t(b)[:, c0:c0 + CH], OP.mult), "ve")
            wait("vector", "mm", sm)
            op("vector", lambda e, b=b: e.tensor_scalar(sscr_state[:, :], st_fb(b), ecolP[:, b:b + 1], 0.0, OP.mult, OP.add))
            op("vector", lambda e, b=b: e.scalar_tensor_tensor(
                out=sscr_state[:, :], in0=pb[1][:, 0:128], scalar=ecolN[:, 2 * b:2 * b + 1],
                in1=sscr_state[:, :], op0=OP.mult, op1=OP.add))
            sv = op("vector", lambda e, b=b: e.scalar_tensor_tensor(
                out=st_fb(b), in0=pb[1][:, 128:256], scalar=ecolN[:, 2 * b + 1:2 * b + 2],
                in1=sscr_state[:, :], op0=OP.mult, op1=OP.add), "ve")
            cb = op("vector", lambda e, b=b: e.tensor_copy(sbf[:, :], st_fb(b)), "ve")
            wait("tensor", "ve", cb)
            st_tr = op("tensor", lambda e: e.matmul(pb[1][:, 256:384], sbf[:, :], idb[:, :], start=True, stop=True, skip_group_check=True), "mm")
            wait("scalar", "mm", st_tr)
            sv2 = op("scalar", lambda e, b=b: e.copy(st_bb(b), pb[1][:, 256:384]), "sc")
            wait("tensor", "sc", sv2)
            wait("vector", "sc", sv2)


        def emit_exchange(e_):
            wait("gpsimd", "ve")
            for s in range(8):
                dma("gpsimd", ibs[e_].ap()[s, :, :], send[:, s * 512:(s + 1) * 512], "sd", serial=(s == 0))
            wait("gpsimd", "sd")
            op("gpsimd", lambda e, e_=e_: _cc_alltoall(e, ibs[e_].ap().opt(), obs[e_].ap().opt()), "cc")
            wait("sync", "cc", cnt["cc"])
            for r in range(8):
                dma("sync", recv[:, r * 512:(r + 1) * 512], obs[e_].ap()[r, :, :], "rv", serial=(r == 0))

        def emit_p2(e_):
            wait("tensor", "rv", cnt["rv"])
            wait("sync", "ve")
            wait("vector", "w")
            wait("vector", "x")
            wait("vector", "mm")
            xrv = 0
            for oc in range(8):
                wait("sync", "ve")
                wait("sync", "mm")
                wait("sync", "gp")
                wv = dma("sync", xf[:, 0:1024], woT[oc, :, :], "x")
                wait("vector", "x", wv)
                cv = op("vector", lambda e: e.tensor_copy(wst[:, 0:1024], xf[:, 0:1024]), "ve")
                wait("tensor", "ve", cv)
                for r in range(8):
                    mm = op("tensor", lambda e, r=r, oc=oc: e.matmul(
                        pb[5][:, 0:512], wst[:, r * 128:(r + 1) * 128],
                        recv[:, r * 512:(r + 1) * 512], start=(r == 0), stop=(r == 7)),
                        "mm" if r == 7 else None)
                xrv = dma("sync", x1f[:, :], xre[oc, :, e_ * 512:(e_ + 1) * 512], "x")
                wait("vector", "mm", mm)
                wait("vector", "x", xrv)
                rv1 = op("vector", lambda e: e.tensor_tensor(x1f[:, :], pb[5][:, 0:512], x1f[:, :], OP.add), "ve")
                wait("scalar", "ve", rv1)
                sv = op("scalar", lambda e: e.activation(sqs[:, :], x1f[:, :], AF.Square), "sc")
                wait("tensor", "sc", sv)
                op("tensor", lambda e, oc=oc: e.matmul(pb[4][0:1, 0:512], onc[:, :], x1f[:, :],
                                                       start=(oc == 0), stop=(oc == 7)))
                mm = op("tensor", lambda e, oc=oc: e.matmul(pb[3][0:1, 0:512], onc[:, :], sqs[:, :],
                                                            start=(oc == 0), stop=(oc == 7)), "mm")
                wait("gpsimd", "mm", mm)
                dv = dma("gpsimd", x1d.ap()[:, oc * 512:(oc + 1) * 512], x1f[:, :], "gp")
            wait("vector", "mm", mm)
            op("vector", lambda e: e.tensor_copy(rowa[:, :], pb[4][0:1, 0:512]))
            rv2 = op("vector", lambda e: e.tensor_copy(rowb[:, :], pb[3][0:1, 0:512]), "ve")
            emit_ln_rows(rv2)
            wait("tensor", "ve")
            bc = op("tensor", lambda e: e.matmul(pb[0][:, 0:512], onr[:, :], rowa[:, :], start=True, stop=True))
            bc = op("tensor", lambda e: e.matmul(pb[1][:, 0:512], onr[:, :], rowb[:, :], start=True, stop=True), "mm")
            wait("vector", "mm", bc)
            wait("scalar", "gp", cnt["gp"])
            for oc in range(8):
                wait("scalar", "ld")
                xv = dma("scalar", x1f[:, :], x1d.ap()[:, oc * 512:(oc + 1) * 512], "x")
                wait("vector", "x", xv)
                op("vector", lambda e: e.tensor_tensor(x1f[:, :], x1f[:, :], pb[0][:, 0:512], OP.subtract))
                op("vector", lambda e: e.tensor_tensor(x1f[:, :], x1f[:, :], pb[1][:, 0:512], OP.mult))
                cv = op("vector", lambda e, oc=oc: e.tensor_copy(x1b[:, oc * 512:(oc + 1) * 512], x1f[:, :]), "ve")
                wait("scalar", "ve", cv)
                dma("scalar", x1d.ap()[:, oc * 512:(oc + 1) * 512], x1f[:, :], "ld")
            wait("tensor", "ve", cnt["ve"])
            slot_mm = [cnt["mm"], cnt["mm"]]
            for oc in range(32):
                sl = (oc % 2) * 4096
                wk = "w" if oc % 2 == 0 else "x"
                wait("sync", "mm", slot_mm[oc % 2])
                wv = dma("sync", wst[:, sl:sl + 1024], w1T[oc, :, :], wk)
                wv = dma("sync", wst[:, sl + 1024:sl + 2048], w2T[oc, :, :], wk, serial=False)
                wait("tensor", wk, wv)
                wait("tensor", "ve", cnt["ve"])
                wait("tensor", "sc")
                for r in range(8):
                    op("tensor", lambda e, r=r, sl=sl: e.matmul(
                        pb[0][:, 0:512], wst[:, sl + r * 128:sl + (r + 1) * 128],
                        x1b[:, r * 512:(r + 1) * 512], start=(r == 0), stop=(r == 7)))
                for r in range(8):
                    mm = op("tensor", lambda e, r=r, sl=sl: e.matmul(
                        pb[1][:, 0:512], wst[:, sl + 1024 + r * 128:sl + 1024 + (r + 1) * 128],
                        x1b[:, r * 512:(r + 1) * 512], start=(r == 0), stop=(r == 7)),
                        "mm" if r == 7 else None)
                slot_mm[oc % 2] = mm
                wait("scalar", "mm", mm)
                hv = op("scalar", lambda e, oc=oc: e.activation(hs0[:, :], pb[0][:, 0:512], AF.Silu,
                                                                scale=fsc[:, oc:oc + 1]), "sc")
                wait("vector", "sc", hv)
                op("vector", lambda e, oc=oc: e.scalar_tensor_tensor(
                    out=ht[:, oc * 512:(oc + 1) * 512], in0=pb[1][:, 0:512],
                    scalar=fsc[:, 32 + oc:33 + oc], in1=hs0[:, :], op0=OP.mult, op1=OP.mult), "ve")
            for stage, (wmat, scoff) in enumerate(((h1T, 64), (h2T, 96))):
                slot_mm = [cnt["mm"], cnt["mm"]]
                for oc in range(32):
                    sl = (oc % 2) * 4096
                    wk = "w" if oc % 2 == 0 else "x"
                    wait("sync", "mm", slot_mm[oc % 2])
                    wv = None
                    for piece in range(2):
                        wv = dma("sync", wst[:, sl + piece * 2048:sl + (piece + 1) * 2048],
                                 wmat[oc, :, piece * 2048:(piece + 1) * 2048], wk, serial=(piece == 0))
                    wait("tensor", wk, wv)
                    wait("tensor", "gp", cnt["gp"])
                    wait("tensor", "sc")
                    wait("tensor", "ve")
                    for r in range(32):
                        mm = op("tensor", lambda e, r=r, sl=sl: e.matmul(
                            pb[0][:, 0:512], wst[:, sl + r * 128:sl + (r + 1) * 128],
                            ht[:, r * 512:(r + 1) * 512], start=(r == 0), stop=(r == 31)),
                            "mm" if r == 31 else None)
                    slot_mm[oc % 2] = mm
                    wait("scalar", "mm", mm)
                    hb = hs0 if oc % 2 == 0 else hs1
                    wait("scalar", "gp", cnt["gp"])
                    hv = op("scalar", lambda e, oc=oc, scoff=scoff, hb=hb: e.activation(
                        hb[:, :], pb[0][:, 0:512], AF.Gelu, scale=fsc[:, scoff + oc:scoff + oc + 1]), "sc")
                    wait("gpsimd", "sc", hv)
                    dma("gpsimd", hd.ap()[:, oc * 512:(oc + 1) * 512], hb[:, :], "gp")
                wait("scalar", "gp", cnt["gp"])
                wait("scalar", "mm")
                hv = 0
                for piece in range(8):
                    hv = dma("scalar", ht[:, piece * 2048:(piece + 1) * 2048],
                             hd.ap()[:, piece * 2048:(piece + 1) * 2048], "x", serial=(piece == 0))
                wait("tensor", "x", hv)
            slot_mm = [cnt["mm"], cnt["mm"]]
            for oc in range(8):
                sl = (oc % 2) * 4096
                wk = "w" if oc % 2 == 0 else "x"
                wait("sync", "mm", slot_mm[oc % 2])
                wv = None
                for piece in range(2):
                    wv = dma("sync", wst[:, sl + piece * 2048:sl + (piece + 1) * 2048],
                             w3T[oc, :, piece * 2048:(piece + 1) * 2048], wk, serial=(piece == 0))
                wait("tensor", wk, wv)
                wait("tensor", "sc")
                wait("tensor", "ve")
                for r in range(32):
                    mm = op("tensor", lambda e, r=r, sl=sl: e.matmul(
                        pb[0][:, 0:512], wst[:, sl + r * 128:sl + (r + 1) * 128],
                        ht[:, r * 512:(r + 1) * 512], start=(r == 0), stop=(r == 31)),
                        "mm" if r == 31 else None)
                slot_mm[oc % 2] = mm
                wait("scalar", "gp")
                wait("scalar", "ld")
                xv = dma("scalar", x1f[:, :], x1d.ap()[:, oc * 512:(oc + 1) * 512], "x")
                wait("vector", "mm", mm)
                wait("vector", "x", xv)
                rv1 = op("vector", lambda e, oc=oc: e.scalar_tensor_tensor(
                    out=r2f[:, :], in0=pb[0][:, 0:512], scalar=fsc[:, 128 + oc:129 + oc],
                    in1=x1f[:, :], op0=OP.mult, op1=OP.add), "ve")
                wait("scalar", "ve", rv1)
                sv = op("scalar", lambda e: e.activation(sqs[:, :], r2f[:, :], AF.Square), "sc")
                wait("tensor", "sc", sv)
                op("tensor", lambda e, oc=oc: e.matmul(pb[4][0:1, 0:512], onc[:, :], r2f[:, :],
                                                       start=(oc == 0), stop=(oc == 7)))
                mm = op("tensor", lambda e, oc=oc: e.matmul(pb[3][0:1, 0:512], onc[:, :], sqs[:, :],
                                                            start=(oc == 0), stop=(oc == 7)), "mm")
                wait("gpsimd", "mm", mm)
                dma("gpsimd", r2d.ap()[:, oc * 512:(oc + 1) * 512], r2f[:, :], "gp")
            wait("vector", "mm", mm)
            op("vector", lambda e: e.tensor_copy(rowa[:, :], pb[4][0:1, 0:512]))
            rv2 = op("vector", lambda e: e.tensor_copy(rowb[:, :], pb[3][0:1, 0:512]), "ve")
            emit_ln_rows(rv2)
            wait("tensor", "ve")
            bc = op("tensor", lambda e: e.matmul(pb[0][:, 0:512], onr[:, :], rowa[:, :], start=True, stop=True))
            bc = op("tensor", lambda e: e.matmul(pb[1][:, 0:512], onr[:, :], rowb[:, :], start=True, stop=True), "mm")
            wait("vector", "mm", bc)
            wait("vector", "gp", cnt["gp"])
            wait("scalar", "gp")
            for oc in range(8):
                wait("scalar", "ve")
                xv = dma("scalar", r2f[:, :], r2d.ap()[:, oc * 512:(oc + 1) * 512], "x")
                wait("vector", "x", xv)
                wait("vector", "st")
                op("vector", lambda e: e.tensor_tensor(outf[:, :], r2f[:, :], pb[0][:, 0:512], OP.subtract))
                ov = op("vector", lambda e: e.tensor_tensor(outf[:, :], outf[:, :], pb[1][:, 0:512], OP.mult), "ve")
                wait("sync", "ve", ov)
                dma("sync", outp[oc, :, e_ * 512:(e_ + 1) * 512], outf[:, :], "st")

        def emit_ln_rows(rv2):
            wait("scalar", "ve", rv2)
            op("scalar", lambda e: e.mul(rowa[:, :], rowa[:, :], 1.0 / 1024.0))
            op("scalar", lambda e: e.mul(rowb[:, :], rowb[:, :], 1.0 / 1024.0))
            sv = op("scalar", lambda e: e.activation(sqs[0:1, 0:512], rowa[:, :], AF.Square), "sc")
            wait("vector", "sc", sv)
            vv = op("vector", lambda e: e.tensor_tensor(rowb[:, :], rowb[:, :], sqs[0:1, 0:512], OP.subtract), "ve")
            wait("scalar", "ve", vv)
            sv = op("scalar", lambda e: e.activation(rowb[:, :], rowb[:, :], AF.Sqrt, bias=epsc[0:1, 0:1]), "sc")
            wait("vector", "sc", sv)
            op("vector", lambda e: e.reciprocal(rowb[:, :], rowb[:, :]), "ve")

        ntt = _DBG_TILES if _DBG_TILES is not None else NTT
        for tt in range(ntt):
            if tt >= 2:
                wait("vector", "sd", cnt["sd"])
            for b in range(B):
                emit_proj(tt, b)
            nch = _DBG_CHUNKS if _DBG_CHUNKS is not None else NCH
            for c in range(nch):
                if _DBG_STAGE >= 1:
                    for b in range(B):
                        emit_prep(tt, b, c)
                if _DBG_STAGE >= 2:
                    emit_chain(tt, c)
                if _DBG_STAGE >= 3:
                    for b in range(B):
                        emit_post(tt, b, c)
            if tt % 2 == 1 and _DBG_P2:
                emit_exchange(tt // 2)
                emit_p2(tt // 2)

        if _DBG_DUMP:
            wait("vector", "mm")
            op("vector", lambda e: e.tensor_copy(p3dump[:, :], pb[3][:, 0:256]), "ve")
            wait("sync", "ve")
            wait("sync", "sc")
            dma("sync", dbgo[:, :], send[:, :], "st")
            dma("sync", dbgs[:, :], st_f[:, :], "st")
            dma("sync", dbgq[:, :], qT_[:, :], "st")
            dma("sync", dbgk[:, :], kT_[:, :], "st")
            dma("sync", dbgv[:, :], vT_[:, :], "st")
            dma("sync", dbgsp[:, :], spT_[:, :], "st")
            dma("sync", dbgln[:, :], lnT_[:, :], "st")
            dma("sync", dbgbe[:, :], beta_[:, :], "st")
            dma("sync", dbgu[:, 0:256], u0[:, :], "st")
            dma("sync", dbgu[:, 256:512], u1[:, :], "st")
            dma("sync", dbgos[:, 0:128], oS[:, :], "st")
            dma("sync", dbgos[:, 128:256], oS2[:, :], "st")
            dma("sync", dbgoc[:, :], ocdump[:, :], "st")
            dma("sync", dbggq[:, 0:128], gq01[:, :], "st")
            dma("sync", dbggq[:, 128:256], gbd0[:, :], "st")
            dma("sync", dbgp3[:, :], p3dump[:, :], "st")
            dma("sync", dbgmk[:, 0:32], mks[:, :], "st")
            dma("sync", dbgmk[:, 32:96], mg0[:, :], "st")
            dma("sync", dbge[:, 0:512], e1t[:, :], "st")
            dma("sync", dbge[:, 512:1024], e2t[:, :], "st")
        wait("sync", "st", cnt["st"])

        blk = es.enter_context(nc.Block())

        @blk.sync
        def _(e):
            for f in prog["sync"]:
                f(e)

        @blk.tensor
        def _(e):
            for f in prog["tensor"]:
                f(e)

        @blk.vector
        def _(e):
            for f in prog["vector"]:
                f(e)

        @blk.scalar
        def _(e):
            for f in prog["scalar"]:
                f(e)

        @blk.gpsimd
        def _(e):
            for f in prog["gpsimd"]:
                f(e)

    return nc



_NC_CACHE = {}


def _consts():
    identb = np.eye(128, dtype=ml_dtypes.bfloat16)
    identf = np.eye(128, dtype=np.float32)
    onecf = np.ones((128, 1), np.float32)
    onerf = np.ones((1, 128), np.float32)
    r = np.arange(32)
    i32 = np.arange(32)
    m = -(r[:, None] < i32[None, :]).astype(np.float32)
    mk32q = np.tile(m, (4, 1))
    i64 = np.arange(64)
    mq0 = np.tile(-(r[:, None] <= i64[None, :]).astype(np.float32), (4, 1))
    mq1 = np.tile(-((r[:, None] + 32) <= i64[None, :]).astype(np.float32), (4, 1))
    return identb, identf, onecf, onerf, mk32q, mq0, mq1


def kernel(**inp):
    x = np.asarray(inp["L0_attn_x"], np.float32)
    xfm = np.ascontiguousarray(x.transpose(0, 2, 1)).reshape(B, 8, 128, L)

    def colslice(W, h):
        return np.ascontiguousarray(W[:, h * 128:(h + 1) * 128].reshape(8, 128, 128))

    Wq, Wk, Wv, Wa, Wg = (np.asarray(inp[f"L0_attn_W{n}"], np.float32) for n in "qkvag")
    Wb = np.asarray(inp["L0_attn_Wb"], np.float32)
    Wo = np.asarray(inp["L0_attn_Wo"], np.float32)
    ba = np.asarray(inp["L0_attn_ba"], np.float32)
    bb = np.asarray(inp["L0_attn_bb"], np.float32)
    cqw = np.asarray(inp["L0_attn_cqw"], np.float32)
    ckw = np.asarray(inp["L0_attn_ckw"], np.float32)
    cvw = np.asarray(inp["L0_attn_cvw"], np.float32)
    cqb = np.asarray(inp["L0_attn_cqb"], np.float32)
    ckb = np.asarray(inp["L0_attn_ckb"], np.float32)
    cvb = np.asarray(inp["L0_attn_cvb"], np.float32)
    lngv = np.asarray(inp["L0_attn_ln_g"], np.float32)
    lnbv = np.asarray(inp["L0_attn_ln_b"], np.float32)

    woTh = np.ascontiguousarray(Wo.reshape(8, 128, 8, 128).transpose(2, 1, 0, 3)).reshape(8, 128, 8 * 128)

    def ffn_lhsT(W, no, nk):
        Wr = W.reshape(no, 128, nk, 128)
        return np.ascontiguousarray(Wr.transpose(0, 3, 2, 1)).reshape(no, 128, nk * 128)

    w1w = np.asarray(inp["L0_ffn_w1_w"], np.float32)
    w2w = np.asarray(inp["L0_ffn_w2_w"], np.float32)
    h1w = np.asarray(inp["L0_ffn_h1_w"], np.float32)
    h2w = np.asarray(inp["L0_ffn_h2_w"], np.float32)
    w3w = np.asarray(inp["L0_ffn_w3_w"], np.float32)
    w1Tb = ffn_lhsT(w1w, 32, 8).astype(ml_dtypes.bfloat16)
    w2Tb = ffn_lhsT(w2w, 32, 8).astype(ml_dtypes.bfloat16)
    h1Tb = ffn_lhsT(h1w, 32, 32).astype(ml_dtypes.bfloat16)
    h2Tb = ffn_lhsT(h2w, 32, 32).astype(ml_dtypes.bfloat16)
    w3Tb = ffn_lhsT(w3w, 8, 32).astype(ml_dtypes.bfloat16)
    fsclv = np.zeros((128, 5 * 32), np.float32)
    for fi, sname in enumerate(("w1_s", "w2_s", "h1_s", "h2_s", "w3_s")):
        sv = np.asarray(inp["L0_ffn_" + sname], np.float32)
        nch = sv.shape[0] // 128
        fsclv[:, fi * 32:fi * 32 + nch] = sv.reshape(nch, 128).T

    identb, identf, onecf, onerf, mk32q, mq0, mq1 = _consts()

    if "nc" not in _NC_CACHE:
        _NC_CACHE["nc"] = build_nc()
    ncp = _NC_CACHE["nc"]

    in_maps = []
    for s in range(8):
        h = s
        mb, half = s // 2, s % 2
        xres = np.empty((8, 128, NRND * 512), np.float32)
        for e in range(NRND):
            xres[:, :, e * 512:(e + 1) * 512] = xfm[mb, :, :, e * 1024 + half * 512: e * 1024 + half * 512 + 512]
        w5v = np.concatenate([colslice(W, h) for W in (Wq, Wk, Wv, Wa, Wg)], 0)
        w5v = np.ascontiguousarray(w5v.transpose(1, 0, 2)).reshape(128, 40 * 128)
        convwv = np.stack([cqw[h * 128:(h + 1) * 128], ckw[h * 128:(h + 1) * 128],
                           cvw[h * 128:(h + 1) * 128]], 0)
        convwv = np.ascontiguousarray(convwv.transpose(1, 0, 2)).reshape(128, 12)
        convbv = np.stack([cqb[h * 128:(h + 1) * 128], ckb[h * 128:(h + 1) * 128],
                           cvb[h * 128:(h + 1) * 128]], 1)
        in_maps.append({
            "xfm": xfm, "xre": xres, "w5": w5v,
            "wbcol": np.ascontiguousarray(Wb[:, h].reshape(8, 128).T),
            "convw": convwv, "convb": np.ascontiguousarray(convbv),
            "bneg": ba[h * 128:(h + 1) * 128].reshape(128, 1).copy(),
            "bbv": bb[h].reshape(1, 1),
            "lngb": np.stack([lngv, lnbv], 1),
            "woT": woTh, "w1T": w1Tb, "w2T": w2Tb, "h1T": h1Tb, "h2T": h2Tb, "w3T": w3Tb,
            "fscl": fsclv, "identb": identb, "identf": identf,
            "onecf": onecf, "onerf": onerf, "mk32q": mk32q, "mq0": mq0, "mq1": mq1,
        })

    res = run_bass_kernel_spmd(ncp, in_maps, core_ids=list(range(8)))
    out = np.empty((B, L, D), np.float32)
    for s in range(8):
        mb, half = s // 2, s % 2
        o = res.results[s]["outp"]
        for e in range(NRND):
            seg = o[:, :, e * 512:(e + 1) * 512]
            t0 = e * 1024 + half * 512
            out[mb, t0:t0 + 512, :] = seg.transpose(2, 0, 1).reshape(512, D)
    return out


# version 07
# speedup vs baseline: 1.0487x; 1.0487x over previous
import contextlib
import math
import numpy as np
import ml_dtypes
import concourse.bass as bass
import concourse.mybir as mybir
from concourse.bass_utils import run_bass_kernel_spmd
from concourse.aot_env import allow_ici_all_to_all


def _cc_alltoall(e, iap, oap):
    with allow_ici_all_to_all():
        return e.collective_compute(
            "AllToAll", mybir.AluOpType.bypass, replica_groups=[list(range(8))],
            ins=[iap], outs=[oap])

F32 = mybir.dt.float32
BF16 = mybir.dt.bfloat16
AF = mybir.ActivationFunctionType
OP = mybir.AluOpType

B, L, D = 4, 4096, 1024
TT, CH = 512, 64
NTT, NCH = L // TT, TT // CH
NRND = 4
DKI = 1.0 / math.sqrt(128.0)
NIT = 31
_DBG_TILES = None
_DBG_P2 = True
_DBG_DUMP = False
_DBG_CHUNKS = None
_DBG_STAGE = 3
_DBG_PREP = 99
_DBG_BSET = (0, 1, 2, 3)


def build_nc():
    nc = bass.Bass("TRN2", target_bir_lowering=False)

    xfm = nc.dram_tensor("xfm", [B, 8, 128, L], F32, kind="ExternalInput").ap()
    xre = nc.dram_tensor("xre", [8, 128, NRND * 512], F32, kind="ExternalInput").ap()
    w5 = nc.dram_tensor("w5", [128, 40 * 128], F32, kind="ExternalInput").ap()
    wbcol = nc.dram_tensor("wbcol", [128, 8], F32, kind="ExternalInput").ap()
    convw = nc.dram_tensor("convw", [128, 12], F32, kind="ExternalInput").ap()
    convb = nc.dram_tensor("convb", [128, 3], F32, kind="ExternalInput").ap()
    bneg = nc.dram_tensor("bneg", [128, 1], F32, kind="ExternalInput").ap()
    bbv = nc.dram_tensor("bbv", [1, 1], F32, kind="ExternalInput").ap()
    lngb = nc.dram_tensor("lngb", [128, 2], F32, kind="ExternalInput").ap()
    woT = nc.dram_tensor("woT", [8, 128, 8 * 128], F32, kind="ExternalInput").ap()
    w1T = nc.dram_tensor("w1T", [32, 128, 8 * 128], BF16, kind="ExternalInput").ap()
    w2T = nc.dram_tensor("w2T", [32, 128, 8 * 128], BF16, kind="ExternalInput").ap()
    h1T = nc.dram_tensor("h1T", [32, 128, 32 * 128], BF16, kind="ExternalInput").ap()
    h2T = nc.dram_tensor("h2T", [32, 128, 32 * 128], BF16, kind="ExternalInput").ap()
    w3T = nc.dram_tensor("w3T", [8, 128, 32 * 128], BF16, kind="ExternalInput").ap()
    fscl = nc.dram_tensor("fscl", [128, 5 * 32], F32, kind="ExternalInput").ap()
    identb = nc.dram_tensor("identb", [128, 128], BF16, kind="ExternalInput").ap()
    identf = nc.dram_tensor("identf", [128, 128], F32, kind="ExternalInput").ap()
    onecf = nc.dram_tensor("onecf", [128, 1], F32, kind="ExternalInput").ap()
    onerf = nc.dram_tensor("onerf", [1, 128], F32, kind="ExternalInput").ap()
    mk32q = nc.dram_tensor("mk32q", [128, 32], F32, kind="ExternalInput").ap()
    mq0 = nc.dram_tensor("mq0", [128, 64], F32, kind="ExternalInput").ap()
    mq1 = nc.dram_tensor("mq1", [128, 64], F32, kind="ExternalInput").ap()
    ibs = [nc.dram_tensor(f"ib{e}", [8, 128, 512], BF16) for e in range(NRND)]
    obs = [nc.dram_tensor(f"ob{e}", [8, 128, 512], BF16) for e in range(NRND)]
    x1d = nc.dram_tensor("x1d", [128, 8 * 512], F32)
    r2d = nc.dram_tensor("r2d", [128, 8 * 512], F32)
    hd = nc.dram_tensor("hd", [128, 32 * 512], BF16)
    outp = nc.dram_tensor("outp", [8, 128, NRND * 512], F32, kind="ExternalOutput").ap()
    if _DBG_DUMP:
        dbgo = nc.dram_tensor("dbgo", [128, 8 * 512], BF16, kind="ExternalOutput").ap()
        dbgs = nc.dram_tensor("dbgs", [128, B * 128], F32, kind="ExternalOutput").ap()
        dbgq = nc.dram_tensor("dbgq", [128, B * 512], BF16, kind="ExternalOutput").ap()
        dbgk = nc.dram_tensor("dbgk", [128, B * 512], BF16, kind="ExternalOutput").ap()
        dbgv = nc.dram_tensor("dbgv", [128, B * 512], BF16, kind="ExternalOutput").ap()
        dbgsp = nc.dram_tensor("dbgsp", [128, B * 512], F32, kind="ExternalOutput").ap()
        dbgln = nc.dram_tensor("dbgln", [128, B * 512], F32, kind="ExternalOutput").ap()
        dbgbe = nc.dram_tensor("dbgbe", [1, B * 512], F32, kind="ExternalOutput").ap()
        dbgu = nc.dram_tensor("dbgu", [128, 512], BF16, kind="ExternalOutput").ap()
        dbgos = nc.dram_tensor("dbgos", [64, 256], F32, kind="ExternalOutput").ap()
        dbgoc = nc.dram_tensor("dbgoc", [64, 256], F32, kind="ExternalOutput").ap()
        dbggq = nc.dram_tensor("dbggq", [128, 256], BF16, kind="ExternalOutput").ap()
        dbgmk = nc.dram_tensor("dbgmk", [128, 128], F32, kind="ExternalOutput").ap()
        dbgp3 = nc.dram_tensor("dbgp3", [128, 256], F32, kind="ExternalOutput").ap()
        dbge = nc.dram_tensor("dbge", [64, 8 * 128], F32, kind="ExternalOutput").ap()

    es = contextlib.ExitStack()
    with es:
        T = lambda name, shape, dt=F32: es.enter_context(nc.sbuf_tensor(name, shape, dt))
        wq = T("wq", [128, 40 * 128 + 8], BF16)
        cdiag = T("cdiag", [128, 12 * 128], BF16)
        cbias = T("cbias", [128, 3], F32)
        bnegc = T("bnegc", [128, 1], F32)
        bbc = T("bbc", [1, 1], F32)
        lng = T("lng", [128, 2], F32)
        idb = T("idb", [128, 128], BF16)
        idf = T("idf", [128, 128], F32)
        onc = T("onc", [128, 1], F32)
        onr = T("onr", [1, 128], F32)
        mks = T("mks", [128, 32], F32)
        mg0 = T("mg0", [128, 64], F32)
        mg1 = T("mg1", [128, 64], F32)
        fsc = T("fsc", [128, 5 * 32], F32)
        xf = T("xf", [128, 8 * TT], F32)
        xb = T("xb", [128, 8 * TT], BF16)
        qT_ = T("qT_", [128, B * TT], BF16)
        kT_ = T("kT_", [128, B * TT], BF16)
        vT_ = T("vT_", [128, B * TT], BF16)
        gT_ = T("gT_", [128, B * TT], BF16)
        spT_ = T("spT_", [128, B * TT], F32)
        lnT_ = T("lnT_", [128, B * TT], F32)
        beta_ = T("beta_", [1, B * TT], F32)
        pre = T("pre", [128, TT + 3], F32)
        preb = T("preb", [128, TT + 3], BF16)
        halo = T("halo", [128, 9 * B * 2], F32)
        st_f = T("st_f", [128, B * 128], F32)
        st_b = T("st_b", [128, B * 128], BF16)
        sbf = T("sbf", [128, 128], BF16)
        kct = T("kct", [128, 256], BF16)
        gq01 = T("gq01", [128, 128], BF16)
        gbd0 = T("gbd0", [128, 128], BF16)
        gbd1 = T("gbd1", [128, 128], BF16)
        gbdx = T("gbdx", [128, 128], BF16)
        efm = T("efm", [128, 3 * 64], F32)
        rhsp = T("rhsp", [128, 64], F32)
        e1t = T("e1t", [64, B * 128], F32)
        e2t = T("e2t", [64, B * 128], F32)
        cp0 = T("cp0", [128, 256], F32)
        cp1 = T("cp1", [128, 256], F32)
        invs = T("invs", [128, 256], F32)
        bstk = T("bstk", [128, 2], F32)
        bstn = T("bstn", [128, 2], F32)
        rstk = T("rstk", [128, 512], BF16)
        delta3 = [T(f"delta{i}", [128, 256], BF16) for i in range(2)]
        u0 = T("u0", [128, 256], BF16)
        u1 = T("u1", [128, 256], BF16)
        oS = T("oS", [64, 128], F32)
        oS2 = T("oS2", [64, 128], F32)
        ocol = T("ocol", [64, 2], F32)
        mcol = T("mcol", [128, 2], F32)
        ecolP = T("ecolP", [128, B], F32)
        ecolN = T("ecolN", [128, 2 * B], F32)
        ognm = T("ognm", [128, 64], F32)
        send = T("send", [128, 8 * 512], BF16)
        recv = T("recv", [128, 8 * 512], BF16)
        x1f = T("x1f", [128, 512], F32)
        x1b = T("x1b", [128, 8 * 512], BF16)
        r2f = T("r2f", [128, 512], F32)
        sqs = T("sqs", [128, 512], F32)
        rowa = T("rowa", [1, 512], F32)
        rowb = T("rowb", [1, 512], F32)
        hs0 = T("hs0", [128, 512], BF16)
        hs1 = T("hs1", [128, 512], BF16)
        ht = T("ht", [128, 32 * 512], BF16)
        wst = T("wst", [128, 2 * 4096], BF16)
        outf = T("outf", [128, 512], F32)
        sscr_state = T("sscr_state", [128, 128], F32)
        epsc = T("epsc", [128, 1], F32)
        ocdump = T("ocdump", [64, 256], F32)
        p3dump = T("p3dump", [128, 256], F32)

        pb = [es.enter_context(nc.psum_tensor(f"pb{i}", [128, 512], F32)) for i in range(6)]
        pbar = es.enter_context(nc.psum_tensor("pbar", [128, 512], F32))
        ptb = es.enter_context(nc.psum_tensor("ptb", [128, 512], BF16))

        sems = {}
        for k in ("ld", "x", "w", "mm", "ve", "sc", "cc", "sd", "rv", "st", "gp"):
            sems[k] = es.enter_context(nc.semaphore("s_" + k))
        cnt = {k: 0 for k in sems}
        prog = {k: [] for k in ("sync", "tensor", "vector", "scalar", "gpsimd")}

        def emit(eng, fn):
            prog[eng].append(fn)

        def dma(eng, out, in_, key, serial=True):
            if serial:
                wait(eng, key, cnt[key])
            cnt[key] += 16
            emit(eng, lambda e, out=out, in_=in_, key=key: e.dma_start(out=out, in_=in_).then_inc(sems[key], 16))
            return cnt[key]

        def wait(eng, key, val=None):
            if val is None:
                val = cnt[key]
            emit(eng, lambda e, key=key, val=val: e.wait_ge(sems[key], val))

        def op(eng, fn, key=None, nodrain=False):
            if eng in ("vector", "scalar", "gpsimd") and not nodrain:
                emit(eng, lambda e: e.drain())
            if key:
                cnt[key] += 1
                emit(eng, lambda e, fn=fn, key=key: fn(e).then_inc(sems[key], 1))
                return cnt[key]
            emit(eng, lambda e, fn=fn: fn(e))
            return None

        TP = [(0, 0), (0, 32), (0, 64), (0, 96)]

        def bar_mm():
            return op("tensor", lambda e: e.matmul(pbar[:, 0:4], idb[:, :], idb[:, 0:4],
                                                   start=True, stop=True, skip_group_check=True), "mm")

        dma("sync", idb[:, :], identb[:, :], "ld")
        dma("sync", idf[:, :], identf[:, :], "ld")
        dma("sync", onc[:, :], onecf[:, :], "ld")
        dma("sync", onr[:, :], onerf[:, :], "ld")
        dma("sync", mks[:, :], mk32q[:, :], "ld")
        dma("sync", mg0[:, :], mq0[:, :], "ld")
        dma("sync", mg1[:, :], mq1[:, :], "ld")
        dma("sync", bnegc[:, :], bneg[:, :], "ld")
        dma("sync", bbc[:, :], bbv[:, :], "ld")
        dma("sync", lng[:, :], lngb[:, :], "ld")
        dma("sync", cbias[:, :], convb[:, :], "ld")
        dma("sync", fsc[:, :], fscl[:, :], "ld")
        for g in range(5):
            wait("sync", "ve")
            v = dma("sync", xf[:, 0:8 * 128], w5[:, g * 1024:(g + 1) * 1024], "x")
            wait("vector", "x", v)
            op("vector", lambda e, g=g: e.tensor_copy(wq[:, g * 1024:(g + 1) * 1024], xf[:, 0:1024]), "ve")
        wait("sync", "ve")
        v = dma("sync", xf[:, 0:8], wbcol[:, :], "x")
        wait("vector", "x", v)
        op("vector", lambda e: e.tensor_copy(wq[:, 5120:5128], xf[:, 0:8]), "ve")
        v = dma("sync", xf[:, 8:20], convw[:, :], "x")
        wait("vector", "x", v)
        for t in range(12):
            op("vector", lambda e, t=t: e.tensor_scalar(
                cdiag[:, t * 128:(t + 1) * 128], idb[:, :], xf[:, 8 + t:9 + t], 0.0, OP.mult, OP.add))
        op("vector", lambda e: e.memset(epsc[:, :], 1e-5))
        op("vector", lambda e: e.memset(send[:, :], 0.0))
        op("vector", lambda e: e.memset(recv[:, :], 0.0))
        op("vector", lambda e: e.memset(halo[:, :], 0.0))
        op("vector", lambda e: e.memset(st_f[:, :], 0.0))
        op("vector", lambda e: e.memset(st_b[:, :], 0.0))
        op("vector", lambda e: e.memset(gbd0[:, :], 0.0))
        op("vector", lambda e: e.memset(gbd1[:, :], 0.0))
        op("vector", lambda e: e.memset(gbdx[:, :], 0.0))
        op("vector", lambda e: e.memset(gq01[:, :], 0.0))
        op("vector", lambda e: e.memset(kct[:, :], 0.0))
        op("vector", lambda e: e.memset(rstk[:, :], 0.0))
        op("vector", lambda e: e.memset(invs[:, :], 1.0))
        op("vector", lambda e: e.memset(bstk[:, :], 0.0))
        op("vector", lambda e: e.memset(bstn[:, :], 0.0), "ve")
        wait("tensor", "ve")
        wait("scalar", "ve")

        q_t = lambda b: qT_[:, b * TT:(b + 1) * TT]
        k_t = lambda b: kT_[:, b * TT:(b + 1) * TT]
        v_t = lambda b: vT_[:, b * TT:(b + 1) * TT]
        g_t = lambda b: gT_[:, b * TT:(b + 1) * TT]
        sp_t = lambda b: spT_[:, b * TT:(b + 1) * TT]
        ln_t = lambda b: lnT_[:, b * TT:(b + 1) * TT]
        be_t = lambda b: beta_[:, b * TT:(b + 1) * TT]
        st_fb = lambda b: st_f[:, b * 128:(b + 1) * 128]
        st_bb = lambda b: st_b[:, b * 128:(b + 1) * 128]

        def emit_proj(tt, b):
            t0 = tt * TT
            wait("sync", "ve")
            xv = 0
            for c in range(8):
                xv = dma("sync", xf[:, c * TT:(c + 1) * TT], xfm[b, c, :, t0:t0 + TT], "x", serial=(c == 0))
            wait("vector", "x", xv)
            wait("vector", "mm")
            for c in range(8):
                vx = op("vector", lambda e, c=c: e.tensor_copy(xb[:, c * TT:(c + 1) * TT], xf[:, c * TT:(c + 1) * TT]),
                        "ve" if c == 7 else None)
            wait("tensor", "ve", vx)
            for ti in range(3):
                for c in range(8):
                    mm = op("tensor", lambda e, c=c, ti=ti: e.matmul(
                        pb[0][:, 0:TT], wq[:, (ti * 8 + c) * 128:(ti * 8 + c + 1) * 128],
                        xb[:, c * TT:(c + 1) * TT], start=(c == 0), stop=(c == 7)),
                        "mm" if c == 7 else None)
                wait("scalar", "mm", mm)
                wait("scalar", "ve")
                sc = op("scalar", lambda e: e.copy(pre[:, 3:3 + TT], pb[0][:, 0:TT]), "sc")
                wait("vector", "sc", sc)
                hi = (ti * B + b) * 3 + (tt % 2) * 36
                ho = (ti * B + b) * 3 + ((tt + 1) % 2) * 36
                op("vector", lambda e, hi=hi: e.tensor_copy(pre[:, 0:3], halo[:, hi:hi + 3]))
                op("vector", lambda e, ho=ho: e.tensor_copy(halo[:, ho:ho + 3], pre[:, TT:TT + 3]))
                cv = op("vector", lambda e: e.tensor_copy(preb[:, :], pre[:, :]), "ve")
                wait("tensor", "ve", cv)
                for j in range(4):
                    mm = op("tensor", lambda e, ti=ti, j=j: e.matmul(
                        pb[1][:, 0:TT], cdiag[:, (ti * 4 + j) * 128:(ti * 4 + j + 1) * 128],
                        preb[:, j:j + TT], start=(j == 0), stop=(j == 3)),
                        "mm" if j == 3 else None)
                wait("scalar", "mm", mm)
                dstt = (q_t(b), k_t(b), v_t(b))[ti]
                scv = op("scalar", lambda e, ti=ti, dstt=dstt: e.activation(
                    dstt, pb[1][:, 0:TT], AF.Silu, bias=cbias[:, ti:ti + 1], scale=1.0), "sc")
                if ti == 1:
                    wait("vector", "sc", scv)
                    op("vector", lambda e, dstt=dstt: e.tensor_scalar(dstt, dstt, DKI, 0.0, OP.mult, OP.add), "ve")
            for c in range(8):
                mm = op("tensor", lambda e, c=c: e.matmul(
                    pb[0][:, 0:TT], wq[:, (24 + c) * 128:(24 + c + 1) * 128],
                    xb[:, c * TT:(c + 1) * TT], start=(c == 0), stop=(c == 7)),
                    "mm" if c == 7 else None)
            wait("scalar", "mm", mm)
            op("scalar", lambda e: e.activation(xf[:, 0:TT], pb[0][:, 0:TT], AF.Sigmoid,
                                                bias=bnegc[:, 0:1], scale=1.0))
            op("scalar", lambda e, b=b: e.activation(sp_t(b), xf[:, 0:TT], AF.Ln), "sc")
            wait("tensor", "sc")
            for c in range(8):
                mm = op("tensor", lambda e, c=c: e.matmul(
                    pb[1][:, 0:TT], wq[:, (32 + c) * 128:(32 + c + 1) * 128],
                    xb[:, c * TT:(c + 1) * TT], start=(c == 0), stop=(c == 7)),
                    "mm" if c == 7 else None)
            wait("scalar", "mm", mm)
            op("scalar", lambda e, b=b: e.activation(g_t(b), pb[1][:, 0:TT], AF.Sigmoid), "sc")
            wait("tensor", "sc")
            for c in range(8):
                mm = op("tensor", lambda e, c=c: e.matmul(
                    pb[0][0:1, 0:TT], wq[:, 5120 + c:5121 + c],
                    xb[:, c * TT:(c + 1) * TT], start=(c == 0), stop=(c == 7)),
                    "mm" if c == 7 else None)
            wait("scalar", "mm", mm)
            scb = op("scalar", lambda e, b=b: e.activation(be_t(b), pb[0][0:1, 0:TT], AF.Sigmoid,
                                                           bias=bbc[0:1, 0:1], scale=1.0), "sc")
            wait("vector", "sc", scb)
            for c in range(NCH):
                op("vector", lambda e, b=b, c=c: e.tensor_tensor_scan(
                    out=ln_t(b)[:, c * CH:(c + 1) * CH], data0=sp_t(b)[:, c * CH:(c + 1) * CH],
                    data1=sp_t(b)[:, c * CH:(c + 1) * CH], initial=0.0, op0=OP.add, op1=OP.bypass),
                   "ve" if c == NCH - 1 else None)

        def emit_prep(tt, b, c):
            tp = TP[b]
            r0, r1 = 32 * b, 32 * b + 32
            c0 = c * CH
            Kc = k_t(b)[:, c0:c0 + CH]
            Qc = q_t(b)[:, c0:c0 + CH]
            lnc = ln_t(b)[:, c0:c0 + CH]
            op("vector", lambda e, lnc=lnc: e.tensor_scalar(mcol[:, 0:1], lnc[:, CH - 1:CH], 0.5, 0.0, OP.mult, OP.add))
            mv = op("vector", lambda e, lnc=lnc: e.tensor_scalar(mcol[:, 1:2], lnc[:, CH - 1:CH], -0.5, 0.0, OP.mult, OP.add), "ve")
            wait("scalar", "ve", mv)
            wait("scalar", "mm")
            op("scalar", lambda e, lnc=lnc: e.activation(efm[:, 0:64], lnc, AF.Exp, scale=1.0))
            e2v = op("scalar", lambda e, lnc=lnc: e.activation(efm[:, 64:128], lnc, AF.Exp, bias=mcol[:, 1:2], scale=1.0), "sc")
            wait("vector", "sc", e2v)
            tsub = op("vector", lambda e, b=b, c0=c0, lnc=lnc: e.tensor_tensor(
                rhsp[:, :], lnc, sp_t(b)[:, c0:c0 + CH], OP.subtract), "ve")
            wait("scalar", "ve", tsub)
            e3 = op("scalar", lambda e: e.activation(efm[:, 128:192], rhsp[:, :], AF.Exp, bias=mcol[:, 0:1], scale=-1.0), "sc")
            wait("vector", "sc", e3)
            op("vector", lambda e, b=b, c0=c0: e.tensor_tensor(rhsp[:, :], efm[:, 128:192], v_t(b)[:, c0:c0 + CH], OP.mult))
            op("vector", lambda e: e.tensor_scalar(rhsp[:, :], rhsp[:, :], -1.0, 0.0, OP.mult, OP.add))
            op("vector", lambda e, b=b: e.tensor_copy(ecolP[:, b:b + 1], efm[:, 63:64]))
            op("vector", lambda e, b=b: e.tensor_scalar(ecolN[:, 2 * b:2 * b + 1], efm[:, 63:64], -1.0, 0.0, OP.mult, OP.add))
            rr = op("vector", lambda e, b=b: e.tensor_scalar(ecolN[:, 2 * b + 1:2 * b + 2], efm[:, 127:128], -1.0, 0.0, OP.mult, OP.add), "ve")
            if _DBG_PREP < 2:
                return
            wait("tensor", "ve", rr)
            pbt = pb[b % 2]
            op("tensor", lambda e, pbt=pbt, r0=r0, r1=r1, b=b, c0=c0, tp=tp: e.matmul(
                pbt[r0:r1, 0:128], sp_t(b)[:, c0:c0 + 32], idf[:, :], start=True, stop=True, tile_position=tp, skip_group_check=True))
            op("tensor", lambda e, pbt=pbt, r0=r0, r1=r1, b=b, c0=c0, tp=tp: e.matmul(
                pbt[r0:r1, 128:256], sp_t(b)[:, c0 + 32:c0 + 64], idf[:, :], start=True, stop=True, tile_position=tp, skip_group_check=True))
            op("tensor", lambda e, pbt=pbt, r0=r0, r1=r1, tp=tp: e.matmul(
                pbt[r0:r1, 256:384], rhsp[:, 0:32], idf[:, :], start=True, stop=True, tile_position=tp, skip_group_check=True))
            op("tensor", lambda e, pbt=pbt, r0=r0, r1=r1, tp=tp: e.matmul(
                pbt[r0:r1, 384:512], rhsp[:, 32:64], idf[:, :], start=True, stop=True, tile_position=tp, skip_group_check=True))
            if _DBG_PREP < 3:
                return
            op("tensor", lambda e, b=b, c0=c0, tp=tp: e.matmul(
                pb[4][32 * b:32 * b + 32, 0:1], be_t(b)[0:1, c0:c0 + 32], onc[0:1, 0:1], start=True, stop=True, tile_position=tp, skip_group_check=True))
            tb = op("tensor", lambda e, b=b, c0=c0, tp=tp: e.matmul(
                pb[4][32 * b:32 * b + 32, 1:2], be_t(b)[0:1, c0 + 32:c0 + 64], onc[0:1, 0:1], start=True, stop=True, tile_position=tp, skip_group_check=True), "mm")
            if _DBG_PREP < 4:
                return
            op("tensor", lambda e: e.matmul(pb[2][0:64, 0:128], efm[:, 0:64], idf[:, :], start=True, stop=True, skip_group_check=True))
            op("tensor", lambda e: e.matmul(pb[2][0:64, 128:256], efm[:, 64:128], idf[:, :], start=True, stop=True, skip_group_check=True))
            if _DBG_PREP < 5:
                return
            op("tensor", lambda e, Kc=Kc, b=b, tp=tp: e.matmul(
                pb[3][32 * b:32 * b + 32, 0:64], Kc[:, 0:32], Kc[:, :], start=True, stop=True, tile_position=(0, 32 * b), skip_group_check=True))
            op("tensor", lambda e, Kc=Kc, b=b: e.matmul(
                pb[3][32 * b:32 * b + 32, 64:96], Kc[:, 32:64], Kc[:, 32:64], start=True, stop=True, tile_position=(0, 32 * b), skip_group_check=True))
            op("tensor", lambda e, Kc=Kc, Qc=Qc, b=b: e.matmul(
                pb[3][32 * b:32 * b + 32, 128:192], Kc[:, 0:32], Qc[:, :], start=True, stop=True, tile_position=(0, 32 * b), skip_group_check=True))
            op("tensor", lambda e, Kc=Kc, Qc=Qc, b=b: e.matmul(
                pb[3][32 * b:32 * b + 32, 192:256], Kc[:, 32:64], Qc[:, :], start=True, stop=True, tile_position=(0, 32 * b), skip_group_check=True))
            if _DBG_PREP < 6:
                return
            op("tensor", lambda e, Kc=Kc, b=b, tp=tp: e.matmul(
                pb[4][32 * b:32 * b + 32, 128:256], Kc[:, 0:32], idb[:, :], start=True, stop=True, tile_position=tp, skip_group_check=True))
            op("tensor", lambda e, Kc=Kc, b=b, tp=tp: e.matmul(
                pb[4][32 * b:32 * b + 32, 256:384], Kc[:, 32:64], idb[:, :], start=True, stop=True, tile_position=tp, skip_group_check=True))
            tv = bar_mm()
            if _DBG_PREP < 7:
                return
            wait("scalar", "mm", tv)
            op("scalar", lambda e, pbt=pbt, r0=r0, r1=r1: e.activation(invs[r0:r1, 0:128], pbt[r0:r1, 0:128], AF.Exp, scale=-1.0))
            op("scalar", lambda e, pbt=pbt, r0=r0, r1=r1: e.activation(invs[r0:r1, 128:256], pbt[r0:r1, 128:256], AF.Exp, scale=-1.0))
            sv = op("scalar", lambda e, b=b: e.copy(kct[32 * b:32 * b + 32, :], pb[4][32 * b:32 * b + 32, 128:384]), "sc")
            if _DBG_PREP < 8:
                wait("tensor", "sc", sv)
                return
            op("scalar", lambda e, r0=r0, r1=r1: e.copy(bstk[r0:r1, 0:2], pb[4][r0:r1, 0:2]))
            op("scalar", lambda e, r0=r0, r1=r1: e.mul(bstn[r0:r1, 0:2], pb[4][r0:r1, 0:2], -1.0))
            op("scalar", lambda e, pbt=pbt, r0=r0, r1=r1: e.copy(rstk[r0:r1, 128:256], pbt[r0:r1, 256:384]))
            sv = op("scalar", lambda e, pbt=pbt, r0=r0, r1=r1: e.copy(rstk[r0:r1, 384:512], pbt[r0:r1, 384:512]), "sc")
            wait("vector", "mm", tv)
            op("vector", lambda e, b=b: e.tensor_copy(e1t[:, b * 128:(b + 1) * 128], pb[2][0:64, 0:128]))
            esv = op("vector", lambda e, b=b: e.tensor_copy(e2t[:, b * 128:(b + 1) * 128], pb[2][0:64, 128:256]), "ve")

            op("vector", lambda e, r0=r0, r1=r1: e.tensor_tensor(gbd0[r0:r1, r0:r1], pb[3][r0:r1, 0:32], mks[r0:r1, :], OP.mult))
            op("vector", lambda e, r0=r0, r1=r1: e.tensor_scalar(gbdx[r0:r1, r0:r1], pb[3][r0:r1, 32:64], -1.0, 0.0, OP.mult, OP.add))
            op("vector", lambda e, r0=r0, r1=r1: e.tensor_tensor(gbd1[r0:r1, r0:r1], pb[3][r0:r1, 64:96], mks[r0:r1, :], OP.mult))
            op("vector", lambda e, r0=r0, r1=r1: e.tensor_tensor(gq01[r0:r1, 0:64], pb[3][r0:r1, 128:192], mg0[r0:r1, :], OP.mult))
            vv = op("vector", lambda e, r0=r0, r1=r1: e.tensor_tensor(gq01[r0:r1, 64:128], pb[3][r0:r1, 192:256], mg1[r0:r1, :], OP.mult), "ve")
            wait("tensor", "ve", vv)
            wait("tensor", "sc", sv)

        def emit_chain(tt, c):
            c0 = c * CH
            wait("vector", "sc")
            op("vector", lambda e: e.tensor_scalar(cp0[:, 0:128], invs[:, 0:128], bstk[:, 0:1], 0.0, OP.mult, OP.add))
            op("vector", lambda e: e.tensor_copy(cp0[:, 128:256], cp0[:, 0:128]))
            op("vector", lambda e: e.tensor_scalar(cp1[:, 0:128], invs[:, 128:256], bstk[:, 1:2], 0.0, OP.mult, OP.add))
            cpv = op("vector", lambda e: e.tensor_copy(cp1[:, 128:256], cp1[:, 0:128]), "ve")
            wait("tensor", "ve", cpv)
            op("tensor", lambda e: e.matmul(pb[2][:, 0:256], idb[:, :], rstk[:, 0:256], start=True, stop=True))
            for b in range(B):
                op("tensor", lambda e, b=b, c0=c0: e.matmul(
                    pb[2][32 * b:32 * b + 32, 0:128], k_t(b)[:, c0:c0 + 32], st_bb(b),
                    start=False, stop=False, tile_position=(0, 32 * b),
                    skip_group_check=True))
            mmv = bar_mm()
            wait("vector", "mm", mmv)
            dv = op("vector", lambda e: e.tensor_tensor(delta3[0][:, :], pb[2][:, 0:256], cp0[:, :], OP.mult), "ve")
            wait("tensor", "ve", dv)
            for j in range(NIT):
                d = delta3[j % 2]
                mmv = op("tensor", lambda e, d=d: e.matmul(pb[3][:, 0:256], gbd0[:, :], d[:, :], start=True, stop=True), "mm")
                op("tensor", lambda e, d=d, j=j: e.matmul(pb[4][:, 0:256], idb[:, :], d[:, :], start=(j == 0), stop=False))
                wait("vector", "mm", mmv)
                dn = delta3[(j + 1) % 2]
                dv = op("vector", lambda e, dn=dn: e.tensor_tensor(dn[:, :], pb[3][:, 0:256], cp0[:, :], OP.mult), "ve", nodrain=True)
                wait("tensor", "ve", dv)
            mmv = op("tensor", lambda e: e.matmul(pb[4][:, 0:256], idb[:, :], delta3[NIT % 2][:, :], start=False, stop=True), "mm")
            wait("scalar", "mm", mmv)
            uv = op("scalar", lambda e: e.copy(u0[:, :], pb[4][:, 0:256]), "sc")
            wait("tensor", "sc", uv)
            op("tensor", lambda e: e.matmul(pb[5][:, 0:256], idb[:, :], rstk[:, 256:512], start=True, stop=True))
            for b in range(B):
                op("tensor", lambda e, b=b, c0=c0: e.matmul(
                    pb[5][32 * b:32 * b + 32, 0:128], k_t(b)[:, c0 + 32:c0 + 64], st_bb(b),
                    start=False, stop=False, tile_position=(0, 32 * b),
                    skip_group_check=True))
            mmv = op("tensor", lambda e: e.matmul(pb[5][:, 0:256], gbdx[:, :], u0[:, :], start=False, stop=True,
                                                  skip_group_check=True), "mm")
            wait("vector", "mm", mmv)
            dv = op("vector", lambda e: e.tensor_tensor(delta3[0][:, :], pb[5][:, 0:256], cp1[:, :], OP.mult), "ve")
            wait("tensor", "ve", dv)
            for j in range(NIT):
                d = delta3[j % 2]
                mmv = op("tensor", lambda e, d=d: e.matmul(pb[3][:, 0:256], gbd1[:, :], d[:, :], start=True, stop=True), "mm")
                op("tensor", lambda e, d=d, j=j: e.matmul(pb[4][:, 0:256], idb[:, :], d[:, :], start=(j == 0), stop=False))
                wait("vector", "mm", mmv)
                dn = delta3[(j + 1) % 2]
                dv = op("vector", lambda e, dn=dn: e.tensor_tensor(dn[:, :], pb[3][:, 0:256], cp1[:, :], OP.mult), "ve", nodrain=True)
                wait("tensor", "ve", dv)
            mmv = op("tensor", lambda e: e.matmul(pb[4][:, 0:256], idb[:, :], delta3[NIT % 2][:, :], start=False, stop=True), "mm")
            wait("scalar", "mm", mmv)
            uv = op("scalar", lambda e: e.copy(u1[:, :], pb[4][:, 0:256]), "sc")
            wait("tensor", "sc", uv)
            wait("vector", "sc", uv)

        def emit_post(tt, b, c):
            tp = TP[b]
            r0, r1 = 32 * b, 32 * b + 32
            c0 = c * CH
            slot = 2 * b + (tt % 2)
            op("tensor", lambda e, b=b, c0=c0: e.matmul(
                pb[2][0:64, 0:128], q_t(b)[:, c0:c0 + CH], st_bb(b), start=True, stop=True, skip_group_check=True))
            op("tensor", lambda e, r0=r0, r1=r1: e.matmul(
                pb[2][0:64, 0:128], gq01[r0:r1, 0:64], u0[r0:r1, 0:128], start=False, stop=False, tile_position=(r0, 0), skip_group_check=True))
            op("tensor", lambda e, r0=r0, r1=r1: e.matmul(
                pb[2][0:64, 0:128], gq01[r0:r1, 64:128], u1[r0:r1, 0:128], start=False, stop=False, tile_position=(r0, 0), skip_group_check=True))
            op("tensor", lambda e, r0=r0, r1=r1: e.matmul(
                pb[2][0:64, 128:256], gq01[r0:r1, 0:64], u0[r0:r1, 128:256], start=True, stop=True, tile_position=(r0, 0), skip_group_check=True))
            op("tensor", lambda e, r0=r0, r1=r1: e.matmul(
                pb[2][0:64, 128:256], gq01[r0:r1, 64:128], u1[r0:r1, 128:256], start=False, stop=False, tile_position=(r0, 0), skip_group_check=True))
            om = bar_mm()
            op("tensor", lambda e, r0=r0, r1=r1: e.matmul(
                pb[1][:, 0:128], u0[r0:r1, 0:128], kct[r0:r1, 0:128], start=True, stop=True, tile_position=(r0, 0), skip_group_check=True))
            op("tensor", lambda e, r0=r0, r1=r1: e.matmul(
                pb[1][:, 0:128], u1[r0:r1, 0:128], kct[r0:r1, 128:256], start=False, stop=False, tile_position=(r0, 0), skip_group_check=True))
            op("tensor", lambda e, r0=r0, r1=r1: e.matmul(
                pb[1][:, 128:256], u0[r0:r1, 128:256], kct[r0:r1, 0:128], start=True, stop=True, tile_position=(r0, 0), skip_group_check=True))
            op("tensor", lambda e, r0=r0, r1=r1: e.matmul(
                pb[1][:, 128:256], u1[r0:r1, 128:256], kct[r0:r1, 128:256], start=False, stop=False, tile_position=(r0, 0), skip_group_check=True))
            sm = bar_mm()
            wait("vector", "mm", om)
            if _DBG_DUMP and b == 3:
                op("vector", lambda e: e.tensor_copy(ocdump[0:64, 0:128], pb[2][0:64, 0:128]))
                op("vector", lambda e: e.tensor_copy(ocdump[0:64, 128:256], pb[2][0:64, 128:256]))
            op("vector", lambda e, b=b: e.tensor_tensor(oS[:, :], pb[2][0:64, 0:128], e1t[:, b * 128:(b + 1) * 128], OP.mult))
            op("vector", lambda e, b=b: e.tensor_tensor(oS2[:, :], pb[2][0:64, 128:256], e2t[:, b * 128:(b + 1) * 128], OP.mult))
            op("vector", lambda e: e.tensor_tensor(oS[:, :], oS[:, :], oS2[:, :], OP.add))
            op("vector", lambda e: e.tensor_reduce(ocol[:, 0:1], oS[:, :], mybir.AxisListType.X, OP.add))
            op("vector", lambda e: e.tensor_scalar(ocol[:, 0:1], ocol[:, 0:1], 1.0 / 128.0, 0.0, OP.mult, OP.add))
            ov = op("vector", lambda e: e.tensor_scalar(oS[:, :], oS[:, :], ocol[:, 0:1], 0.0, OP.subtract, OP.add), "ve")
            wait("scalar", "ve", ov)
            sq = op("scalar", lambda e: e.activation(oS2[:, :], oS[:, :], AF.Square), "sc")
            wait("vector", "sc", sq)
            rv = op("vector", lambda e: e.tensor_reduce(ocol[:, 1:2], oS2[:, :], mybir.AxisListType.X, OP.add), "ve")
            wait("scalar", "ve", rv)
            sq = op("scalar", lambda e: e.activation(ocol[:, 1:2], ocol[:, 1:2], AF.Sqrt, bias=epsc[0:64, 0:1], scale=1.0 / 128.0), "sc")
            wait("vector", "sc", sq)
            op("vector", lambda e: e.reciprocal(ocol[:, 1:2], ocol[:, 1:2]))
            onv = op("vector", lambda e: e.tensor_scalar(oS[:, :], oS[:, :], ocol[:, 1:2], 0.0, OP.mult, OP.add), "ve")
            wait("tensor", "ve", onv)
            ot = op("tensor", lambda e: e.matmul(pb[2][0:128, 256:320], oS[:, :], idf[0:64, 0:64], start=True, stop=True, skip_group_check=True), "mm")
            wait("vector", "mm", ot)
            og = op("vector", lambda e: e.tensor_scalar(ognm[:, :], pb[2][0:128, 256:320],
                                                        lng[:, 0:1], lng[:, 1:2], OP.mult, OP.add), "ve")
            gv = op("vector", lambda e, b=b, c0=c0, slot=slot, c=c: e.tensor_tensor(
                send[:, slot * 512 + c * 64:slot * 512 + (c + 1) * 64], ognm[:, :],
                g_t(b)[:, c0:c0 + CH], OP.mult), "ve")
            wait("vector", "mm", sm)
            op("vector", lambda e, b=b: e.tensor_scalar(sscr_state[:, :], st_fb(b), ecolP[:, b:b + 1], 0.0, OP.mult, OP.add))
            op("vector", lambda e, b=b: e.scalar_tensor_tensor(
                out=sscr_state[:, :], in0=pb[1][:, 0:128], scalar=ecolN[:, 2 * b:2 * b + 1],
                in1=sscr_state[:, :], op0=OP.mult, op1=OP.add))
            sv = op("vector", lambda e, b=b: e.scalar_tensor_tensor(
                out=st_fb(b), in0=pb[1][:, 128:256], scalar=ecolN[:, 2 * b + 1:2 * b + 2],
                in1=sscr_state[:, :], op0=OP.mult, op1=OP.add), "ve")
            cb = op("vector", lambda e, b=b: e.tensor_copy(sbf[:, :], st_fb(b)), "ve")
            wait("tensor", "ve", cb)
            st_tr = op("tensor", lambda e: e.matmul(pb[1][:, 256:384], sbf[:, :], idb[:, :], start=True, stop=True, skip_group_check=True), "mm")
            wait("scalar", "mm", st_tr)
            sv2 = op("scalar", lambda e, b=b: e.copy(st_bb(b), pb[1][:, 256:384]), "sc")
            wait("tensor", "sc", sv2)
            wait("vector", "sc", sv2)


        def emit_exchange(e_):
            wait("gpsimd", "ve")
            for s in range(8):
                dma("gpsimd", ibs[e_].ap()[s, :, :], send[:, s * 512:(s + 1) * 512], "sd", serial=(s == 0))
            wait("gpsimd", "sd")
            op("gpsimd", lambda e, e_=e_: _cc_alltoall(e, ibs[e_].ap().opt(), obs[e_].ap().opt()), "cc")
            wait("sync", "cc", cnt["cc"])
            for r in range(8):
                dma("sync", recv[:, r * 512:(r + 1) * 512], obs[e_].ap()[r, :, :], "rv", serial=(r == 0))

        def emit_p2(e_):
            wait("tensor", "rv", cnt["rv"])
            wait("sync", "ve")
            wait("vector", "w")
            wait("vector", "x")
            wait("vector", "mm")
            xrv = 0
            for oc in range(8):
                wait("sync", "ve")
                wait("sync", "mm")
                wait("sync", "gp")
                wv = dma("sync", xf[:, 0:1024], woT[oc, :, :], "x")
                wait("vector", "x", wv)
                cv = op("vector", lambda e: e.tensor_copy(wst[:, 0:1024], xf[:, 0:1024]), "ve")
                wait("tensor", "ve", cv)
                for r in range(8):
                    mm = op("tensor", lambda e, r=r, oc=oc: e.matmul(
                        pb[5][:, 0:512], wst[:, r * 128:(r + 1) * 128],
                        recv[:, r * 512:(r + 1) * 512], start=(r == 0), stop=(r == 7)),
                        "mm" if r == 7 else None)
                xrv = dma("sync", x1f[:, :], xre[oc, :, e_ * 512:(e_ + 1) * 512], "x")
                wait("vector", "mm", mm)
                wait("vector", "x", xrv)
                rv1 = op("vector", lambda e: e.tensor_tensor(x1f[:, :], pb[5][:, 0:512], x1f[:, :], OP.add), "ve")
                wait("scalar", "ve", rv1)
                sv = op("scalar", lambda e: e.activation(sqs[:, :], x1f[:, :], AF.Square), "sc")
                wait("tensor", "sc", sv)
                op("tensor", lambda e, oc=oc: e.matmul(pb[4][0:1, 0:512], onc[:, :], x1f[:, :],
                                                       start=(oc == 0), stop=(oc == 7)))
                mm = op("tensor", lambda e, oc=oc: e.matmul(pb[3][0:1, 0:512], onc[:, :], sqs[:, :],
                                                            start=(oc == 0), stop=(oc == 7)), "mm")
                wait("gpsimd", "mm", mm)
                dv = dma("gpsimd", x1d.ap()[:, oc * 512:(oc + 1) * 512], x1f[:, :], "gp")
            wait("vector", "mm", mm)
            op("vector", lambda e: e.tensor_copy(rowa[:, :], pb[4][0:1, 0:512]))
            rv2 = op("vector", lambda e: e.tensor_copy(rowb[:, :], pb[3][0:1, 0:512]), "ve")
            emit_ln_rows(rv2)
            wait("tensor", "ve")
            bc = op("tensor", lambda e: e.matmul(pb[0][:, 0:512], onr[:, :], rowa[:, :], start=True, stop=True))
            bc = op("tensor", lambda e: e.matmul(pb[1][:, 0:512], onr[:, :], rowb[:, :], start=True, stop=True), "mm")
            wait("vector", "mm", bc)
            wait("scalar", "gp", cnt["gp"])
            for oc in range(8):
                wait("scalar", "ld")
                xv = dma("scalar", x1f[:, :], x1d.ap()[:, oc * 512:(oc + 1) * 512], "x")
                wait("vector", "x", xv)
                op("vector", lambda e: e.tensor_tensor(x1f[:, :], x1f[:, :], pb[0][:, 0:512], OP.subtract))
                op("vector", lambda e: e.tensor_tensor(x1f[:, :], x1f[:, :], pb[1][:, 0:512], OP.mult))
                cv = op("vector", lambda e, oc=oc: e.tensor_copy(x1b[:, oc * 512:(oc + 1) * 512], x1f[:, :]), "ve")
                wait("scalar", "ve", cv)
                dma("scalar", x1d.ap()[:, oc * 512:(oc + 1) * 512], x1f[:, :], "ld")
            wait("tensor", "ve", cnt["ve"])
            slot_mm = [cnt["mm"], cnt["mm"]]
            sc_done = [cnt["sc"], cnt["sc"]]
            ve_done = [cnt["ve"], cnt["ve"]]
            bkA = [pb[0], pb[2]]
            bkB = [pb[1], pb[5]]
            for oc in range(32):
                par = oc % 2
                sl = par * 4096
                wk = "w" if par == 0 else "x"
                pa, pbb = bkA[par], bkB[par]
                hb = hs0 if par == 0 else hs1
                wait("sync", "mm", slot_mm[par])
                wv = dma("sync", wst[:, sl:sl + 1024], w1T[oc, :, :], wk)
                wv = dma("sync", wst[:, sl + 1024:sl + 2048], w2T[oc, :, :], wk, serial=False)
                wait("tensor", wk, wv)
                wait("tensor", "ve", ve_done[par])
                wait("tensor", "sc", sc_done[par])
                for r in range(8):
                    op("tensor", lambda e, r=r, sl=sl, pa=pa: e.matmul(
                        pa[:, 0:512], wst[:, sl + r * 128:sl + (r + 1) * 128],
                        x1b[:, r * 512:(r + 1) * 512], start=(r == 0), stop=(r == 7)))
                for r in range(8):
                    mm = op("tensor", lambda e, r=r, sl=sl, pbb=pbb: e.matmul(
                        pbb[:, 0:512], wst[:, sl + 1024 + r * 128:sl + 1024 + (r + 1) * 128],
                        x1b[:, r * 512:(r + 1) * 512], start=(r == 0), stop=(r == 7)),
                        "mm" if r == 7 else None)
                slot_mm[par] = mm
                wait("scalar", "mm", mm)
                hv = op("scalar", lambda e, oc=oc, pa=pa, hb=hb: e.activation(hb[:, :], pa[:, 0:512], AF.Silu,
                                                                              scale=fsc[:, oc:oc + 1]), "sc")
                sc_done[par] = hv
                wait("vector", "sc", hv)
                ve_done[par] = op("vector", lambda e, oc=oc, pbb=pbb, hb=hb: e.scalar_tensor_tensor(
                    out=ht[:, oc * 512:(oc + 1) * 512], in0=pbb[:, 0:512],
                    scalar=fsc[:, 32 + oc:33 + oc], in1=hb[:, :], op0=OP.mult, op1=OP.mult), "ve")
            for stage, (wmat, scoff) in enumerate(((h1T, 64), (h2T, 96))):
                slot_mm = [cnt["mm"], cnt["mm"]]
                sc_done = [cnt["sc"], cnt["sc"]]
                gp_done = [cnt["gp"], cnt["gp"]]
                wait("tensor", "ve", cnt["ve"])
                for oc in range(32):
                    par = oc % 2
                    sl = par * 4096
                    wk = "w" if par == 0 else "x"
                    pa = pb[par]
                    hb = hs0 if par == 0 else hs1
                    wait("sync", "mm", slot_mm[par])
                    wv = None
                    for piece in range(2):
                        wv = dma("sync", wst[:, sl + piece * 2048:sl + (piece + 1) * 2048],
                                 wmat[oc, :, piece * 2048:(piece + 1) * 2048], wk, serial=(piece == 0))
                    wait("tensor", wk, wv)
                    wait("tensor", "sc", sc_done[par])
                    for r in range(32):
                        mm = op("tensor", lambda e, r=r, sl=sl, pa=pa: e.matmul(
                            pa[:, 0:512], wst[:, sl + r * 128:sl + (r + 1) * 128],
                            ht[:, r * 512:(r + 1) * 512], start=(r == 0), stop=(r == 31)),
                            "mm" if r == 31 else None)
                    slot_mm[par] = mm
                    wait("scalar", "mm", mm)
                    wait("scalar", "gp", gp_done[par])
                    hv = op("scalar", lambda e, oc=oc, scoff=scoff, hb=hb, pa=pa: e.activation(
                        hb[:, :], pa[:, 0:512], AF.Gelu, scale=fsc[:, scoff + oc:scoff + oc + 1]), "sc")
                    sc_done[par] = hv
                    wait("gpsimd", "sc", hv)
                    gp_done[par] = dma("gpsimd", hd.ap()[:, oc * 512:(oc + 1) * 512], hb[:, :], "gp")
                wait("scalar", "gp", cnt["gp"])
                wait("scalar", "mm")
                hv = 0
                for piece in range(8):
                    hv = dma("scalar", ht[:, piece * 2048:(piece + 1) * 2048],
                             hd.ap()[:, piece * 2048:(piece + 1) * 2048], "x", serial=(piece == 0))
                wait("tensor", "x", hv)
            slot_mm = [cnt["mm"], cnt["mm"]]
            ve_done = [cnt["ve"], cnt["ve"]]
            gp_last = cnt["gp"]
            for oc in range(8):
                par = oc % 2
                sl = par * 4096
                wk = "w" if par == 0 else "x"
                pa = pb[par]
                wait("sync", "mm", slot_mm[par])
                wv = None
                for piece in range(2):
                    wv = dma("sync", wst[:, sl + piece * 2048:sl + (piece + 1) * 2048],
                             w3T[oc, :, piece * 2048:(piece + 1) * 2048], wk, serial=(piece == 0))
                wait("tensor", wk, wv)
                if oc < 2:
                    wait("tensor", "sc")
                wait("tensor", "ve", ve_done[par])
                for r in range(32):
                    mm = op("tensor", lambda e, r=r, sl=sl, pa=pa: e.matmul(
                        pa[:, 0:512], wst[:, sl + r * 128:sl + (r + 1) * 128],
                        ht[:, r * 512:(r + 1) * 512], start=(r == 0), stop=(r == 31)),
                        "mm" if r == 31 else None)
                slot_mm[par] = mm
                wait("scalar", "ld")
                xv = dma("scalar", x1f[:, :], x1d.ap()[:, oc * 512:(oc + 1) * 512], "x")
                wait("vector", "mm", mm)
                wait("vector", "x", xv)
                wait("vector", "gp", gp_last)
                rv1 = op("vector", lambda e, oc=oc, pa=pa: e.scalar_tensor_tensor(
                    out=r2f[:, :], in0=pa[:, 0:512], scalar=fsc[:, 128 + oc:129 + oc],
                    in1=x1f[:, :], op0=OP.mult, op1=OP.add), "ve")
                ve_done[par] = rv1
                wait("scalar", "ve", rv1)
                sv = op("scalar", lambda e: e.activation(sqs[:, :], r2f[:, :], AF.Square), "sc")
                wait("tensor", "sc", sv)
                op("tensor", lambda e, oc=oc: e.matmul(pb[4][0:1, 0:512], onc[:, :], r2f[:, :],
                                                       start=(oc == 0), stop=(oc == 7)))
                mm = op("tensor", lambda e, oc=oc: e.matmul(pb[3][0:1, 0:512], onc[:, :], sqs[:, :],
                                                            start=(oc == 0), stop=(oc == 7)), "mm")
                wait("gpsimd", "mm", mm)
                gp_last = dma("gpsimd", r2d.ap()[:, oc * 512:(oc + 1) * 512], r2f[:, :], "gp")
            wait("vector", "mm", mm)
            op("vector", lambda e: e.tensor_copy(rowa[:, :], pb[4][0:1, 0:512]))
            rv2 = op("vector", lambda e: e.tensor_copy(rowb[:, :], pb[3][0:1, 0:512]), "ve")
            emit_ln_rows(rv2)
            wait("tensor", "ve")
            bc = op("tensor", lambda e: e.matmul(pb[0][:, 0:512], onr[:, :], rowa[:, :], start=True, stop=True))
            bc = op("tensor", lambda e: e.matmul(pb[1][:, 0:512], onr[:, :], rowb[:, :], start=True, stop=True), "mm")
            wait("vector", "mm", bc)
            wait("vector", "gp", cnt["gp"])
            wait("scalar", "gp")
            for oc in range(8):
                wait("scalar", "ve")
                xv = dma("scalar", r2f[:, :], r2d.ap()[:, oc * 512:(oc + 1) * 512], "x")
                wait("vector", "x", xv)
                wait("vector", "st")
                op("vector", lambda e: e.tensor_tensor(outf[:, :], r2f[:, :], pb[0][:, 0:512], OP.subtract))
                ov = op("vector", lambda e: e.tensor_tensor(outf[:, :], outf[:, :], pb[1][:, 0:512], OP.mult), "ve")
                wait("sync", "ve", ov)
                dma("sync", outp[oc, :, e_ * 512:(e_ + 1) * 512], outf[:, :], "st")

        def emit_ln_rows(rv2):
            wait("scalar", "ve", rv2)
            op("scalar", lambda e: e.mul(rowa[:, :], rowa[:, :], 1.0 / 1024.0))
            op("scalar", lambda e: e.mul(rowb[:, :], rowb[:, :], 1.0 / 1024.0))
            sv = op("scalar", lambda e: e.activation(sqs[0:1, 0:512], rowa[:, :], AF.Square), "sc")
            wait("vector", "sc", sv)
            vv = op("vector", lambda e: e.tensor_tensor(rowb[:, :], rowb[:, :], sqs[0:1, 0:512], OP.subtract), "ve")
            wait("scalar", "ve", vv)
            sv = op("scalar", lambda e: e.activation(rowb[:, :], rowb[:, :], AF.Sqrt, bias=epsc[0:1, 0:1]), "sc")
            wait("vector", "sc", sv)
            op("vector", lambda e: e.reciprocal(rowb[:, :], rowb[:, :]), "ve")

        ntt = _DBG_TILES if _DBG_TILES is not None else NTT
        for tt in range(ntt):
            if tt >= 2:
                wait("vector", "sd", cnt["sd"])
            for b in range(B):
                emit_proj(tt, b)
            nch = _DBG_CHUNKS if _DBG_CHUNKS is not None else NCH
            for c in range(nch):
                if _DBG_STAGE >= 1:
                    for b in range(B):
                        emit_prep(tt, b, c)
                if _DBG_STAGE >= 2:
                    emit_chain(tt, c)
                if _DBG_STAGE >= 3:
                    for b in range(B):
                        emit_post(tt, b, c)
            if tt % 2 == 1 and _DBG_P2:
                emit_exchange(tt // 2)
                emit_p2(tt // 2)

        if _DBG_DUMP:
            wait("vector", "mm")
            op("vector", lambda e: e.tensor_copy(p3dump[:, :], pb[3][:, 0:256]), "ve")
            wait("sync", "ve")
            wait("sync", "sc")
            dma("sync", dbgo[:, :], send[:, :], "st")
            dma("sync", dbgs[:, :], st_f[:, :], "st")
            dma("sync", dbgq[:, :], qT_[:, :], "st")
            dma("sync", dbgk[:, :], kT_[:, :], "st")
            dma("sync", dbgv[:, :], vT_[:, :], "st")
            dma("sync", dbgsp[:, :], spT_[:, :], "st")
            dma("sync", dbgln[:, :], lnT_[:, :], "st")
            dma("sync", dbgbe[:, :], beta_[:, :], "st")
            dma("sync", dbgu[:, 0:256], u0[:, :], "st")
            dma("sync", dbgu[:, 256:512], u1[:, :], "st")
            dma("sync", dbgos[:, 0:128], oS[:, :], "st")
            dma("sync", dbgos[:, 128:256], oS2[:, :], "st")
            dma("sync", dbgoc[:, :], ocdump[:, :], "st")
            dma("sync", dbggq[:, 0:128], gq01[:, :], "st")
            dma("sync", dbggq[:, 128:256], gbd0[:, :], "st")
            dma("sync", dbgp3[:, :], p3dump[:, :], "st")
            dma("sync", dbgmk[:, 0:32], mks[:, :], "st")
            dma("sync", dbgmk[:, 32:96], mg0[:, :], "st")
            dma("sync", dbge[:, 0:512], e1t[:, :], "st")
            dma("sync", dbge[:, 512:1024], e2t[:, :], "st")
        wait("sync", "st", cnt["st"])

        blk = es.enter_context(nc.Block())

        @blk.sync
        def _(e):
            for f in prog["sync"]:
                f(e)

        @blk.tensor
        def _(e):
            for f in prog["tensor"]:
                f(e)

        @blk.vector
        def _(e):
            for f in prog["vector"]:
                f(e)

        @blk.scalar
        def _(e):
            for f in prog["scalar"]:
                f(e)

        @blk.gpsimd
        def _(e):
            for f in prog["gpsimd"]:
                f(e)

    return nc



_NC_CACHE = {}


def _consts():
    identb = np.eye(128, dtype=ml_dtypes.bfloat16)
    identf = np.eye(128, dtype=np.float32)
    onecf = np.ones((128, 1), np.float32)
    onerf = np.ones((1, 128), np.float32)
    r = np.arange(32)
    i32 = np.arange(32)
    m = -(r[:, None] < i32[None, :]).astype(np.float32)
    mk32q = np.tile(m, (4, 1))
    i64 = np.arange(64)
    mq0 = np.tile(-(r[:, None] <= i64[None, :]).astype(np.float32), (4, 1))
    mq1 = np.tile(-((r[:, None] + 32) <= i64[None, :]).astype(np.float32), (4, 1))
    return identb, identf, onecf, onerf, mk32q, mq0, mq1


def kernel(**inp):
    x = np.asarray(inp["L0_attn_x"], np.float32)
    xfm = np.ascontiguousarray(x.transpose(0, 2, 1)).reshape(B, 8, 128, L)

    def colslice(W, h):
        return np.ascontiguousarray(W[:, h * 128:(h + 1) * 128].reshape(8, 128, 128))

    Wq, Wk, Wv, Wa, Wg = (np.asarray(inp[f"L0_attn_W{n}"], np.float32) for n in "qkvag")
    Wb = np.asarray(inp["L0_attn_Wb"], np.float32)
    Wo = np.asarray(inp["L0_attn_Wo"], np.float32)
    ba = np.asarray(inp["L0_attn_ba"], np.float32)
    bb = np.asarray(inp["L0_attn_bb"], np.float32)
    cqw = np.asarray(inp["L0_attn_cqw"], np.float32)
    ckw = np.asarray(inp["L0_attn_ckw"], np.float32)
    cvw = np.asarray(inp["L0_attn_cvw"], np.float32)
    cqb = np.asarray(inp["L0_attn_cqb"], np.float32)
    ckb = np.asarray(inp["L0_attn_ckb"], np.float32)
    cvb = np.asarray(inp["L0_attn_cvb"], np.float32)
    lngv = np.asarray(inp["L0_attn_ln_g"], np.float32)
    lnbv = np.asarray(inp["L0_attn_ln_b"], np.float32)

    woTh = np.ascontiguousarray(Wo.reshape(8, 128, 8, 128).transpose(2, 1, 0, 3)).reshape(8, 128, 8 * 128)

    def ffn_lhsT(W, no, nk):
        Wr = W.reshape(no, 128, nk, 128)
        return np.ascontiguousarray(Wr.transpose(0, 3, 2, 1)).reshape(no, 128, nk * 128)

    w1w = np.asarray(inp["L0_ffn_w1_w"], np.float32)
    w2w = np.asarray(inp["L0_ffn_w2_w"], np.float32)
    h1w = np.asarray(inp["L0_ffn_h1_w"], np.float32)
    h2w = np.asarray(inp["L0_ffn_h2_w"], np.float32)
    w3w = np.asarray(inp["L0_ffn_w3_w"], np.float32)
    w1Tb = ffn_lhsT(w1w, 32, 8).astype(ml_dtypes.bfloat16)
    w2Tb = ffn_lhsT(w2w, 32, 8).astype(ml_dtypes.bfloat16)
    h1Tb = ffn_lhsT(h1w, 32, 32).astype(ml_dtypes.bfloat16)
    h2Tb = ffn_lhsT(h2w, 32, 32).astype(ml_dtypes.bfloat16)
    w3Tb = ffn_lhsT(w3w, 8, 32).astype(ml_dtypes.bfloat16)
    fsclv = np.zeros((128, 5 * 32), np.float32)
    for fi, sname in enumerate(("w1_s", "w2_s", "h1_s", "h2_s", "w3_s")):
        sv = np.asarray(inp["L0_ffn_" + sname], np.float32)
        nch = sv.shape[0] // 128
        fsclv[:, fi * 32:fi * 32 + nch] = sv.reshape(nch, 128).T

    identb, identf, onecf, onerf, mk32q, mq0, mq1 = _consts()

    if "nc" not in _NC_CACHE:
        _NC_CACHE["nc"] = build_nc()
    ncp = _NC_CACHE["nc"]

    in_maps = []
    for s in range(8):
        h = s
        mb, half = s // 2, s % 2
        xres = np.empty((8, 128, NRND * 512), np.float32)
        for e in range(NRND):
            xres[:, :, e * 512:(e + 1) * 512] = xfm[mb, :, :, e * 1024 + half * 512: e * 1024 + half * 512 + 512]
        w5v = np.concatenate([colslice(W, h) for W in (Wq, Wk, Wv, Wa, Wg)], 0)
        w5v = np.ascontiguousarray(w5v.transpose(1, 0, 2)).reshape(128, 40 * 128)
        convwv = np.stack([cqw[h * 128:(h + 1) * 128], ckw[h * 128:(h + 1) * 128],
                           cvw[h * 128:(h + 1) * 128]], 0)
        convwv = np.ascontiguousarray(convwv.transpose(1, 0, 2)).reshape(128, 12)
        convbv = np.stack([cqb[h * 128:(h + 1) * 128], ckb[h * 128:(h + 1) * 128],
                           cvb[h * 128:(h + 1) * 128]], 1)
        in_maps.append({
            "xfm": xfm, "xre": xres, "w5": w5v,
            "wbcol": np.ascontiguousarray(Wb[:, h].reshape(8, 128).T),
            "convw": convwv, "convb": np.ascontiguousarray(convbv),
            "bneg": ba[h * 128:(h + 1) * 128].reshape(128, 1).copy(),
            "bbv": bb[h].reshape(1, 1),
            "lngb": np.stack([lngv, lnbv], 1),
            "woT": woTh, "w1T": w1Tb, "w2T": w2Tb, "h1T": h1Tb, "h2T": h2Tb, "w3T": w3Tb,
            "fscl": fsclv, "identb": identb, "identf": identf,
            "onecf": onecf, "onerf": onerf, "mk32q": mk32q, "mq0": mq0, "mq1": mq1,
        })

    res = run_bass_kernel_spmd(ncp, in_maps, core_ids=list(range(8)))
    out = np.empty((B, L, D), np.float32)
    for s in range(8):
        mb, half = s // 2, s % 2
        o = res.results[s]["outp"]
        for e in range(NRND):
            seg = o[:, :, e * 512:(e + 1) * 512]
            t0 = e * 1024 + half * 512
            out[mb, t0:t0 + 512, :] = seg.transpose(2, 0, 1).reshape(512, D)
    return out


# version 08
# speedup vs baseline: 1.2384x; 1.1809x over previous
import contextlib
import math
import numpy as np
import ml_dtypes
import concourse.bass as bass
import concourse.mybir as mybir
from concourse.bass_utils import run_bass_kernel_spmd
from concourse.aot_env import allow_ici_all_to_all


def _cc_alltoall(e, iap, oap):
    with allow_ici_all_to_all():
        return e.collective_compute(
            "AllToAll", mybir.AluOpType.bypass, replica_groups=[list(range(8))],
            ins=[iap], outs=[oap])

F32 = mybir.dt.float32
BF16 = mybir.dt.bfloat16
AF = mybir.ActivationFunctionType
OP = mybir.AluOpType

B, L, D = 4, 4096, 1024
TT, CH = 512, 64
NTT, NCH = L // TT, TT // CH
NRND = 4
DKI = 1.0 / math.sqrt(128.0)
NIT = 31
_DBG_TILES = None
_DBG_P2 = True
_DBG_DUMP = False
_DBG_CHUNKS = None
_DBG_STAGE = 3
_DBG_PREP = 99
_DBG_BSET = (0, 1, 2, 3)


def build_nc():
    nc = bass.Bass("TRN2", target_bir_lowering=False)

    xfm = nc.dram_tensor("xfm", [B, 8, 128, L], F32, kind="ExternalInput").ap()
    xre = nc.dram_tensor("xre", [8, 128, NRND * 512], F32, kind="ExternalInput").ap()
    w5 = nc.dram_tensor("w5", [128, 40 * 128], F32, kind="ExternalInput").ap()
    wbcol = nc.dram_tensor("wbcol", [128, 8], F32, kind="ExternalInput").ap()
    convw = nc.dram_tensor("convw", [128, 12], F32, kind="ExternalInput").ap()
    convb = nc.dram_tensor("convb", [128, 3], F32, kind="ExternalInput").ap()
    bneg = nc.dram_tensor("bneg", [128, 1], F32, kind="ExternalInput").ap()
    bbv = nc.dram_tensor("bbv", [1, 1], F32, kind="ExternalInput").ap()
    lngb = nc.dram_tensor("lngb", [128, 2], F32, kind="ExternalInput").ap()
    woT = nc.dram_tensor("woT", [8, 128, 8 * 128], F32, kind="ExternalInput").ap()
    w1T = nc.dram_tensor("w1T", [32, 128, 8 * 128], BF16, kind="ExternalInput").ap()
    w2T = nc.dram_tensor("w2T", [32, 128, 8 * 128], BF16, kind="ExternalInput").ap()
    h1T = nc.dram_tensor("h1T", [32, 128, 32 * 128], BF16, kind="ExternalInput").ap()
    h2T = nc.dram_tensor("h2T", [32, 128, 32 * 128], BF16, kind="ExternalInput").ap()
    w3T = nc.dram_tensor("w3T", [8, 128, 32 * 128], BF16, kind="ExternalInput").ap()
    fscl = nc.dram_tensor("fscl", [128, 5 * 32], F32, kind="ExternalInput").ap()
    identb = nc.dram_tensor("identb", [128, 128], BF16, kind="ExternalInput").ap()
    identf = nc.dram_tensor("identf", [128, 128], F32, kind="ExternalInput").ap()
    onecf = nc.dram_tensor("onecf", [128, 1], F32, kind="ExternalInput").ap()
    onerf = nc.dram_tensor("onerf", [1, 128], F32, kind="ExternalInput").ap()
    mk32q = nc.dram_tensor("mk32q", [128, 32], F32, kind="ExternalInput").ap()
    mq0 = nc.dram_tensor("mq0", [128, 64], F32, kind="ExternalInput").ap()
    mq1 = nc.dram_tensor("mq1", [128, 64], F32, kind="ExternalInput").ap()
    ibs = [nc.dram_tensor(f"ib{e}", [8, 128, 512], BF16) for e in range(NRND)]
    obs = [nc.dram_tensor(f"ob{e}", [8, 128, 512], BF16) for e in range(NRND)]
    x1d = nc.dram_tensor("x1d", [128, 8 * 512], F32)
    r2d = nc.dram_tensor("r2d", [128, 8 * 512], F32)
    hd = nc.dram_tensor("hd", [128, 32 * 512], BF16)
    outp = nc.dram_tensor("outp", [8, 128, NRND * 512], F32, kind="ExternalOutput").ap()
    if _DBG_DUMP:
        dbgo = nc.dram_tensor("dbgo", [128, 8 * 512], BF16, kind="ExternalOutput").ap()
        dbgs = nc.dram_tensor("dbgs", [128, B * 128], F32, kind="ExternalOutput").ap()
        dbgq = nc.dram_tensor("dbgq", [128, B * 512], BF16, kind="ExternalOutput").ap()
        dbgk = nc.dram_tensor("dbgk", [128, B * 512], BF16, kind="ExternalOutput").ap()
        dbgv = nc.dram_tensor("dbgv", [128, B * 512], BF16, kind="ExternalOutput").ap()
        dbgsp = nc.dram_tensor("dbgsp", [128, B * 512], F32, kind="ExternalOutput").ap()
        dbgln = nc.dram_tensor("dbgln", [128, B * 512], F32, kind="ExternalOutput").ap()
        dbgbe = nc.dram_tensor("dbgbe", [1, B * 512], F32, kind="ExternalOutput").ap()
        dbgu = nc.dram_tensor("dbgu", [128, 512], BF16, kind="ExternalOutput").ap()
        dbgos = nc.dram_tensor("dbgos", [64, 256], F32, kind="ExternalOutput").ap()
        dbgoc = nc.dram_tensor("dbgoc", [64, 256], F32, kind="ExternalOutput").ap()
        dbggq = nc.dram_tensor("dbggq", [128, 256], BF16, kind="ExternalOutput").ap()
        dbgmk = nc.dram_tensor("dbgmk", [128, 128], F32, kind="ExternalOutput").ap()
        dbgp3 = nc.dram_tensor("dbgp3", [128, 256], F32, kind="ExternalOutput").ap()
        dbge = nc.dram_tensor("dbge", [64, 8 * 128], F32, kind="ExternalOutput").ap()

    es = contextlib.ExitStack()
    with es:
        T = lambda name, shape, dt=F32: es.enter_context(nc.sbuf_tensor(name, shape, dt))
        wq = T("wq", [128, 40 * 128 + 8], BF16)
        cdiag = T("cdiag", [128, 12 * 128], BF16)
        cbias = T("cbias", [128, 3], F32)
        bnegc = T("bnegc", [128, 1], F32)
        bbc = T("bbc", [1, 1], F32)
        lng = T("lng", [128, 2], F32)
        idb = T("idb", [128, 128], BF16)
        idf = T("idf", [128, 128], F32)
        onc = T("onc", [128, 1], F32)
        onr = T("onr", [1, 128], F32)
        mks = T("mks", [128, 32], F32)
        mg0 = T("mg0", [128, 64], F32)
        mg1 = T("mg1", [128, 64], F32)
        fsc = T("fsc", [128, 5 * 32], F32)
        xf = T("xf", [128, 8 * TT], F32)
        xb = T("xb", [128, 8 * TT], BF16)
        qT_ = T("qT_", [128, B * TT], BF16)
        kT_ = T("kT_", [128, B * TT], BF16)
        vT_ = T("vT_", [128, B * TT], BF16)
        gT_ = T("gT_", [128, B * TT], BF16)
        spT_ = T("spT_", [128, B * TT], F32)
        lnT_ = T("lnT_", [128, B * TT], F32)
        beta_ = T("beta_", [1, B * TT], F32)
        pre = T("pre", [128, TT + 3], F32)
        preb = T("preb", [128, TT + 3], BF16)
        halo = T("halo", [128, 9 * B * 2], F32)
        st_f = T("st_f", [128, B * 128], F32)
        st_b = T("st_b", [128, B * 128], BF16)
        sbf = T("sbf", [128, 128], BF16)
        kct = T("kct", [128, 256], BF16)
        gq01 = T("gq01", [128, 128], BF16)
        gbd0 = T("gbd0", [128, 128], BF16)
        gbd1 = T("gbd1", [128, 128], BF16)
        gbdx = T("gbdx", [128, 128], BF16)
        efm = T("efm", [128, 4 * 192], F32)
        rhsp = T("rhsp", [128, 256], F32)
        e1t = T("e1t", [64, B * 128], F32)
        e2t = T("e2t", [64, B * 128], F32)
        cp0 = T("cp0", [128, 256], F32)
        cp1 = T("cp1", [128, 256], F32)
        invs = T("invs", [128, 256], F32)
        bstk = T("bstk", [128, 2], F32)
        bstn = T("bstn", [128, 2], F32)
        rstk = T("rstk", [128, 512], BF16)
        delta3 = [T(f"delta{i}", [128, 256], BF16) for i in range(2)]
        u0 = T("u0", [128, 256], BF16)
        u1 = T("u1", [128, 256], BF16)
        oS = T("oS", [64, 128], F32)
        oS2 = T("oS2", [64, 128], F32)
        ocol = T("ocol", [64, 2], F32)
        mcol = T("mcol", [128, 8], F32)
        ecolP = T("ecolP", [128, B], F32)
        ecolN = T("ecolN", [128, 2 * B], F32)
        ognm = T("ognm", [128, 64], F32)
        send = T("send", [128, 8 * 512], BF16)
        recv = T("recv", [128, 8 * 512], BF16)
        x1f = T("x1f", [128, 512], F32)
        x1b = T("x1b", [128, 8 * 512], BF16)
        r2f = T("r2f", [128, 512], F32)
        sqs = T("sqs", [128, 512], F32)
        rowa = T("rowa", [1, 512], F32)
        rowb = T("rowb", [1, 512], F32)
        hs0 = T("hs0", [128, 512], BF16)
        hs1 = T("hs1", [128, 512], BF16)
        ht = T("ht", [128, 32 * 512], BF16)
        wst = T("wst", [128, 2 * 4096], BF16)
        outf = T("outf", [128, 512], F32)
        sscr_state = T("sscr_state", [128, 128], F32)
        epsc = T("epsc", [128, 1], F32)
        ocdump = T("ocdump", [64, 256], F32)
        p3dump = T("p3dump", [128, 256], F32)

        pb = [es.enter_context(nc.psum_tensor(f"pb{i}", [128, 512], F32)) for i in range(6)]
        pbar = es.enter_context(nc.psum_tensor("pbar", [128, 512], F32))
        ptb = es.enter_context(nc.psum_tensor("ptb", [128, 512], BF16))

        sems = {}
        for k in ("ld", "x", "w", "mm", "ve", "sc", "cc", "sd", "rv", "st", "gp"):
            sems[k] = es.enter_context(nc.semaphore("s_" + k))
        cnt = {k: 0 for k in sems}
        prog = {k: [] for k in ("sync", "tensor", "vector", "scalar", "gpsimd")}

        def emit(eng, fn):
            prog[eng].append(fn)

        def dma(eng, out, in_, key, serial=True):
            if serial:
                wait(eng, key, cnt[key])
            cnt[key] += 16
            emit(eng, lambda e, out=out, in_=in_, key=key: e.dma_start(out=out, in_=in_).then_inc(sems[key], 16))
            return cnt[key]

        def wait(eng, key, val=None):
            if val is None:
                val = cnt[key]
            emit(eng, lambda e, key=key, val=val: e.wait_ge(sems[key], val))

        def op(eng, fn, key=None, nodrain=False):
            if eng in ("vector", "scalar", "gpsimd") and not nodrain:
                emit(eng, lambda e: e.drain())
            if key:
                cnt[key] += 1
                emit(eng, lambda e, fn=fn, key=key: fn(e).then_inc(sems[key], 1))
                return cnt[key]
            emit(eng, lambda e, fn=fn: fn(e))
            return None

        TP = [(0, 0), (0, 32), (0, 64), (0, 96)]

        def bar_mm():
            return op("tensor", lambda e: e.matmul(pbar[:, 0:4], idb[:, :], idb[:, 0:4],
                                                   start=True, stop=True, skip_group_check=True), "mm")

        dma("sync", idb[:, :], identb[:, :], "ld")
        dma("sync", idf[:, :], identf[:, :], "ld")
        dma("sync", onc[:, :], onecf[:, :], "ld")
        dma("sync", onr[:, :], onerf[:, :], "ld")
        dma("sync", mks[:, :], mk32q[:, :], "ld")
        dma("sync", mg0[:, :], mq0[:, :], "ld")
        dma("sync", mg1[:, :], mq1[:, :], "ld")
        dma("sync", bnegc[:, :], bneg[:, :], "ld")
        dma("sync", bbc[:, :], bbv[:, :], "ld")
        dma("sync", lng[:, :], lngb[:, :], "ld")
        dma("sync", cbias[:, :], convb[:, :], "ld")
        dma("sync", fsc[:, :], fscl[:, :], "ld")
        for g in range(5):
            wait("sync", "ve")
            v = dma("sync", xf[:, 0:8 * 128], w5[:, g * 1024:(g + 1) * 1024], "x")
            wait("vector", "x", v)
            op("vector", lambda e, g=g: e.tensor_copy(wq[:, g * 1024:(g + 1) * 1024], xf[:, 0:1024]), "ve")
        wait("sync", "ve")
        v = dma("sync", xf[:, 0:8], wbcol[:, :], "x")
        wait("vector", "x", v)
        op("vector", lambda e: e.tensor_copy(wq[:, 5120:5128], xf[:, 0:8]), "ve")
        v = dma("sync", xf[:, 8:20], convw[:, :], "x")
        wait("vector", "x", v)
        for t in range(12):
            op("vector", lambda e, t=t: e.tensor_scalar(
                cdiag[:, t * 128:(t + 1) * 128], idb[:, :], xf[:, 8 + t:9 + t], 0.0, OP.mult, OP.add))
        op("vector", lambda e: e.memset(epsc[:, :], 1e-5))
        op("vector", lambda e: e.memset(send[:, :], 0.0))
        op("vector", lambda e: e.memset(recv[:, :], 0.0))
        op("vector", lambda e: e.memset(halo[:, :], 0.0))
        op("vector", lambda e: e.memset(st_f[:, :], 0.0))
        op("vector", lambda e: e.memset(st_b[:, :], 0.0))
        op("vector", lambda e: e.memset(gbd0[:, :], 0.0))
        op("vector", lambda e: e.memset(gbd1[:, :], 0.0))
        op("vector", lambda e: e.memset(gbdx[:, :], 0.0))
        op("vector", lambda e: e.memset(gq01[:, :], 0.0))
        op("vector", lambda e: e.memset(kct[:, :], 0.0))
        op("vector", lambda e: e.memset(rstk[:, :], 0.0))
        op("vector", lambda e: e.memset(invs[:, :], 1.0))
        op("vector", lambda e: e.memset(bstk[:, :], 0.0))
        op("vector", lambda e: e.memset(bstn[:, :], 0.0), "ve")
        wait("tensor", "ve")
        wait("scalar", "ve")

        q_t = lambda b: qT_[:, b * TT:(b + 1) * TT]
        k_t = lambda b: kT_[:, b * TT:(b + 1) * TT]
        v_t = lambda b: vT_[:, b * TT:(b + 1) * TT]
        g_t = lambda b: gT_[:, b * TT:(b + 1) * TT]
        sp_t = lambda b: spT_[:, b * TT:(b + 1) * TT]
        ln_t = lambda b: lnT_[:, b * TT:(b + 1) * TT]
        be_t = lambda b: beta_[:, b * TT:(b + 1) * TT]
        st_fb = lambda b: st_f[:, b * 128:(b + 1) * 128]
        st_bb = lambda b: st_b[:, b * 128:(b + 1) * 128]

        def emit_proj(tt, b):
            t0 = tt * TT
            wait("sync", "ve")
            xv = 0
            for c in range(8):
                xv = dma("sync", xf[:, c * TT:(c + 1) * TT], xfm[b, c, :, t0:t0 + TT], "x", serial=(c == 0))
            wait("vector", "x", xv)
            wait("vector", "mm")
            for c in range(8):
                vx = op("vector", lambda e, c=c: e.tensor_copy(xb[:, c * TT:(c + 1) * TT], xf[:, c * TT:(c + 1) * TT]),
                        "ve" if c == 7 else None)
            wait("tensor", "ve", vx)
            for ti in range(3):
                for c in range(8):
                    mm = op("tensor", lambda e, c=c, ti=ti: e.matmul(
                        pb[0][:, 0:TT], wq[:, (ti * 8 + c) * 128:(ti * 8 + c + 1) * 128],
                        xb[:, c * TT:(c + 1) * TT], start=(c == 0), stop=(c == 7)),
                        "mm" if c == 7 else None)
                wait("scalar", "mm", mm)
                wait("scalar", "ve")
                sc = op("scalar", lambda e: e.copy(pre[:, 3:3 + TT], pb[0][:, 0:TT]), "sc")
                wait("vector", "sc", sc)
                hi = (ti * B + b) * 3 + (tt % 2) * 36
                ho = (ti * B + b) * 3 + ((tt + 1) % 2) * 36
                op("vector", lambda e, hi=hi: e.tensor_copy(pre[:, 0:3], halo[:, hi:hi + 3]))
                op("vector", lambda e, ho=ho: e.tensor_copy(halo[:, ho:ho + 3], pre[:, TT:TT + 3]))
                cv = op("vector", lambda e: e.tensor_copy(preb[:, :], pre[:, :]), "ve")
                wait("tensor", "ve", cv)
                for j in range(4):
                    mm = op("tensor", lambda e, ti=ti, j=j: e.matmul(
                        pb[1][:, 0:TT], cdiag[:, (ti * 4 + j) * 128:(ti * 4 + j + 1) * 128],
                        preb[:, j:j + TT], start=(j == 0), stop=(j == 3)),
                        "mm" if j == 3 else None)
                wait("scalar", "mm", mm)
                dstt = (q_t(b), k_t(b), v_t(b))[ti]
                scv = op("scalar", lambda e, ti=ti, dstt=dstt: e.activation(
                    dstt, pb[1][:, 0:TT], AF.Silu, bias=cbias[:, ti:ti + 1], scale=1.0), "sc")
                if ti == 1:
                    wait("vector", "sc", scv)
                    op("vector", lambda e, dstt=dstt: e.tensor_scalar(dstt, dstt, DKI, 0.0, OP.mult, OP.add), "ve")
            for c in range(8):
                mm = op("tensor", lambda e, c=c: e.matmul(
                    pb[0][:, 0:TT], wq[:, (24 + c) * 128:(24 + c + 1) * 128],
                    xb[:, c * TT:(c + 1) * TT], start=(c == 0), stop=(c == 7)),
                    "mm" if c == 7 else None)
            wait("scalar", "mm", mm)
            op("scalar", lambda e: e.activation(xf[:, 0:TT], pb[0][:, 0:TT], AF.Sigmoid,
                                                bias=bnegc[:, 0:1], scale=1.0))
            op("scalar", lambda e, b=b: e.activation(sp_t(b), xf[:, 0:TT], AF.Ln), "sc")
            wait("tensor", "sc")
            for c in range(8):
                mm = op("tensor", lambda e, c=c: e.matmul(
                    pb[1][:, 0:TT], wq[:, (32 + c) * 128:(32 + c + 1) * 128],
                    xb[:, c * TT:(c + 1) * TT], start=(c == 0), stop=(c == 7)),
                    "mm" if c == 7 else None)
            wait("scalar", "mm", mm)
            op("scalar", lambda e, b=b: e.activation(g_t(b), pb[1][:, 0:TT], AF.Sigmoid), "sc")
            wait("tensor", "sc")
            for c in range(8):
                mm = op("tensor", lambda e, c=c: e.matmul(
                    pb[0][0:1, 0:TT], wq[:, 5120 + c:5121 + c],
                    xb[:, c * TT:(c + 1) * TT], start=(c == 0), stop=(c == 7)),
                    "mm" if c == 7 else None)
            wait("scalar", "mm", mm)
            scb = op("scalar", lambda e, b=b: e.activation(be_t(b), pb[0][0:1, 0:TT], AF.Sigmoid,
                                                           bias=bbc[0:1, 0:1], scale=1.0), "sc")
            wait("vector", "sc", scb)
            for c in range(NCH):
                op("vector", lambda e, b=b, c=c: e.tensor_tensor_scan(
                    out=ln_t(b)[:, c * CH:(c + 1) * CH], data0=sp_t(b)[:, c * CH:(c + 1) * CH],
                    data1=sp_t(b)[:, c * CH:(c + 1) * CH], initial=0.0, op0=OP.add, op1=OP.bypass),
                   "ve" if c == NCH - 1 else None)

        def emit_prep_all(tt, c):
            c0 = c * CH
            lncs = [ln_t(b)[:, c0:c0 + CH] for b in range(B)]
            wait("vector", "sc", cnt["sc"])
            for b in range(B):
                op("vector", lambda e, b=b: e.tensor_scalar(mcol[:, 2 * b:2 * b + 1], lncs[b][:, CH - 1:CH], 0.5, 0.0, OP.mult, OP.add))
                mv = op("vector", lambda e, b=b: e.tensor_scalar(mcol[:, 2 * b + 1:2 * b + 2], lncs[b][:, CH - 1:CH], -0.5, 0.0, OP.mult, OP.add),
                        "ve" if b == B - 1 else None)
            wait("scalar", "ve", mv)
            wait("scalar", "mm")
            for b in range(B):
                eo = 192 * b
                op("scalar", lambda e, b=b, eo=eo: e.activation(efm[:, eo:eo + 64], lncs[b], AF.Exp, scale=1.0))
                e2v = op("scalar", lambda e, b=b, eo=eo: e.activation(efm[:, eo + 64:eo + 128], lncs[b], AF.Exp, bias=mcol[:, 2 * b + 1:2 * b + 2], scale=1.0),
                         "sc" if b == B - 1 else None)
            wait("vector", "sc", e2v)
            for b in range(B):
                tsub = op("vector", lambda e, b=b, c0=c0: e.tensor_tensor(
                    rhsp[:, 64 * b:64 * b + 64], lncs[b], sp_t(b)[:, c0:c0 + CH], OP.subtract), "ve" if b == B - 1 else None)
            wait("scalar", "ve", tsub)
            for b in range(B):
                eo = 192 * b
                e3 = op("scalar", lambda e, b=b, eo=eo: e.activation(efm[:, eo + 128:eo + 192], rhsp[:, 64 * b:64 * b + 64], AF.Exp, bias=mcol[:, 2 * b:2 * b + 1], scale=-1.0),
                        "sc" if b == B - 1 else None)
            wait("vector", "sc", e3)
            for b in range(B):
                eo, ro = 192 * b, 64 * b
                op("vector", lambda e, b=b, c0=c0, eo=eo, ro=ro: e.tensor_tensor(rhsp[:, ro:ro + 64], efm[:, eo + 128:eo + 192], v_t(b)[:, c0:c0 + CH], OP.mult))
                op("vector", lambda e, ro=ro: e.tensor_scalar(rhsp[:, ro:ro + 64], rhsp[:, ro:ro + 64], -1.0, 0.0, OP.mult, OP.add))
                op("vector", lambda e, b=b, eo=eo: e.tensor_copy(ecolP[:, b:b + 1], efm[:, eo + 63:eo + 64]))
                op("vector", lambda e, b=b, eo=eo: e.tensor_scalar(ecolN[:, 2 * b:2 * b + 1], efm[:, eo + 63:eo + 64], -1.0, 0.0, OP.mult, OP.add))
                rr = op("vector", lambda e, b=b, eo=eo: e.tensor_scalar(ecolN[:, 2 * b + 1:2 * b + 2], efm[:, eo + 127:eo + 128], -1.0, 0.0, OP.mult, OP.add),
                        "ve" if b == B - 1 else None)
            wait("tensor", "ve", rr)
            for b in range(B):
                tp = TP[b]
                r0, r1 = 32 * b, 32 * b + 32
                Kc = k_t(b)[:, c0:c0 + CH]
                Qc = q_t(b)[:, c0:c0 + CH]
                eo, ro = 192 * b, 64 * b
                pbt = pb[b % 2]
                op("tensor", lambda e, pbt=pbt, r0=r0, r1=r1, b=b, c0=c0, tp=tp: e.matmul(
                    pbt[r0:r1, 0:128], sp_t(b)[:, c0:c0 + 32], idf[:, :], start=True, stop=True, tile_position=tp, skip_group_check=True))
                op("tensor", lambda e, pbt=pbt, r0=r0, r1=r1, b=b, c0=c0, tp=tp: e.matmul(
                    pbt[r0:r1, 128:256], sp_t(b)[:, c0 + 32:c0 + 64], idf[:, :], start=True, stop=True, tile_position=tp, skip_group_check=True))
                op("tensor", lambda e, pbt=pbt, r0=r0, r1=r1, ro=ro, tp=tp: e.matmul(
                    pbt[r0:r1, 256:384], rhsp[:, ro:ro + 32], idf[:, :], start=True, stop=True, tile_position=tp, skip_group_check=True))
                op("tensor", lambda e, pbt=pbt, r0=r0, r1=r1, ro=ro, tp=tp: e.matmul(
                    pbt[r0:r1, 384:512], rhsp[:, ro + 32:ro + 64], idf[:, :], start=True, stop=True, tile_position=tp, skip_group_check=True))
                op("tensor", lambda e, b=b, c0=c0, tp=tp: e.matmul(
                    pb[4][32 * b:32 * b + 32, 0:1], be_t(b)[0:1, c0:c0 + 32], onc[0:1, 0:1], start=True, stop=True, tile_position=tp, skip_group_check=True))
                op("tensor", lambda e, b=b, c0=c0, tp=tp: e.matmul(
                    pb[4][32 * b:32 * b + 32, 1:2], be_t(b)[0:1, c0 + 32:c0 + 64], onc[0:1, 0:1], start=True, stop=True, tile_position=tp, skip_group_check=True))
                op("tensor", lambda e, b=b, eo=eo: e.matmul(pb[2][0:64, 128 * b:128 * b + 128], efm[:, eo:eo + 64], idf[:, :], start=True, stop=True, skip_group_check=True))
                op("tensor", lambda e, b=b, eo=eo: e.matmul(pb[5][0:64, 128 * b:128 * b + 128], efm[:, eo + 64:eo + 128], idf[:, :], start=True, stop=True, skip_group_check=True))
                op("tensor", lambda e, Kc=Kc, b=b: e.matmul(
                    pb[3][32 * b:32 * b + 32, 0:64], Kc[:, 0:32], Kc[:, :], start=True, stop=True, tile_position=(0, 32 * b), skip_group_check=True))
                op("tensor", lambda e, Kc=Kc, b=b: e.matmul(
                    pb[3][32 * b:32 * b + 32, 64:96], Kc[:, 32:64], Kc[:, 32:64], start=True, stop=True, tile_position=(0, 32 * b), skip_group_check=True))
                op("tensor", lambda e, Kc=Kc, Qc=Qc, b=b: e.matmul(
                    pb[3][32 * b:32 * b + 32, 128:192], Kc[:, 0:32], Qc[:, :], start=True, stop=True, tile_position=(0, 32 * b), skip_group_check=True))
                op("tensor", lambda e, Kc=Kc, Qc=Qc, b=b: e.matmul(
                    pb[3][32 * b:32 * b + 32, 192:256], Kc[:, 32:64], Qc[:, :], start=True, stop=True, tile_position=(0, 32 * b), skip_group_check=True))
                op("tensor", lambda e, Kc=Kc, b=b, tp=tp: e.matmul(
                    pb[4][32 * b:32 * b + 32, 128:256], Kc[:, 0:32], idb[:, :], start=True, stop=True, tile_position=tp, skip_group_check=True))
                op("tensor", lambda e, Kc=Kc, b=b, tp=tp: e.matmul(
                    pb[4][32 * b:32 * b + 32, 256:384], Kc[:, 32:64], idb[:, :], start=True, stop=True, tile_position=tp, skip_group_check=True))
            tv = bar_mm()
            wait("scalar", "mm", tv)
            for b in range(B):
                r0, r1 = 32 * b, 32 * b + 32
                pbt = pb[b % 2]
                op("scalar", lambda e, pbt=pbt, r0=r0, r1=r1: e.activation(invs[r0:r1, 0:128], pbt[r0:r1, 0:128], AF.Exp, scale=-1.0))
                op("scalar", lambda e, pbt=pbt, r0=r0, r1=r1: e.activation(invs[r0:r1, 128:256], pbt[r0:r1, 128:256], AF.Exp, scale=-1.0))
                op("scalar", lambda e, b=b: e.copy(kct[32 * b:32 * b + 32, :], pb[4][32 * b:32 * b + 32, 128:384]))
                op("scalar", lambda e, r0=r0, r1=r1: e.copy(bstk[r0:r1, 0:2], pb[4][r0:r1, 0:2]))
                op("scalar", lambda e, r0=r0, r1=r1: e.mul(bstn[r0:r1, 0:2], pb[4][r0:r1, 0:2], -1.0))
                op("scalar", lambda e, pbt=pbt, r0=r0, r1=r1: e.copy(rstk[r0:r1, 128:256], pbt[r0:r1, 256:384]))
                sv = op("scalar", lambda e, pbt=pbt, r0=r0, r1=r1: e.copy(rstk[r0:r1, 384:512], pbt[r0:r1, 384:512]),
                        "sc" if b == B - 1 else None)
            wait("vector", "mm", tv)
            for b in range(B):
                r0, r1 = 32 * b, 32 * b + 32
                op("vector", lambda e, b=b: e.tensor_copy(e1t[:, b * 128:(b + 1) * 128], pb[2][0:64, 128 * b:128 * b + 128]))
                op("vector", lambda e, b=b: e.tensor_copy(e2t[:, b * 128:(b + 1) * 128], pb[5][0:64, 128 * b:128 * b + 128]))
                op("vector", lambda e, r0=r0, r1=r1: e.tensor_tensor(gbd0[r0:r1, r0:r1], pb[3][r0:r1, 0:32], mks[r0:r1, :], OP.mult))
                op("vector", lambda e, r0=r0, r1=r1: e.tensor_scalar(gbdx[r0:r1, r0:r1], pb[3][r0:r1, 32:64], -1.0, 0.0, OP.mult, OP.add))
                op("vector", lambda e, r0=r0, r1=r1: e.tensor_tensor(gbd1[r0:r1, r0:r1], pb[3][r0:r1, 64:96], mks[r0:r1, :], OP.mult))
                op("vector", lambda e, r0=r0, r1=r1: e.tensor_tensor(gq01[r0:r1, 0:64], pb[3][r0:r1, 128:192], mg0[r0:r1, :], OP.mult))
                vv = op("vector", lambda e, r0=r0, r1=r1: e.tensor_tensor(gq01[r0:r1, 64:128], pb[3][r0:r1, 192:256], mg1[r0:r1, :], OP.mult),
                        "ve" if b == B - 1 else None)
            wait("tensor", "ve", vv)
            wait("tensor", "sc", sv)

        def emit_chain(tt, c):
            c0 = c * CH
            wait("vector", "sc")
            op("vector", lambda e: e.tensor_scalar(cp0[:, 0:128], invs[:, 0:128], bstk[:, 0:1], 0.0, OP.mult, OP.add))
            op("vector", lambda e: e.tensor_copy(cp0[:, 128:256], cp0[:, 0:128]))
            op("vector", lambda e: e.tensor_scalar(cp1[:, 0:128], invs[:, 128:256], bstk[:, 1:2], 0.0, OP.mult, OP.add))
            cpv = op("vector", lambda e: e.tensor_copy(cp1[:, 128:256], cp1[:, 0:128]), "ve")
            wait("tensor", "ve", cpv)
            op("tensor", lambda e: e.matmul(pb[2][:, 0:256], idb[:, :], rstk[:, 0:256], start=True, stop=True))
            for b in range(B):
                op("tensor", lambda e, b=b, c0=c0: e.matmul(
                    pb[2][32 * b:32 * b + 32, 0:128], k_t(b)[:, c0:c0 + 32], st_bb(b),
                    start=False, stop=False, tile_position=(0, 32 * b),
                    skip_group_check=True))
            mmv = bar_mm()
            wait("vector", "mm", mmv)
            dv = op("vector", lambda e: e.tensor_tensor(delta3[0][:, :], pb[2][:, 0:256], cp0[:, :], OP.mult), "ve")
            wait("tensor", "ve", dv)
            for j in range(NIT):
                d = delta3[j % 2]
                mmv = op("tensor", lambda e, d=d: e.matmul(pb[3][:, 0:256], gbd0[:, :], d[:, :], start=True, stop=True), "mm")
                op("tensor", lambda e, d=d, j=j: e.matmul(pb[4][:, 0:256], idb[:, :], d[:, :], start=(j == 0), stop=False))
                wait("vector", "mm", mmv)
                dn = delta3[(j + 1) % 2]
                dv = op("vector", lambda e, dn=dn: e.tensor_tensor(dn[:, :], pb[3][:, 0:256], cp0[:, :], OP.mult), "ve", nodrain=True)
                wait("tensor", "ve", dv)
            mmv = op("tensor", lambda e: e.matmul(pb[4][:, 0:256], idb[:, :], delta3[NIT % 2][:, :], start=False, stop=True), "mm")
            wait("scalar", "mm", mmv)
            uv = op("scalar", lambda e: e.copy(u0[:, :], pb[4][:, 0:256]), "sc")
            wait("tensor", "sc", uv)
            op("tensor", lambda e: e.matmul(pb[5][:, 0:256], idb[:, :], rstk[:, 256:512], start=True, stop=True))
            for b in range(B):
                op("tensor", lambda e, b=b, c0=c0: e.matmul(
                    pb[5][32 * b:32 * b + 32, 0:128], k_t(b)[:, c0 + 32:c0 + 64], st_bb(b),
                    start=False, stop=False, tile_position=(0, 32 * b),
                    skip_group_check=True))
            mmv = op("tensor", lambda e: e.matmul(pb[5][:, 0:256], gbdx[:, :], u0[:, :], start=False, stop=True,
                                                  skip_group_check=True), "mm")
            wait("vector", "mm", mmv)
            dv = op("vector", lambda e: e.tensor_tensor(delta3[0][:, :], pb[5][:, 0:256], cp1[:, :], OP.mult), "ve")
            wait("tensor", "ve", dv)
            for j in range(NIT):
                d = delta3[j % 2]
                mmv = op("tensor", lambda e, d=d: e.matmul(pb[3][:, 0:256], gbd1[:, :], d[:, :], start=True, stop=True), "mm")
                op("tensor", lambda e, d=d, j=j: e.matmul(pb[4][:, 0:256], idb[:, :], d[:, :], start=(j == 0), stop=False))
                wait("vector", "mm", mmv)
                dn = delta3[(j + 1) % 2]
                dv = op("vector", lambda e, dn=dn: e.tensor_tensor(dn[:, :], pb[3][:, 0:256], cp1[:, :], OP.mult), "ve", nodrain=True)
                wait("tensor", "ve", dv)
            mmv = op("tensor", lambda e: e.matmul(pb[4][:, 0:256], idb[:, :], delta3[NIT % 2][:, :], start=False, stop=True), "mm")
            wait("scalar", "mm", mmv)
            uv = op("scalar", lambda e: e.copy(u1[:, :], pb[4][:, 0:256]), "sc")
            wait("tensor", "sc", uv)
            wait("vector", "sc", uv)

        def emit_post(tt, b, c):
            tp = TP[b]
            r0, r1 = 32 * b, 32 * b + 32
            c0 = c * CH
            slot = 2 * b + (tt % 2)
            op("tensor", lambda e, b=b, c0=c0: e.matmul(
                pb[2][0:64, 0:128], q_t(b)[:, c0:c0 + CH], st_bb(b), start=True, stop=True, skip_group_check=True))
            op("tensor", lambda e, r0=r0, r1=r1: e.matmul(
                pb[2][0:64, 0:128], gq01[r0:r1, 0:64], u0[r0:r1, 0:128], start=False, stop=False, tile_position=(r0, 0), skip_group_check=True))
            op("tensor", lambda e, r0=r0, r1=r1: e.matmul(
                pb[2][0:64, 0:128], gq01[r0:r1, 64:128], u1[r0:r1, 0:128], start=False, stop=False, tile_position=(r0, 0), skip_group_check=True))
            op("tensor", lambda e, r0=r0, r1=r1: e.matmul(
                pb[2][0:64, 128:256], gq01[r0:r1, 0:64], u0[r0:r1, 128:256], start=True, stop=True, tile_position=(r0, 0), skip_group_check=True))
            op("tensor", lambda e, r0=r0, r1=r1: e.matmul(
                pb[2][0:64, 128:256], gq01[r0:r1, 64:128], u1[r0:r1, 128:256], start=False, stop=False, tile_position=(r0, 0), skip_group_check=True))
            om = bar_mm()
            op("tensor", lambda e, r0=r0, r1=r1: e.matmul(
                pb[1][:, 0:128], u0[r0:r1, 0:128], kct[r0:r1, 0:128], start=True, stop=True, tile_position=(r0, 0), skip_group_check=True))
            op("tensor", lambda e, r0=r0, r1=r1: e.matmul(
                pb[1][:, 0:128], u1[r0:r1, 0:128], kct[r0:r1, 128:256], start=False, stop=False, tile_position=(r0, 0), skip_group_check=True))
            op("tensor", lambda e, r0=r0, r1=r1: e.matmul(
                pb[1][:, 128:256], u0[r0:r1, 128:256], kct[r0:r1, 0:128], start=True, stop=True, tile_position=(r0, 0), skip_group_check=True))
            op("tensor", lambda e, r0=r0, r1=r1: e.matmul(
                pb[1][:, 128:256], u1[r0:r1, 128:256], kct[r0:r1, 128:256], start=False, stop=False, tile_position=(r0, 0), skip_group_check=True))
            sm = bar_mm()
            wait("vector", "mm", om)
            if _DBG_DUMP and b == 3:
                op("vector", lambda e: e.tensor_copy(ocdump[0:64, 0:128], pb[2][0:64, 0:128]))
                op("vector", lambda e: e.tensor_copy(ocdump[0:64, 128:256], pb[2][0:64, 128:256]))
            op("vector", lambda e, b=b: e.tensor_tensor(oS[:, :], pb[2][0:64, 0:128], e1t[:, b * 128:(b + 1) * 128], OP.mult))
            op("vector", lambda e, b=b: e.tensor_tensor(oS2[:, :], pb[2][0:64, 128:256], e2t[:, b * 128:(b + 1) * 128], OP.mult))
            op("vector", lambda e: e.tensor_tensor(oS[:, :], oS[:, :], oS2[:, :], OP.add))
            op("vector", lambda e: e.tensor_reduce(ocol[:, 0:1], oS[:, :], mybir.AxisListType.X, OP.add))
            op("vector", lambda e: e.tensor_scalar(ocol[:, 0:1], ocol[:, 0:1], 1.0 / 128.0, 0.0, OP.mult, OP.add))
            ov = op("vector", lambda e: e.tensor_scalar(oS[:, :], oS[:, :], ocol[:, 0:1], 0.0, OP.subtract, OP.add), "ve")
            wait("scalar", "ve", ov)
            sq = op("scalar", lambda e: e.activation(oS2[:, :], oS[:, :], AF.Square), "sc")
            wait("vector", "sc", sq)
            rv = op("vector", lambda e: e.tensor_reduce(ocol[:, 1:2], oS2[:, :], mybir.AxisListType.X, OP.add), "ve")
            wait("scalar", "ve", rv)
            sq = op("scalar", lambda e: e.activation(ocol[:, 1:2], ocol[:, 1:2], AF.Sqrt, bias=epsc[0:64, 0:1], scale=1.0 / 128.0), "sc")
            wait("vector", "sc", sq)
            op("vector", lambda e: e.reciprocal(ocol[:, 1:2], ocol[:, 1:2]))
            onv = op("vector", lambda e: e.tensor_scalar(oS[:, :], oS[:, :], ocol[:, 1:2], 0.0, OP.mult, OP.add), "ve")
            wait("tensor", "ve", onv)
            ot = op("tensor", lambda e: e.matmul(pb[2][0:128, 256:320], oS[:, :], idf[0:64, 0:64], start=True, stop=True, skip_group_check=True), "mm")
            wait("vector", "mm", ot)
            og = op("vector", lambda e: e.tensor_scalar(ognm[:, :], pb[2][0:128, 256:320],
                                                        lng[:, 0:1], lng[:, 1:2], OP.mult, OP.add), "ve")
            gv = op("vector", lambda e, b=b, c0=c0, slot=slot, c=c: e.tensor_tensor(
                send[:, slot * 512 + c * 64:slot * 512 + (c + 1) * 64], ognm[:, :],
                g_t(b)[:, c0:c0 + CH], OP.mult), "ve")
            wait("vector", "mm", sm)
            op("vector", lambda e, b=b: e.tensor_scalar(sscr_state[:, :], st_fb(b), ecolP[:, b:b + 1], 0.0, OP.mult, OP.add))
            op("vector", lambda e, b=b: e.scalar_tensor_tensor(
                out=sscr_state[:, :], in0=pb[1][:, 0:128], scalar=ecolN[:, 2 * b:2 * b + 1],
                in1=sscr_state[:, :], op0=OP.mult, op1=OP.add))
            sv = op("vector", lambda e, b=b: e.scalar_tensor_tensor(
                out=st_fb(b), in0=pb[1][:, 128:256], scalar=ecolN[:, 2 * b + 1:2 * b + 2],
                in1=sscr_state[:, :], op0=OP.mult, op1=OP.add), "ve")
            cb = op("vector", lambda e, b=b: e.tensor_copy(sbf[:, :], st_fb(b)), "ve")
            wait("tensor", "ve", cb)
            st_tr = op("tensor", lambda e: e.matmul(pb[1][:, 256:384], sbf[:, :], idb[:, :], start=True, stop=True, skip_group_check=True), "mm")
            wait("scalar", "mm", st_tr)
            sv2 = op("scalar", lambda e, b=b: e.copy(st_bb(b), pb[1][:, 256:384]), "sc")
            wait("tensor", "sc", sv2)
            wait("vector", "sc", sv2)


        def emit_exchange(e_):
            wait("gpsimd", "ve")
            for s in range(8):
                dma("gpsimd", ibs[e_].ap()[s, :, :], send[:, s * 512:(s + 1) * 512], "sd", serial=(s == 0))
            wait("gpsimd", "sd")
            op("gpsimd", lambda e, e_=e_: _cc_alltoall(e, ibs[e_].ap().opt(), obs[e_].ap().opt()), "cc")
            wait("sync", "cc", cnt["cc"])
            for r in range(8):
                dma("sync", recv[:, r * 512:(r + 1) * 512], obs[e_].ap()[r, :, :], "rv", serial=(r == 0))

        def emit_p2(e_):
            wait("tensor", "rv", cnt["rv"])
            wait("sync", "ve")
            wait("vector", "w")
            wait("vector", "x")
            wait("vector", "mm")
            xrv = 0
            for oc in range(8):
                wait("sync", "ve")
                wait("sync", "mm")
                wait("sync", "gp")
                wv = dma("sync", xf[:, 0:1024], woT[oc, :, :], "x")
                wait("vector", "x", wv)
                cv = op("vector", lambda e: e.tensor_copy(wst[:, 0:1024], xf[:, 0:1024]), "ve")
                wait("tensor", "ve", cv)
                for r in range(8):
                    mm = op("tensor", lambda e, r=r, oc=oc: e.matmul(
                        pb[5][:, 0:512], wst[:, r * 128:(r + 1) * 128],
                        recv[:, r * 512:(r + 1) * 512], start=(r == 0), stop=(r == 7)),
                        "mm" if r == 7 else None)
                xrv = dma("sync", x1f[:, :], xre[oc, :, e_ * 512:(e_ + 1) * 512], "x")
                wait("vector", "mm", mm)
                wait("vector", "x", xrv)
                rv1 = op("vector", lambda e: e.tensor_tensor(x1f[:, :], pb[5][:, 0:512], x1f[:, :], OP.add), "ve")
                wait("scalar", "ve", rv1)
                sv = op("scalar", lambda e: e.activation(sqs[:, :], x1f[:, :], AF.Square), "sc")
                wait("tensor", "sc", sv)
                op("tensor", lambda e, oc=oc: e.matmul(pb[4][0:1, 0:512], onc[:, :], x1f[:, :],
                                                       start=(oc == 0), stop=(oc == 7)))
                mm = op("tensor", lambda e, oc=oc: e.matmul(pb[3][0:1, 0:512], onc[:, :], sqs[:, :],
                                                            start=(oc == 0), stop=(oc == 7)), "mm")
                wait("gpsimd", "mm", mm)
                dv = dma("gpsimd", x1d.ap()[:, oc * 512:(oc + 1) * 512], x1f[:, :], "gp")
            wait("vector", "mm", mm)
            op("vector", lambda e: e.tensor_copy(rowa[:, :], pb[4][0:1, 0:512]))
            rv2 = op("vector", lambda e: e.tensor_copy(rowb[:, :], pb[3][0:1, 0:512]), "ve")
            emit_ln_rows(rv2)
            wait("tensor", "ve")
            bc = op("tensor", lambda e: e.matmul(pb[0][:, 0:512], onr[:, :], rowa[:, :], start=True, stop=True))
            bc = op("tensor", lambda e: e.matmul(pb[1][:, 0:512], onr[:, :], rowb[:, :], start=True, stop=True), "mm")
            wait("vector", "mm", bc)
            wait("scalar", "gp", cnt["gp"])
            for oc in range(8):
                wait("scalar", "ld")
                xv = dma("scalar", x1f[:, :], x1d.ap()[:, oc * 512:(oc + 1) * 512], "x")
                wait("vector", "x", xv)
                op("vector", lambda e: e.tensor_tensor(x1f[:, :], x1f[:, :], pb[0][:, 0:512], OP.subtract))
                op("vector", lambda e: e.tensor_tensor(x1f[:, :], x1f[:, :], pb[1][:, 0:512], OP.mult))
                cv = op("vector", lambda e, oc=oc: e.tensor_copy(x1b[:, oc * 512:(oc + 1) * 512], x1f[:, :]), "ve")
                wait("scalar", "ve", cv)
                dma("scalar", x1d.ap()[:, oc * 512:(oc + 1) * 512], x1f[:, :], "ld")
            wait("tensor", "ve", cnt["ve"])
            slot_mm = [cnt["mm"], cnt["mm"]]
            sc_done = [cnt["sc"], cnt["sc"]]
            ve_done = [cnt["ve"], cnt["ve"]]
            bkA = [pb[0], pb[2]]
            bkB = [pb[1], pb[5]]
            for oc in range(32):
                par = oc % 2
                sl = par * 4096
                wk = "w" if par == 0 else "x"
                pa, pbb = bkA[par], bkB[par]
                hb = hs0 if par == 0 else hs1
                wait("sync", "mm", slot_mm[par])
                wv = dma("sync", wst[:, sl:sl + 1024], w1T[oc, :, :], wk)
                wv = dma("sync", wst[:, sl + 1024:sl + 2048], w2T[oc, :, :], wk, serial=False)
                wait("tensor", wk, wv)
                wait("tensor", "ve", ve_done[par])
                wait("tensor", "sc", sc_done[par])
                for r in range(8):
                    op("tensor", lambda e, r=r, sl=sl, pa=pa: e.matmul(
                        pa[:, 0:512], wst[:, sl + r * 128:sl + (r + 1) * 128],
                        x1b[:, r * 512:(r + 1) * 512], start=(r == 0), stop=(r == 7)))
                for r in range(8):
                    mm = op("tensor", lambda e, r=r, sl=sl, pbb=pbb: e.matmul(
                        pbb[:, 0:512], wst[:, sl + 1024 + r * 128:sl + 1024 + (r + 1) * 128],
                        x1b[:, r * 512:(r + 1) * 512], start=(r == 0), stop=(r == 7)),
                        "mm" if r == 7 else None)
                slot_mm[par] = mm
                wait("scalar", "mm", mm)
                hv = op("scalar", lambda e, oc=oc, pa=pa, hb=hb: e.activation(hb[:, :], pa[:, 0:512], AF.Silu,
                                                                              scale=fsc[:, oc:oc + 1]), "sc")
                sc_done[par] = hv
                wait("vector", "sc", hv)
                ve_done[par] = op("vector", lambda e, oc=oc, pbb=pbb, hb=hb: e.scalar_tensor_tensor(
                    out=ht[:, oc * 512:(oc + 1) * 512], in0=pbb[:, 0:512],
                    scalar=fsc[:, 32 + oc:33 + oc], in1=hb[:, :], op0=OP.mult, op1=OP.mult), "ve")
            for stage, (wmat, scoff) in enumerate(((h1T, 64), (h2T, 96))):
                slot_mm = [cnt["mm"], cnt["mm"]]
                sc_done = [cnt["sc"], cnt["sc"]]
                gp_done = [cnt["gp"], cnt["gp"]]
                wait("tensor", "ve", cnt["ve"])
                for oc in range(32):
                    par = oc % 2
                    sl = par * 4096
                    wk = "w" if par == 0 else "x"
                    pa = pb[par]
                    hb = hs0 if par == 0 else hs1
                    wait("sync", "mm", slot_mm[par])
                    wv = None
                    for piece in range(2):
                        wv = dma("sync", wst[:, sl + piece * 2048:sl + (piece + 1) * 2048],
                                 wmat[oc, :, piece * 2048:(piece + 1) * 2048], wk, serial=(piece == 0))
                    wait("tensor", wk, wv)
                    wait("tensor", "sc", sc_done[par])
                    for r in range(32):
                        mm = op("tensor", lambda e, r=r, sl=sl, pa=pa: e.matmul(
                            pa[:, 0:512], wst[:, sl + r * 128:sl + (r + 1) * 128],
                            ht[:, r * 512:(r + 1) * 512], start=(r == 0), stop=(r == 31)),
                            "mm" if r == 31 else None)
                    slot_mm[par] = mm
                    wait("scalar", "mm", mm)
                    wait("scalar", "gp", gp_done[par])
                    hv = op("scalar", lambda e, oc=oc, scoff=scoff, hb=hb, pa=pa: e.activation(
                        hb[:, :], pa[:, 0:512], AF.Gelu, scale=fsc[:, scoff + oc:scoff + oc + 1]), "sc")
                    sc_done[par] = hv
                    wait("gpsimd", "sc", hv)
                    gp_done[par] = dma("gpsimd", hd.ap()[:, oc * 512:(oc + 1) * 512], hb[:, :], "gp")
                wait("scalar", "gp", cnt["gp"])
                wait("scalar", "mm")
                hv = 0
                for piece in range(8):
                    hv = dma("scalar", ht[:, piece * 2048:(piece + 1) * 2048],
                             hd.ap()[:, piece * 2048:(piece + 1) * 2048], "x", serial=(piece == 0))
                wait("tensor", "x", hv)
            slot_mm = [cnt["mm"], cnt["mm"]]
            ve_done = [cnt["ve"], cnt["ve"]]
            gp_last = cnt["gp"]
            for oc in range(8):
                par = oc % 2
                sl = par * 4096
                wk = "w" if par == 0 else "x"
                pa = pb[par]
                wait("sync", "mm", slot_mm[par])
                wv = None
                for piece in range(2):
                    wv = dma("sync", wst[:, sl + piece * 2048:sl + (piece + 1) * 2048],
                             w3T[oc, :, piece * 2048:(piece + 1) * 2048], wk, serial=(piece == 0))
                wait("tensor", wk, wv)
                if oc < 2:
                    wait("tensor", "sc")
                wait("tensor", "ve", ve_done[par])
                for r in range(32):
                    mm = op("tensor", lambda e, r=r, sl=sl, pa=pa: e.matmul(
                        pa[:, 0:512], wst[:, sl + r * 128:sl + (r + 1) * 128],
                        ht[:, r * 512:(r + 1) * 512], start=(r == 0), stop=(r == 31)),
                        "mm" if r == 31 else None)
                slot_mm[par] = mm
                wait("scalar", "ld")
                xv = dma("scalar", x1f[:, :], x1d.ap()[:, oc * 512:(oc + 1) * 512], "x")
                wait("vector", "mm", mm)
                wait("vector", "x", xv)
                wait("vector", "gp", gp_last)
                rv1 = op("vector", lambda e, oc=oc, pa=pa: e.scalar_tensor_tensor(
                    out=r2f[:, :], in0=pa[:, 0:512], scalar=fsc[:, 128 + oc:129 + oc],
                    in1=x1f[:, :], op0=OP.mult, op1=OP.add), "ve")
                ve_done[par] = rv1
                wait("scalar", "ve", rv1)
                sv = op("scalar", lambda e: e.activation(sqs[:, :], r2f[:, :], AF.Square), "sc")
                wait("tensor", "sc", sv)
                op("tensor", lambda e, oc=oc: e.matmul(pb[4][0:1, 0:512], onc[:, :], r2f[:, :],
                                                       start=(oc == 0), stop=(oc == 7)))
                mm = op("tensor", lambda e, oc=oc: e.matmul(pb[3][0:1, 0:512], onc[:, :], sqs[:, :],
                                                            start=(oc == 0), stop=(oc == 7)), "mm")
                wait("gpsimd", "mm", mm)
                gp_last = dma("gpsimd", r2d.ap()[:, oc * 512:(oc + 1) * 512], r2f[:, :], "gp")
            wait("vector", "mm", mm)
            op("vector", lambda e: e.tensor_copy(rowa[:, :], pb[4][0:1, 0:512]))
            rv2 = op("vector", lambda e: e.tensor_copy(rowb[:, :], pb[3][0:1, 0:512]), "ve")
            emit_ln_rows(rv2)
            wait("tensor", "ve")
            bc = op("tensor", lambda e: e.matmul(pb[0][:, 0:512], onr[:, :], rowa[:, :], start=True, stop=True))
            bc = op("tensor", lambda e: e.matmul(pb[1][:, 0:512], onr[:, :], rowb[:, :], start=True, stop=True), "mm")
            wait("vector", "mm", bc)
            wait("vector", "gp", cnt["gp"])
            wait("scalar", "gp")
            for oc in range(8):
                wait("scalar", "ve")
                xv = dma("scalar", r2f[:, :], r2d.ap()[:, oc * 512:(oc + 1) * 512], "x")
                wait("vector", "x", xv)
                wait("vector", "st")
                op("vector", lambda e: e.tensor_tensor(outf[:, :], r2f[:, :], pb[0][:, 0:512], OP.subtract))
                ov = op("vector", lambda e: e.tensor_tensor(outf[:, :], outf[:, :], pb[1][:, 0:512], OP.mult), "ve")
                wait("sync", "ve", ov)
                dma("sync", outp[oc, :, e_ * 512:(e_ + 1) * 512], outf[:, :], "st")

        def emit_ln_rows(rv2):
            wait("scalar", "ve", rv2)
            op("scalar", lambda e: e.mul(rowa[:, :], rowa[:, :], 1.0 / 1024.0))
            op("scalar", lambda e: e.mul(rowb[:, :], rowb[:, :], 1.0 / 1024.0))
            sv = op("scalar", lambda e: e.activation(sqs[0:1, 0:512], rowa[:, :], AF.Square), "sc")
            wait("vector", "sc", sv)
            vv = op("vector", lambda e: e.tensor_tensor(rowb[:, :], rowb[:, :], sqs[0:1, 0:512], OP.subtract), "ve")
            wait("scalar", "ve", vv)
            sv = op("scalar", lambda e: e.activation(rowb[:, :], rowb[:, :], AF.Sqrt, bias=epsc[0:1, 0:1]), "sc")
            wait("vector", "sc", sv)
            op("vector", lambda e: e.reciprocal(rowb[:, :], rowb[:, :]), "ve")

        ntt = _DBG_TILES if _DBG_TILES is not None else NTT
        for tt in range(ntt):
            if tt >= 2:
                wait("vector", "sd", cnt["sd"])
            for b in range(B):
                emit_proj(tt, b)
            nch = _DBG_CHUNKS if _DBG_CHUNKS is not None else NCH
            for c in range(nch):
                if _DBG_STAGE >= 1:
                    emit_prep_all(tt, c)
                if _DBG_STAGE >= 2:
                    emit_chain(tt, c)
                if _DBG_STAGE >= 3:
                    for b in range(B):
                        emit_post(tt, b, c)
            if tt % 2 == 1 and _DBG_P2:
                emit_exchange(tt // 2)
                emit_p2(tt // 2)

        if _DBG_DUMP:
            wait("vector", "mm")
            op("vector", lambda e: e.tensor_copy(p3dump[:, :], pb[3][:, 0:256]), "ve")
            wait("sync", "ve")
            wait("sync", "sc")
            dma("sync", dbgo[:, :], send[:, :], "st")
            dma("sync", dbgs[:, :], st_f[:, :], "st")
            dma("sync", dbgq[:, :], qT_[:, :], "st")
            dma("sync", dbgk[:, :], kT_[:, :], "st")
            dma("sync", dbgv[:, :], vT_[:, :], "st")
            dma("sync", dbgsp[:, :], spT_[:, :], "st")
            dma("sync", dbgln[:, :], lnT_[:, :], "st")
            dma("sync", dbgbe[:, :], beta_[:, :], "st")
            dma("sync", dbgu[:, 0:256], u0[:, :], "st")
            dma("sync", dbgu[:, 256:512], u1[:, :], "st")
            dma("sync", dbgos[:, 0:128], oS[:, :], "st")
            dma("sync", dbgos[:, 128:256], oS2[:, :], "st")
            dma("sync", dbgoc[:, :], ocdump[:, :], "st")
            dma("sync", dbggq[:, 0:128], gq01[:, :], "st")
            dma("sync", dbggq[:, 128:256], gbd0[:, :], "st")
            dma("sync", dbgp3[:, :], p3dump[:, :], "st")
            dma("sync", dbgmk[:, 0:32], mks[:, :], "st")
            dma("sync", dbgmk[:, 32:96], mg0[:, :], "st")
            dma("sync", dbge[:, 0:512], e1t[:, :], "st")
            dma("sync", dbge[:, 512:1024], e2t[:, :], "st")
        wait("sync", "st", cnt["st"])

        blk = es.enter_context(nc.Block())

        @blk.sync
        def _(e):
            for f in prog["sync"]:
                f(e)

        @blk.tensor
        def _(e):
            for f in prog["tensor"]:
                f(e)

        @blk.vector
        def _(e):
            for f in prog["vector"]:
                f(e)

        @blk.scalar
        def _(e):
            for f in prog["scalar"]:
                f(e)

        @blk.gpsimd
        def _(e):
            for f in prog["gpsimd"]:
                f(e)

    return nc



_NC_CACHE = {}


def _consts():
    identb = np.eye(128, dtype=ml_dtypes.bfloat16)
    identf = np.eye(128, dtype=np.float32)
    onecf = np.ones((128, 1), np.float32)
    onerf = np.ones((1, 128), np.float32)
    r = np.arange(32)
    i32 = np.arange(32)
    m = -(r[:, None] < i32[None, :]).astype(np.float32)
    mk32q = np.tile(m, (4, 1))
    i64 = np.arange(64)
    mq0 = np.tile(-(r[:, None] <= i64[None, :]).astype(np.float32), (4, 1))
    mq1 = np.tile(-((r[:, None] + 32) <= i64[None, :]).astype(np.float32), (4, 1))
    return identb, identf, onecf, onerf, mk32q, mq0, mq1


def kernel(**inp):
    x = np.asarray(inp["L0_attn_x"], np.float32)
    xfm = np.ascontiguousarray(x.transpose(0, 2, 1)).reshape(B, 8, 128, L)

    def colslice(W, h):
        return np.ascontiguousarray(W[:, h * 128:(h + 1) * 128].reshape(8, 128, 128))

    Wq, Wk, Wv, Wa, Wg = (np.asarray(inp[f"L0_attn_W{n}"], np.float32) for n in "qkvag")
    Wb = np.asarray(inp["L0_attn_Wb"], np.float32)
    Wo = np.asarray(inp["L0_attn_Wo"], np.float32)
    ba = np.asarray(inp["L0_attn_ba"], np.float32)
    bb = np.asarray(inp["L0_attn_bb"], np.float32)
    cqw = np.asarray(inp["L0_attn_cqw"], np.float32)
    ckw = np.asarray(inp["L0_attn_ckw"], np.float32)
    cvw = np.asarray(inp["L0_attn_cvw"], np.float32)
    cqb = np.asarray(inp["L0_attn_cqb"], np.float32)
    ckb = np.asarray(inp["L0_attn_ckb"], np.float32)
    cvb = np.asarray(inp["L0_attn_cvb"], np.float32)
    lngv = np.asarray(inp["L0_attn_ln_g"], np.float32)
    lnbv = np.asarray(inp["L0_attn_ln_b"], np.float32)

    woTh = np.ascontiguousarray(Wo.reshape(8, 128, 8, 128).transpose(2, 1, 0, 3)).reshape(8, 128, 8 * 128)

    def ffn_lhsT(W, no, nk):
        Wr = W.reshape(no, 128, nk, 128)
        return np.ascontiguousarray(Wr.transpose(0, 3, 2, 1)).reshape(no, 128, nk * 128)

    w1w = np.asarray(inp["L0_ffn_w1_w"], np.float32)
    w2w = np.asarray(inp["L0_ffn_w2_w"], np.float32)
    h1w = np.asarray(inp["L0_ffn_h1_w"], np.float32)
    h2w = np.asarray(inp["L0_ffn_h2_w"], np.float32)
    w3w = np.asarray(inp["L0_ffn_w3_w"], np.float32)
    w1Tb = ffn_lhsT(w1w, 32, 8).astype(ml_dtypes.bfloat16)
    w2Tb = ffn_lhsT(w2w, 32, 8).astype(ml_dtypes.bfloat16)
    h1Tb = ffn_lhsT(h1w, 32, 32).astype(ml_dtypes.bfloat16)
    h2Tb = ffn_lhsT(h2w, 32, 32).astype(ml_dtypes.bfloat16)
    w3Tb = ffn_lhsT(w3w, 8, 32).astype(ml_dtypes.bfloat16)
    fsclv = np.zeros((128, 5 * 32), np.float32)
    for fi, sname in enumerate(("w1_s", "w2_s", "h1_s", "h2_s", "w3_s")):
        sv = np.asarray(inp["L0_ffn_" + sname], np.float32)
        nch = sv.shape[0] // 128
        fsclv[:, fi * 32:fi * 32 + nch] = sv.reshape(nch, 128).T

    identb, identf, onecf, onerf, mk32q, mq0, mq1 = _consts()

    if "nc" not in _NC_CACHE:
        _NC_CACHE["nc"] = build_nc()
    ncp = _NC_CACHE["nc"]

    in_maps = []
    for s in range(8):
        h = s
        mb, half = s // 2, s % 2
        xres = np.empty((8, 128, NRND * 512), np.float32)
        for e in range(NRND):
            xres[:, :, e * 512:(e + 1) * 512] = xfm[mb, :, :, e * 1024 + half * 512: e * 1024 + half * 512 + 512]
        w5v = np.concatenate([colslice(W, h) for W in (Wq, Wk, Wv, Wa, Wg)], 0)
        w5v = np.ascontiguousarray(w5v.transpose(1, 0, 2)).reshape(128, 40 * 128)
        convwv = np.stack([cqw[h * 128:(h + 1) * 128], ckw[h * 128:(h + 1) * 128],
                           cvw[h * 128:(h + 1) * 128]], 0)
        convwv = np.ascontiguousarray(convwv.transpose(1, 0, 2)).reshape(128, 12)
        convbv = np.stack([cqb[h * 128:(h + 1) * 128], ckb[h * 128:(h + 1) * 128],
                           cvb[h * 128:(h + 1) * 128]], 1)
        in_maps.append({
            "xfm": xfm, "xre": xres, "w5": w5v,
            "wbcol": np.ascontiguousarray(Wb[:, h].reshape(8, 128).T),
            "convw": convwv, "convb": np.ascontiguousarray(convbv),
            "bneg": ba[h * 128:(h + 1) * 128].reshape(128, 1).copy(),
            "bbv": bb[h].reshape(1, 1),
            "lngb": np.stack([lngv, lnbv], 1),
            "woT": woTh, "w1T": w1Tb, "w2T": w2Tb, "h1T": h1Tb, "h2T": h2Tb, "w3T": w3Tb,
            "fscl": fsclv, "identb": identb, "identf": identf,
            "onecf": onecf, "onerf": onerf, "mk32q": mk32q, "mq0": mq0, "mq1": mq1,
        })

    res = run_bass_kernel_spmd(ncp, in_maps, core_ids=list(range(8)))
    out = np.empty((B, L, D), np.float32)
    for s in range(8):
        mb, half = s // 2, s % 2
        o = res.results[s]["outp"]
        for e in range(NRND):
            seg = o[:, :, e * 512:(e + 1) * 512]
            t0 = e * 1024 + half * 512
            out[mb, t0:t0 + 512, :] = seg.transpose(2, 0, 1).reshape(512, D)
    return out


# version 09 (final)
# speedup vs baseline: 1.2603x; 1.0177x over previous
import contextlib
import math
import numpy as np
import ml_dtypes
import concourse.bass as bass
import concourse.mybir as mybir
from concourse.bass_utils import run_bass_kernel_spmd
from concourse.aot_env import allow_ici_all_to_all


def _cc_alltoall(e, iap, oap):
    with allow_ici_all_to_all():
        return e.collective_compute(
            "AllToAll", mybir.AluOpType.bypass, replica_groups=[list(range(8))],
            ins=[iap], outs=[oap])

F32 = mybir.dt.float32
BF16 = mybir.dt.bfloat16
AF = mybir.ActivationFunctionType
OP = mybir.AluOpType

B, L, D = 4, 4096, 1024
TT, CH = 512, 64
NTT, NCH = L // TT, TT // CH
NRND = 4
DKI = 1.0 / math.sqrt(128.0)
NIT = 31
_DBG_TILES = None
_DBG_P2 = True
_DBG_DUMP = False
_DBG_CHUNKS = None
_DBG_STAGE = 3
_DBG_PREP = 99
_DBG_BSET = (0, 1, 2, 3)


def build_nc():
    nc = bass.Bass("TRN2", target_bir_lowering=False)

    xfm = nc.dram_tensor("xfm", [B, 8, 128, L], F32, kind="ExternalInput").ap()
    xre = nc.dram_tensor("xre", [8, 128, NRND * 512], F32, kind="ExternalInput").ap()
    w5 = nc.dram_tensor("w5", [128, 40 * 128], F32, kind="ExternalInput").ap()
    wbcol = nc.dram_tensor("wbcol", [128, 8], F32, kind="ExternalInput").ap()
    convw = nc.dram_tensor("convw", [128, 12], F32, kind="ExternalInput").ap()
    convb = nc.dram_tensor("convb", [128, 3], F32, kind="ExternalInput").ap()
    bneg = nc.dram_tensor("bneg", [128, 1], F32, kind="ExternalInput").ap()
    bbv = nc.dram_tensor("bbv", [1, 1], F32, kind="ExternalInput").ap()
    lngb = nc.dram_tensor("lngb", [128, 2], F32, kind="ExternalInput").ap()
    woT = nc.dram_tensor("woT", [8, 128, 8 * 128], F32, kind="ExternalInput").ap()
    w1T = nc.dram_tensor("w1T", [32, 128, 8 * 128], BF16, kind="ExternalInput").ap()
    w2T = nc.dram_tensor("w2T", [32, 128, 8 * 128], BF16, kind="ExternalInput").ap()
    h1T = nc.dram_tensor("h1T", [32, 128, 32 * 128], BF16, kind="ExternalInput").ap()
    h2T = nc.dram_tensor("h2T", [32, 128, 32 * 128], BF16, kind="ExternalInput").ap()
    w3T = nc.dram_tensor("w3T", [8, 128, 32 * 128], BF16, kind="ExternalInput").ap()
    fscl = nc.dram_tensor("fscl", [128, 5 * 32], F32, kind="ExternalInput").ap()
    identb = nc.dram_tensor("identb", [128, 128], BF16, kind="ExternalInput").ap()
    identf = nc.dram_tensor("identf", [128, 128], F32, kind="ExternalInput").ap()
    onecf = nc.dram_tensor("onecf", [128, 1], F32, kind="ExternalInput").ap()
    onerf = nc.dram_tensor("onerf", [1, 128], F32, kind="ExternalInput").ap()
    mk32q = nc.dram_tensor("mk32q", [128, 32], F32, kind="ExternalInput").ap()
    mq0 = nc.dram_tensor("mq0", [128, 64], F32, kind="ExternalInput").ap()
    mq1 = nc.dram_tensor("mq1", [128, 64], F32, kind="ExternalInput").ap()
    ibs = [nc.dram_tensor(f"ib{e}", [8, 128, 512], BF16) for e in range(NRND)]
    obs = [nc.dram_tensor(f"ob{e}", [8, 128, 512], BF16) for e in range(NRND)]
    x1d = nc.dram_tensor("x1d", [128, 8 * 512], F32)
    r2d = nc.dram_tensor("r2d", [128, 8 * 512], F32)
    hd = nc.dram_tensor("hd", [128, 32 * 512], BF16)
    outp = nc.dram_tensor("outp", [8, 128, NRND * 512], F32, kind="ExternalOutput").ap()
    if _DBG_DUMP:
        dbgo = nc.dram_tensor("dbgo", [128, 8 * 512], BF16, kind="ExternalOutput").ap()
        dbgs = nc.dram_tensor("dbgs", [128, B * 128], F32, kind="ExternalOutput").ap()
        dbgq = nc.dram_tensor("dbgq", [128, B * 512], BF16, kind="ExternalOutput").ap()
        dbgk = nc.dram_tensor("dbgk", [128, B * 512], BF16, kind="ExternalOutput").ap()
        dbgv = nc.dram_tensor("dbgv", [128, B * 512], BF16, kind="ExternalOutput").ap()
        dbgsp = nc.dram_tensor("dbgsp", [128, B * 512], F32, kind="ExternalOutput").ap()
        dbgln = nc.dram_tensor("dbgln", [128, B * 512], F32, kind="ExternalOutput").ap()
        dbgbe = nc.dram_tensor("dbgbe", [1, B * 512], F32, kind="ExternalOutput").ap()
        dbgu = nc.dram_tensor("dbgu", [128, 512], BF16, kind="ExternalOutput").ap()
        dbgos = nc.dram_tensor("dbgos", [64, 256], F32, kind="ExternalOutput").ap()
        dbgoc = nc.dram_tensor("dbgoc", [64, 256], F32, kind="ExternalOutput").ap()
        dbggq = nc.dram_tensor("dbggq", [128, 256], BF16, kind="ExternalOutput").ap()
        dbgmk = nc.dram_tensor("dbgmk", [128, 128], F32, kind="ExternalOutput").ap()
        dbgp3 = nc.dram_tensor("dbgp3", [128, 256], F32, kind="ExternalOutput").ap()
        dbge = nc.dram_tensor("dbge", [64, 8 * 128], F32, kind="ExternalOutput").ap()

    es = contextlib.ExitStack()
    with es:
        T = lambda name, shape, dt=F32: es.enter_context(nc.sbuf_tensor(name, shape, dt))
        wq = T("wq", [128, 40 * 128 + 8], BF16)
        cdiag = T("cdiag", [128, 12 * 128], BF16)
        cbias = T("cbias", [128, 3], F32)
        bnegc = T("bnegc", [128, 1], F32)
        bbc = T("bbc", [1, 1], F32)
        lng = T("lng", [128, 2], F32)
        idb = T("idb", [128, 128], BF16)
        idf = T("idf", [128, 128], F32)
        onc = T("onc", [128, 1], F32)
        onr = T("onr", [1, 128], F32)
        mks = T("mks", [128, 32], F32)
        mg0 = T("mg0", [128, 64], F32)
        mg1 = T("mg1", [128, 64], F32)
        fsc = T("fsc", [128, 5 * 32], F32)
        xf = T("xf", [128, 8 * TT], F32)
        xb = T("xb", [128, 8 * TT], BF16)
        qT_ = T("qT_", [128, B * TT], BF16)
        kT_ = T("kT_", [128, B * TT], BF16)
        vT_ = T("vT_", [128, B * TT], BF16)
        gT_ = T("gT_", [128, B * TT], BF16)
        spT_ = T("spT_", [128, B * TT], F32)
        lnT_ = T("lnT_", [128, B * TT], F32)
        beta_ = T("beta_", [1, B * TT], F32)
        pre = T("pre", [128, TT + 3], F32)
        preb = T("preb", [128, TT + 3], BF16)
        halo = T("halo", [128, 9 * B * 2], F32)
        st_f = T("st_f", [128, B * 128], F32)
        st_b = T("st_b", [128, B * 128], BF16)
        sbf = T("sbf", [128, 128], BF16)
        kct = T("kct", [128, 256], BF16)
        gq01 = T("gq01", [128, 128], BF16)
        gbd0 = T("gbd0", [128, 128], BF16)
        gbd1 = T("gbd1", [128, 128], BF16)
        gbdx = T("gbdx", [128, 128], BF16)
        efm = T("efm", [128, 4 * 192], F32)
        rhsp = T("rhsp", [128, 256], F32)
        e1t = T("e1t", [64, B * 128], F32)
        e2t = T("e2t", [64, B * 128], F32)
        cp0 = T("cp0", [128, 256], F32)
        cp1 = T("cp1", [128, 256], F32)
        invs = T("invs", [128, 256], F32)
        bstk = T("bstk", [128, 2], F32)
        bstn = T("bstn", [128, 2], F32)
        rstk = T("rstk", [128, 512], BF16)
        delta3 = [T(f"delta{i}", [128, 256], BF16) for i in range(2)]
        u0 = T("u0", [128, 256], BF16)
        u1 = T("u1", [128, 256], BF16)
        oS = T("oS", [64, 128], F32)
        oS2 = T("oS2", [64, 128], F32)
        ocol = T("ocol", [64, 2], F32)
        mcol = T("mcol", [128, 8], F32)
        ecolP = T("ecolP", [128, B], F32)
        ecolN = T("ecolN", [128, 2 * B], F32)
        ognm = T("ognm", [128, 64], F32)
        send = T("send", [128, 8 * 512], BF16)
        recv = T("recv", [128, 8 * 512], BF16)
        x1f = T("x1f", [128, 512], F32)
        x1b = T("x1b", [128, 8 * 512], BF16)
        r2f = T("r2f", [128, 512], F32)
        sqs = T("sqs", [128, 512], F32)
        rowa = T("rowa", [1, 512], F32)
        rowb = T("rowb", [1, 512], F32)
        hs0 = T("hs0", [128, 512], BF16)
        hs1 = T("hs1", [128, 512], BF16)
        ht = T("ht", [128, 32 * 512], BF16)
        wst = T("wst", [128, 2 * 4096], BF16)
        outf = T("outf", [128, 512], F32)
        sscr_state = T("sscr_state", [128, 128], F32)
        epsc = T("epsc", [128, 1], F32)
        ocdump = T("ocdump", [64, 256], F32)
        p3dump = T("p3dump", [128, 256], F32)

        pb = [es.enter_context(nc.psum_tensor(f"pb{i}", [128, 512], F32)) for i in range(6)]
        pbar = es.enter_context(nc.psum_tensor("pbar", [128, 512], F32))
        ptb = es.enter_context(nc.psum_tensor("ptb", [128, 512], BF16))

        sems = {}
        for k in ("ld", "x", "w", "mm", "ve", "sc", "cc", "sd", "rv", "st", "gp"):
            sems[k] = es.enter_context(nc.semaphore("s_" + k))
        cnt = {k: 0 for k in sems}
        prog = {k: [] for k in ("sync", "tensor", "vector", "scalar", "gpsimd")}

        def emit(eng, fn):
            prog[eng].append(fn)

        def dma(eng, out, in_, key, serial=True):
            if serial:
                wait(eng, key, cnt[key])
            cnt[key] += 16
            emit(eng, lambda e, out=out, in_=in_, key=key: e.dma_start(out=out, in_=in_).then_inc(sems[key], 16))
            return cnt[key]

        def wait(eng, key, val=None):
            if val is None:
                val = cnt[key]
            emit(eng, lambda e, key=key, val=val: e.wait_ge(sems[key], val))

        def op(eng, fn, key=None, nodrain=False):
            if eng in ("vector", "scalar", "gpsimd") and not nodrain:
                emit(eng, lambda e: e.drain())
            if key:
                cnt[key] += 1
                emit(eng, lambda e, fn=fn, key=key: fn(e).then_inc(sems[key], 1))
                return cnt[key]
            emit(eng, lambda e, fn=fn: fn(e))
            return None

        TP = [(0, 0), (0, 32), (0, 64), (0, 96)]

        def bar_mm():
            return op("tensor", lambda e: e.matmul(pbar[:, 0:4], idb[:, :], idb[:, 0:4],
                                                   start=True, stop=True, skip_group_check=True), "mm")

        dma("sync", idb[:, :], identb[:, :], "ld")
        dma("sync", idf[:, :], identf[:, :], "ld")
        dma("sync", onc[:, :], onecf[:, :], "ld")
        dma("sync", onr[:, :], onerf[:, :], "ld")
        dma("sync", mks[:, :], mk32q[:, :], "ld")
        dma("sync", mg0[:, :], mq0[:, :], "ld")
        dma("sync", mg1[:, :], mq1[:, :], "ld")
        dma("sync", bnegc[:, :], bneg[:, :], "ld")
        dma("sync", bbc[:, :], bbv[:, :], "ld")
        dma("sync", lng[:, :], lngb[:, :], "ld")
        dma("sync", cbias[:, :], convb[:, :], "ld")
        dma("sync", fsc[:, :], fscl[:, :], "ld")
        for g in range(5):
            wait("sync", "ve")
            v = dma("sync", xf[:, 0:8 * 128], w5[:, g * 1024:(g + 1) * 1024], "x")
            wait("vector", "x", v)
            op("vector", lambda e, g=g: e.tensor_copy(wq[:, g * 1024:(g + 1) * 1024], xf[:, 0:1024]), "ve")
        wait("sync", "ve")
        v = dma("sync", xf[:, 0:8], wbcol[:, :], "x")
        wait("vector", "x", v)
        op("vector", lambda e: e.tensor_copy(wq[:, 5120:5128], xf[:, 0:8]), "ve")
        v = dma("sync", xf[:, 8:20], convw[:, :], "x")
        wait("vector", "x", v)
        for t in range(12):
            op("vector", lambda e, t=t: e.tensor_scalar(
                cdiag[:, t * 128:(t + 1) * 128], idb[:, :], xf[:, 8 + t:9 + t], 0.0, OP.mult, OP.add))
        op("vector", lambda e: e.memset(epsc[:, :], 1e-5))
        op("vector", lambda e: e.memset(send[:, :], 0.0))
        op("vector", lambda e: e.memset(recv[:, :], 0.0))
        op("vector", lambda e: e.memset(halo[:, :], 0.0))
        op("vector", lambda e: e.memset(st_f[:, :], 0.0))
        op("vector", lambda e: e.memset(st_b[:, :], 0.0))
        op("vector", lambda e: e.memset(gbd0[:, :], 0.0))
        op("vector", lambda e: e.memset(gbd1[:, :], 0.0))
        op("vector", lambda e: e.memset(gbdx[:, :], 0.0))
        op("vector", lambda e: e.memset(gq01[:, :], 0.0))
        op("vector", lambda e: e.memset(kct[:, :], 0.0))
        op("vector", lambda e: e.memset(rstk[:, :], 0.0))
        op("vector", lambda e: e.memset(invs[:, :], 1.0))
        op("vector", lambda e: e.memset(bstk[:, :], 0.0))
        op("vector", lambda e: e.memset(bstn[:, :], 0.0), "ve")
        wait("tensor", "ve")
        wait("scalar", "ve")
        xff_ve = [cnt["ve"]]
        xff_sc = [cnt["sc"]]

        q_t = lambda b: qT_[:, b * TT:(b + 1) * TT]
        k_t = lambda b: kT_[:, b * TT:(b + 1) * TT]
        v_t = lambda b: vT_[:, b * TT:(b + 1) * TT]
        g_t = lambda b: gT_[:, b * TT:(b + 1) * TT]
        sp_t = lambda b: spT_[:, b * TT:(b + 1) * TT]
        ln_t = lambda b: lnT_[:, b * TT:(b + 1) * TT]
        be_t = lambda b: beta_[:, b * TT:(b + 1) * TT]
        st_fb = lambda b: st_f[:, b * 128:(b + 1) * 128]
        st_bb = lambda b: st_b[:, b * 128:(b + 1) * 128]

        def emit_proj(tt, b):
            t0 = tt * TT
            wait("sync", "ve", xff_ve[0])
            wait("sync", "sc", xff_sc[0])
            xv = 0
            for c in range(8):
                xv = dma("sync", xf[:, c * TT:(c + 1) * TT], xfm[b, c, :, t0:t0 + TT], "x", serial=(c == 0))
            wait("vector", "x", xv)
            wait("vector", "mm")
            for c in range(8):
                vx = op("vector", lambda e, c=c: e.tensor_copy(xb[:, c * TT:(c + 1) * TT], xf[:, c * TT:(c + 1) * TT]),
                        "ve" if c == 7 else None)
            xff_ve[0] = vx
            wait("tensor", "ve", vx)
            for ti in range(3):
                for c in range(8):
                    mm = op("tensor", lambda e, c=c, ti=ti: e.matmul(
                        pb[0][:, 0:TT], wq[:, (ti * 8 + c) * 128:(ti * 8 + c + 1) * 128],
                        xb[:, c * TT:(c + 1) * TT], start=(c == 0), stop=(c == 7)),
                        "mm" if c == 7 else None)
                wait("scalar", "mm", mm)
                wait("scalar", "ve")
                sc = op("scalar", lambda e: e.copy(pre[:, 3:3 + TT], pb[0][:, 0:TT]), "sc")
                wait("vector", "sc", sc)
                hi = (ti * B + b) * 3 + (tt % 2) * 36
                ho = (ti * B + b) * 3 + ((tt + 1) % 2) * 36
                op("vector", lambda e, hi=hi: e.tensor_copy(pre[:, 0:3], halo[:, hi:hi + 3]))
                op("vector", lambda e, ho=ho: e.tensor_copy(halo[:, ho:ho + 3], pre[:, TT:TT + 3]))
                cv = op("vector", lambda e: e.tensor_copy(preb[:, :], pre[:, :]), "ve")
                wait("tensor", "ve", cv)
                for j in range(4):
                    mm = op("tensor", lambda e, ti=ti, j=j: e.matmul(
                        pb[1][:, 0:TT], cdiag[:, (ti * 4 + j) * 128:(ti * 4 + j + 1) * 128],
                        preb[:, j:j + TT], start=(j == 0), stop=(j == 3)),
                        "mm" if j == 3 else None)
                wait("scalar", "mm", mm)
                dstt = (q_t(b), k_t(b), v_t(b))[ti]
                scv = op("scalar", lambda e, ti=ti, dstt=dstt: e.activation(
                    dstt, pb[1][:, 0:TT], AF.Silu, bias=cbias[:, ti:ti + 1], scale=1.0), "sc")
                if ti == 1:
                    wait("vector", "sc", scv)
                    op("vector", lambda e, dstt=dstt: e.tensor_scalar(dstt, dstt, DKI, 0.0, OP.mult, OP.add), "ve")
            for c in range(8):
                mm = op("tensor", lambda e, c=c: e.matmul(
                    pb[0][:, 0:TT], wq[:, (24 + c) * 128:(24 + c + 1) * 128],
                    xb[:, c * TT:(c + 1) * TT], start=(c == 0), stop=(c == 7)),
                    "mm" if c == 7 else None)
            wait("scalar", "mm", mm)
            op("scalar", lambda e: e.activation(xf[:, 0:TT], pb[0][:, 0:TT], AF.Sigmoid,
                                                bias=bnegc[:, 0:1], scale=1.0))
            xff_sc[0] = op("scalar", lambda e, b=b: e.activation(sp_t(b), xf[:, 0:TT], AF.Ln), "sc")
            wait("tensor", "sc")
            for c in range(8):
                mm = op("tensor", lambda e, c=c: e.matmul(
                    pb[1][:, 0:TT], wq[:, (32 + c) * 128:(32 + c + 1) * 128],
                    xb[:, c * TT:(c + 1) * TT], start=(c == 0), stop=(c == 7)),
                    "mm" if c == 7 else None)
            wait("scalar", "mm", mm)
            op("scalar", lambda e, b=b: e.activation(g_t(b), pb[1][:, 0:TT], AF.Sigmoid), "sc")
            wait("tensor", "sc")
            for c in range(8):
                mm = op("tensor", lambda e, c=c: e.matmul(
                    pb[0][0:1, 0:TT], wq[:, 5120 + c:5121 + c],
                    xb[:, c * TT:(c + 1) * TT], start=(c == 0), stop=(c == 7)),
                    "mm" if c == 7 else None)
            wait("scalar", "mm", mm)
            scb = op("scalar", lambda e, b=b: e.activation(be_t(b), pb[0][0:1, 0:TT], AF.Sigmoid,
                                                           bias=bbc[0:1, 0:1], scale=1.0), "sc")
            wait("vector", "sc", scb)
            for c in range(NCH):
                op("vector", lambda e, b=b, c=c: e.tensor_tensor_scan(
                    out=ln_t(b)[:, c * CH:(c + 1) * CH], data0=sp_t(b)[:, c * CH:(c + 1) * CH],
                    data1=sp_t(b)[:, c * CH:(c + 1) * CH], initial=0.0, op0=OP.add, op1=OP.bypass),
                   "ve" if c == NCH - 1 else None)

        def emit_prep_all(tt, c):
            c0 = c * CH
            lncs = [ln_t(b)[:, c0:c0 + CH] for b in range(B)]
            wait("vector", "sc", cnt["sc"])
            for b in range(B):
                op("vector", lambda e, b=b: e.tensor_scalar(mcol[:, 2 * b:2 * b + 1], lncs[b][:, CH - 1:CH], 0.5, 0.0, OP.mult, OP.add))
                mv = op("vector", lambda e, b=b: e.tensor_scalar(mcol[:, 2 * b + 1:2 * b + 2], lncs[b][:, CH - 1:CH], -0.5, 0.0, OP.mult, OP.add),
                        "ve" if b == B - 1 else None)
            wait("scalar", "ve", mv)
            wait("scalar", "mm")
            for b in range(B):
                eo = 192 * b
                op("scalar", lambda e, b=b, eo=eo: e.activation(efm[:, eo:eo + 64], lncs[b], AF.Exp, scale=1.0))
                e2v = op("scalar", lambda e, b=b, eo=eo: e.activation(efm[:, eo + 64:eo + 128], lncs[b], AF.Exp, bias=mcol[:, 2 * b + 1:2 * b + 2], scale=1.0),
                         "sc" if b == B - 1 else None)
            wait("vector", "sc", e2v)
            for b in range(B):
                tsub = op("vector", lambda e, b=b, c0=c0: e.tensor_tensor(
                    rhsp[:, 64 * b:64 * b + 64], lncs[b], sp_t(b)[:, c0:c0 + CH], OP.subtract), "ve" if b == B - 1 else None)
            wait("scalar", "ve", tsub)
            for b in range(B):
                eo = 192 * b
                e3 = op("scalar", lambda e, b=b, eo=eo: e.activation(efm[:, eo + 128:eo + 192], rhsp[:, 64 * b:64 * b + 64], AF.Exp, bias=mcol[:, 2 * b:2 * b + 1], scale=-1.0),
                        "sc" if b == B - 1 else None)
            wait("vector", "sc", e3)
            for b in range(B):
                eo, ro = 192 * b, 64 * b
                op("vector", lambda e, b=b, c0=c0, eo=eo, ro=ro: e.tensor_tensor(rhsp[:, ro:ro + 64], efm[:, eo + 128:eo + 192], v_t(b)[:, c0:c0 + CH], OP.mult))
                op("vector", lambda e, ro=ro: e.tensor_scalar(rhsp[:, ro:ro + 64], rhsp[:, ro:ro + 64], -1.0, 0.0, OP.mult, OP.add))
                op("vector", lambda e, b=b, eo=eo: e.tensor_copy(ecolP[:, b:b + 1], efm[:, eo + 63:eo + 64]))
                op("vector", lambda e, b=b, eo=eo: e.tensor_scalar(ecolN[:, 2 * b:2 * b + 1], efm[:, eo + 63:eo + 64], -1.0, 0.0, OP.mult, OP.add))
                rr = op("vector", lambda e, b=b, eo=eo: e.tensor_scalar(ecolN[:, 2 * b + 1:2 * b + 2], efm[:, eo + 127:eo + 128], -1.0, 0.0, OP.mult, OP.add),
                        "ve" if b == B - 1 else None)
            wait("tensor", "ve", rr)
            for b in range(B):
                tp = TP[b]
                r0, r1 = 32 * b, 32 * b + 32
                Kc = k_t(b)[:, c0:c0 + CH]
                Qc = q_t(b)[:, c0:c0 + CH]
                eo, ro = 192 * b, 64 * b
                pbt = pb[b % 2]
                op("tensor", lambda e, pbt=pbt, r0=r0, r1=r1, b=b, c0=c0, tp=tp: e.matmul(
                    pbt[r0:r1, 0:128], sp_t(b)[:, c0:c0 + 32], idf[:, :], start=True, stop=True, tile_position=tp, skip_group_check=True))
                op("tensor", lambda e, pbt=pbt, r0=r0, r1=r1, b=b, c0=c0, tp=tp: e.matmul(
                    pbt[r0:r1, 128:256], sp_t(b)[:, c0 + 32:c0 + 64], idf[:, :], start=True, stop=True, tile_position=tp, skip_group_check=True))
                op("tensor", lambda e, pbt=pbt, r0=r0, r1=r1, ro=ro, tp=tp: e.matmul(
                    pbt[r0:r1, 256:384], rhsp[:, ro:ro + 32], idf[:, :], start=True, stop=True, tile_position=tp, skip_group_check=True))
                op("tensor", lambda e, pbt=pbt, r0=r0, r1=r1, ro=ro, tp=tp: e.matmul(
                    pbt[r0:r1, 384:512], rhsp[:, ro + 32:ro + 64], idf[:, :], start=True, stop=True, tile_position=tp, skip_group_check=True))
                op("tensor", lambda e, b=b, c0=c0, tp=tp: e.matmul(
                    pb[4][32 * b:32 * b + 32, 0:1], be_t(b)[0:1, c0:c0 + 32], onc[0:1, 0:1], start=True, stop=True, tile_position=tp, skip_group_check=True))
                op("tensor", lambda e, b=b, c0=c0, tp=tp: e.matmul(
                    pb[4][32 * b:32 * b + 32, 1:2], be_t(b)[0:1, c0 + 32:c0 + 64], onc[0:1, 0:1], start=True, stop=True, tile_position=tp, skip_group_check=True))
                op("tensor", lambda e, b=b, eo=eo: e.matmul(pb[2][0:64, 128 * b:128 * b + 128], efm[:, eo:eo + 64], idf[:, :], start=True, stop=True, skip_group_check=True))
                op("tensor", lambda e, b=b, eo=eo: e.matmul(pb[5][0:64, 128 * b:128 * b + 128], efm[:, eo + 64:eo + 128], idf[:, :], start=True, stop=True, skip_group_check=True))
                op("tensor", lambda e, Kc=Kc, b=b: e.matmul(
                    pb[3][32 * b:32 * b + 32, 0:64], Kc[:, 0:32], Kc[:, :], start=True, stop=True, tile_position=(0, 32 * b), skip_group_check=True))
                op("tensor", lambda e, Kc=Kc, b=b: e.matmul(
                    pb[3][32 * b:32 * b + 32, 64:96], Kc[:, 32:64], Kc[:, 32:64], start=True, stop=True, tile_position=(0, 32 * b), skip_group_check=True))
                op("tensor", lambda e, Kc=Kc, Qc=Qc, b=b: e.matmul(
                    pb[3][32 * b:32 * b + 32, 128:192], Kc[:, 0:32], Qc[:, :], start=True, stop=True, tile_position=(0, 32 * b), skip_group_check=True))
                op("tensor", lambda e, Kc=Kc, Qc=Qc, b=b: e.matmul(
                    pb[3][32 * b:32 * b + 32, 192:256], Kc[:, 32:64], Qc[:, :], start=True, stop=True, tile_position=(0, 32 * b), skip_group_check=True))
                op("tensor", lambda e, Kc=Kc, b=b, tp=tp: e.matmul(
                    pb[4][32 * b:32 * b + 32, 128:256], Kc[:, 0:32], idb[:, :], start=True, stop=True, tile_position=tp, skip_group_check=True))
                op("tensor", lambda e, Kc=Kc, b=b, tp=tp: e.matmul(
                    pb[4][32 * b:32 * b + 32, 256:384], Kc[:, 32:64], idb[:, :], start=True, stop=True, tile_position=tp, skip_group_check=True))
            tv = bar_mm()
            wait("scalar", "mm", tv)
            for b in range(B):
                r0, r1 = 32 * b, 32 * b + 32
                pbt = pb[b % 2]
                op("scalar", lambda e, pbt=pbt, r0=r0, r1=r1: e.activation(invs[r0:r1, 0:128], pbt[r0:r1, 0:128], AF.Exp, scale=-1.0))
                op("scalar", lambda e, pbt=pbt, r0=r0, r1=r1: e.activation(invs[r0:r1, 128:256], pbt[r0:r1, 128:256], AF.Exp, scale=-1.0))
                op("scalar", lambda e, b=b: e.copy(kct[32 * b:32 * b + 32, :], pb[4][32 * b:32 * b + 32, 128:384]))
                op("scalar", lambda e, r0=r0, r1=r1: e.copy(bstk[r0:r1, 0:2], pb[4][r0:r1, 0:2]))
                op("scalar", lambda e, r0=r0, r1=r1: e.mul(bstn[r0:r1, 0:2], pb[4][r0:r1, 0:2], -1.0))
                op("scalar", lambda e, pbt=pbt, r0=r0, r1=r1: e.copy(rstk[r0:r1, 128:256], pbt[r0:r1, 256:384]))
                sv = op("scalar", lambda e, pbt=pbt, r0=r0, r1=r1: e.copy(rstk[r0:r1, 384:512], pbt[r0:r1, 384:512]),
                        "sc" if b == B - 1 else None)
            wait("vector", "mm", tv)
            for b in range(B):
                r0, r1 = 32 * b, 32 * b + 32
                op("vector", lambda e, b=b: e.tensor_copy(e1t[:, b * 128:(b + 1) * 128], pb[2][0:64, 128 * b:128 * b + 128]))
                op("vector", lambda e, b=b: e.tensor_copy(e2t[:, b * 128:(b + 1) * 128], pb[5][0:64, 128 * b:128 * b + 128]))
                op("vector", lambda e, r0=r0, r1=r1: e.tensor_tensor(gbd0[r0:r1, r0:r1], pb[3][r0:r1, 0:32], mks[r0:r1, :], OP.mult))
                op("vector", lambda e, r0=r0, r1=r1: e.tensor_scalar(gbdx[r0:r1, r0:r1], pb[3][r0:r1, 32:64], -1.0, 0.0, OP.mult, OP.add))
                op("vector", lambda e, r0=r0, r1=r1: e.tensor_tensor(gbd1[r0:r1, r0:r1], pb[3][r0:r1, 64:96], mks[r0:r1, :], OP.mult))
                op("vector", lambda e, r0=r0, r1=r1: e.tensor_tensor(gq01[r0:r1, 0:64], pb[3][r0:r1, 128:192], mg0[r0:r1, :], OP.mult))
                vv = op("vector", lambda e, r0=r0, r1=r1: e.tensor_tensor(gq01[r0:r1, 64:128], pb[3][r0:r1, 192:256], mg1[r0:r1, :], OP.mult),
                        "ve" if b == B - 1 else None)
            wait("tensor", "ve", vv)
            wait("tensor", "sc", sv)

        def emit_chain(tt, c):
            c0 = c * CH
            wait("vector", "sc")
            op("vector", lambda e: e.tensor_scalar(cp0[:, 0:128], invs[:, 0:128], bstk[:, 0:1], 0.0, OP.mult, OP.add))
            op("vector", lambda e: e.tensor_copy(cp0[:, 128:256], cp0[:, 0:128]))
            op("vector", lambda e: e.tensor_scalar(cp1[:, 0:128], invs[:, 128:256], bstk[:, 1:2], 0.0, OP.mult, OP.add))
            cpv = op("vector", lambda e: e.tensor_copy(cp1[:, 128:256], cp1[:, 0:128]), "ve")
            wait("tensor", "ve", cpv)
            op("tensor", lambda e: e.matmul(pb[2][:, 0:256], idb[:, :], rstk[:, 0:256], start=True, stop=True))
            for b in range(B):
                op("tensor", lambda e, b=b, c0=c0: e.matmul(
                    pb[2][32 * b:32 * b + 32, 0:128], k_t(b)[:, c0:c0 + 32], st_bb(b),
                    start=False, stop=False, tile_position=(0, 32 * b),
                    skip_group_check=True))
            mmv = bar_mm()
            wait("vector", "mm", mmv)
            dv = op("vector", lambda e: e.tensor_tensor(delta3[0][:, :], pb[2][:, 0:256], cp0[:, :], OP.mult), "ve")
            wait("tensor", "ve", dv)
            for j in range(NIT):
                d = delta3[j % 2]
                mmv = op("tensor", lambda e, d=d: e.matmul(pb[3][:, 0:256], gbd0[:, :], d[:, :], start=True, stop=True), "mm")
                op("tensor", lambda e, d=d, j=j: e.matmul(pb[4][:, 0:256], idb[:, :], d[:, :], start=(j == 0), stop=False))
                wait("vector", "mm", mmv)
                dn = delta3[(j + 1) % 2]
                dv = op("vector", lambda e, dn=dn: e.tensor_tensor(dn[:, :], pb[3][:, 0:256], cp0[:, :], OP.mult), "ve", nodrain=True)
                wait("tensor", "ve", dv)
            mmv = op("tensor", lambda e: e.matmul(pb[4][:, 0:256], idb[:, :], delta3[NIT % 2][:, :], start=False, stop=True), "mm")
            wait("scalar", "mm", mmv)
            uv = op("scalar", lambda e: e.copy(u0[:, :], pb[4][:, 0:256]), "sc")
            wait("tensor", "sc", uv)
            op("tensor", lambda e: e.matmul(pb[5][:, 0:256], idb[:, :], rstk[:, 256:512], start=True, stop=True))
            for b in range(B):
                op("tensor", lambda e, b=b, c0=c0: e.matmul(
                    pb[5][32 * b:32 * b + 32, 0:128], k_t(b)[:, c0 + 32:c0 + 64], st_bb(b),
                    start=False, stop=False, tile_position=(0, 32 * b),
                    skip_group_check=True))
            mmv = op("tensor", lambda e: e.matmul(pb[5][:, 0:256], gbdx[:, :], u0[:, :], start=False, stop=True,
                                                  skip_group_check=True), "mm")
            wait("vector", "mm", mmv)
            dv = op("vector", lambda e: e.tensor_tensor(delta3[0][:, :], pb[5][:, 0:256], cp1[:, :], OP.mult), "ve")
            wait("tensor", "ve", dv)
            for j in range(NIT):
                d = delta3[j % 2]
                mmv = op("tensor", lambda e, d=d: e.matmul(pb[3][:, 0:256], gbd1[:, :], d[:, :], start=True, stop=True), "mm")
                op("tensor", lambda e, d=d, j=j: e.matmul(pb[4][:, 0:256], idb[:, :], d[:, :], start=(j == 0), stop=False))
                wait("vector", "mm", mmv)
                dn = delta3[(j + 1) % 2]
                dv = op("vector", lambda e, dn=dn: e.tensor_tensor(dn[:, :], pb[3][:, 0:256], cp1[:, :], OP.mult), "ve", nodrain=True)
                wait("tensor", "ve", dv)
            mmv = op("tensor", lambda e: e.matmul(pb[4][:, 0:256], idb[:, :], delta3[NIT % 2][:, :], start=False, stop=True), "mm")
            wait("scalar", "mm", mmv)
            uv = op("scalar", lambda e: e.copy(u1[:, :], pb[4][:, 0:256]), "sc")
            wait("tensor", "sc", uv)
            wait("vector", "sc", uv)

        def emit_post(tt, b, c):
            tp = TP[b]
            r0, r1 = 32 * b, 32 * b + 32
            c0 = c * CH
            slot = 2 * b + (tt % 2)
            op("tensor", lambda e, b=b, c0=c0: e.matmul(
                pb[2][0:64, 0:128], q_t(b)[:, c0:c0 + CH], st_bb(b), start=True, stop=True, skip_group_check=True))
            op("tensor", lambda e, r0=r0, r1=r1: e.matmul(
                pb[2][0:64, 0:128], gq01[r0:r1, 0:64], u0[r0:r1, 0:128], start=False, stop=False, tile_position=(r0, 0), skip_group_check=True))
            op("tensor", lambda e, r0=r0, r1=r1: e.matmul(
                pb[2][0:64, 0:128], gq01[r0:r1, 64:128], u1[r0:r1, 0:128], start=False, stop=False, tile_position=(r0, 0), skip_group_check=True))
            op("tensor", lambda e, r0=r0, r1=r1: e.matmul(
                pb[2][0:64, 128:256], gq01[r0:r1, 0:64], u0[r0:r1, 128:256], start=True, stop=True, tile_position=(r0, 0), skip_group_check=True))
            op("tensor", lambda e, r0=r0, r1=r1: e.matmul(
                pb[2][0:64, 128:256], gq01[r0:r1, 64:128], u1[r0:r1, 128:256], start=False, stop=False, tile_position=(r0, 0), skip_group_check=True))
            om = bar_mm()
            op("tensor", lambda e, r0=r0, r1=r1: e.matmul(
                pb[1][:, 0:128], u0[r0:r1, 0:128], kct[r0:r1, 0:128], start=True, stop=True, tile_position=(r0, 0), skip_group_check=True))
            op("tensor", lambda e, r0=r0, r1=r1: e.matmul(
                pb[1][:, 0:128], u1[r0:r1, 0:128], kct[r0:r1, 128:256], start=False, stop=False, tile_position=(r0, 0), skip_group_check=True))
            op("tensor", lambda e, r0=r0, r1=r1: e.matmul(
                pb[1][:, 128:256], u0[r0:r1, 128:256], kct[r0:r1, 0:128], start=True, stop=True, tile_position=(r0, 0), skip_group_check=True))
            op("tensor", lambda e, r0=r0, r1=r1: e.matmul(
                pb[1][:, 128:256], u1[r0:r1, 128:256], kct[r0:r1, 128:256], start=False, stop=False, tile_position=(r0, 0), skip_group_check=True))
            sm = bar_mm()
            wait("vector", "mm", om)
            if _DBG_DUMP and b == 3:
                op("vector", lambda e: e.tensor_copy(ocdump[0:64, 0:128], pb[2][0:64, 0:128]))
                op("vector", lambda e: e.tensor_copy(ocdump[0:64, 128:256], pb[2][0:64, 128:256]))
            op("vector", lambda e, b=b: e.tensor_tensor(oS[:, :], pb[2][0:64, 0:128], e1t[:, b * 128:(b + 1) * 128], OP.mult))
            op("vector", lambda e, b=b: e.tensor_tensor(oS2[:, :], pb[2][0:64, 128:256], e2t[:, b * 128:(b + 1) * 128], OP.mult))
            op("vector", lambda e: e.tensor_tensor(oS[:, :], oS[:, :], oS2[:, :], OP.add))
            op("vector", lambda e: e.tensor_reduce(ocol[:, 0:1], oS[:, :], mybir.AxisListType.X, OP.add))
            op("vector", lambda e: e.tensor_scalar(ocol[:, 0:1], ocol[:, 0:1], 1.0 / 128.0, 0.0, OP.mult, OP.add))
            ov = op("vector", lambda e: e.tensor_scalar(oS[:, :], oS[:, :], ocol[:, 0:1], 0.0, OP.subtract, OP.add), "ve")
            wait("scalar", "ve", ov)
            sq = op("scalar", lambda e: e.activation(oS2[:, :], oS[:, :], AF.Square), "sc")
            wait("vector", "sc", sq)
            rv = op("vector", lambda e: e.tensor_reduce(ocol[:, 1:2], oS2[:, :], mybir.AxisListType.X, OP.add), "ve")
            wait("scalar", "ve", rv)
            sq = op("scalar", lambda e: e.activation(ocol[:, 1:2], ocol[:, 1:2], AF.Sqrt, bias=epsc[0:64, 0:1], scale=1.0 / 128.0), "sc")
            wait("vector", "sc", sq)
            op("vector", lambda e: e.reciprocal(ocol[:, 1:2], ocol[:, 1:2]))
            onv = op("vector", lambda e: e.tensor_scalar(oS[:, :], oS[:, :], ocol[:, 1:2], 0.0, OP.mult, OP.add), "ve")
            wait("tensor", "ve", onv)
            ot = op("tensor", lambda e: e.matmul(pb[2][0:128, 256:320], oS[:, :], idf[0:64, 0:64], start=True, stop=True, skip_group_check=True), "mm")
            wait("vector", "mm", ot)
            og = op("vector", lambda e: e.tensor_scalar(ognm[:, :], pb[2][0:128, 256:320],
                                                        lng[:, 0:1], lng[:, 1:2], OP.mult, OP.add), "ve")
            gv = op("vector", lambda e, b=b, c0=c0, slot=slot, c=c: e.tensor_tensor(
                send[:, slot * 512 + c * 64:slot * 512 + (c + 1) * 64], ognm[:, :],
                g_t(b)[:, c0:c0 + CH], OP.mult), "ve")
            wait("vector", "mm", sm)
            op("vector", lambda e, b=b: e.tensor_scalar(sscr_state[:, :], st_fb(b), ecolP[:, b:b + 1], 0.0, OP.mult, OP.add))
            op("vector", lambda e, b=b: e.scalar_tensor_tensor(
                out=sscr_state[:, :], in0=pb[1][:, 0:128], scalar=ecolN[:, 2 * b:2 * b + 1],
                in1=sscr_state[:, :], op0=OP.mult, op1=OP.add))
            sv = op("vector", lambda e, b=b: e.scalar_tensor_tensor(
                out=st_fb(b), in0=pb[1][:, 128:256], scalar=ecolN[:, 2 * b + 1:2 * b + 2],
                in1=sscr_state[:, :], op0=OP.mult, op1=OP.add), "ve")
            cb = op("vector", lambda e, b=b: e.tensor_copy(sbf[:, :], st_fb(b)), "ve")
            wait("tensor", "ve", cb)
            st_tr = op("tensor", lambda e: e.matmul(pb[1][:, 256:384], sbf[:, :], idb[:, :], start=True, stop=True, skip_group_check=True), "mm")
            wait("scalar", "mm", st_tr)
            sv2 = op("scalar", lambda e, b=b: e.copy(st_bb(b), pb[1][:, 256:384]), "sc")
            wait("tensor", "sc", sv2)
            wait("vector", "sc", sv2)


        def emit_exchange(e_):
            wait("gpsimd", "ve")
            for s in range(8):
                dma("gpsimd", ibs[e_].ap()[s, :, :], send[:, s * 512:(s + 1) * 512], "sd", serial=(s == 0))
            wait("gpsimd", "sd")
            op("gpsimd", lambda e, e_=e_: _cc_alltoall(e, ibs[e_].ap().opt(), obs[e_].ap().opt()), "cc")
            wait("sync", "cc", cnt["cc"])
            for r in range(8):
                dma("sync", recv[:, r * 512:(r + 1) * 512], obs[e_].ap()[r, :, :], "rv", serial=(r == 0))

        def emit_p2(e_):
            wait("tensor", "rv", cnt["rv"])
            wait("sync", "ve")
            wait("vector", "w")
            wait("vector", "x")
            wait("vector", "mm")
            xrv = 0
            for oc in range(8):
                wait("sync", "ve", xff_ve[0])
                if oc == 0:
                    wait("sync", "sc", xff_sc[0])
                wait("sync", "mm")
                wait("sync", "gp")
                wv = dma("sync", xf[:, 0:1024], woT[oc, :, :], "x")
                wait("vector", "x", wv)
                cv = op("vector", lambda e: e.tensor_copy(wst[:, 0:1024], xf[:, 0:1024]), "ve")
                xff_ve[0] = cv
                wait("tensor", "ve", cv)
                for r in range(8):
                    mm = op("tensor", lambda e, r=r, oc=oc: e.matmul(
                        pb[5][:, 0:512], wst[:, r * 128:(r + 1) * 128],
                        recv[:, r * 512:(r + 1) * 512], start=(r == 0), stop=(r == 7)),
                        "mm" if r == 7 else None)
                xrv = dma("sync", x1f[:, :], xre[oc, :, e_ * 512:(e_ + 1) * 512], "x")
                wait("vector", "mm", mm)
                wait("vector", "x", xrv)
                rv1 = op("vector", lambda e: e.tensor_tensor(x1f[:, :], pb[5][:, 0:512], x1f[:, :], OP.add), "ve")
                wait("scalar", "ve", rv1)
                sv = op("scalar", lambda e: e.activation(sqs[:, :], x1f[:, :], AF.Square), "sc")
                wait("tensor", "sc", sv)
                op("tensor", lambda e, oc=oc: e.matmul(pb[4][0:1, 0:512], onc[:, :], x1f[:, :],
                                                       start=(oc == 0), stop=(oc == 7)))
                mm = op("tensor", lambda e, oc=oc: e.matmul(pb[3][0:1, 0:512], onc[:, :], sqs[:, :],
                                                            start=(oc == 0), stop=(oc == 7)), "mm")
                wait("gpsimd", "mm", mm)
                dv = dma("gpsimd", x1d.ap()[:, oc * 512:(oc + 1) * 512], x1f[:, :], "gp")
            wait("vector", "mm", mm)
            op("vector", lambda e: e.tensor_copy(rowa[:, :], pb[4][0:1, 0:512]))
            rv2 = op("vector", lambda e: e.tensor_copy(rowb[:, :], pb[3][0:1, 0:512]), "ve")
            emit_ln_rows(rv2)
            wait("tensor", "ve")
            bc = op("tensor", lambda e: e.matmul(pb[0][:, 0:512], onr[:, :], rowa[:, :], start=True, stop=True))
            bc = op("tensor", lambda e: e.matmul(pb[1][:, 0:512], onr[:, :], rowb[:, :], start=True, stop=True), "mm")
            wait("vector", "mm", bc)
            wait("scalar", "gp", cnt["gp"])
            for oc in range(8):
                wait("scalar", "ld")
                xv = dma("scalar", x1f[:, :], x1d.ap()[:, oc * 512:(oc + 1) * 512], "x")
                wait("vector", "x", xv)
                op("vector", lambda e: e.tensor_tensor(x1f[:, :], x1f[:, :], pb[0][:, 0:512], OP.subtract))
                op("vector", lambda e: e.tensor_tensor(x1f[:, :], x1f[:, :], pb[1][:, 0:512], OP.mult))
                cv = op("vector", lambda e, oc=oc: e.tensor_copy(x1b[:, oc * 512:(oc + 1) * 512], x1f[:, :]), "ve")
                wait("scalar", "ve", cv)
                dma("scalar", x1d.ap()[:, oc * 512:(oc + 1) * 512], x1f[:, :], "ld")
            wait("tensor", "ve", cnt["ve"])
            slot_mm = [cnt["mm"], cnt["mm"]]
            sc_done = [cnt["sc"], cnt["sc"]]
            ve_done = [cnt["ve"], cnt["ve"]]
            bkA = [pb[0], pb[2]]
            bkB = [pb[1], pb[5]]
            for oc in range(32):
                par = oc % 2
                sl = par * 4096
                wk = "w" if par == 0 else "x"
                pa, pbb = bkA[par], bkB[par]
                hb = hs0 if par == 0 else hs1
                wait("sync", "mm", slot_mm[par])
                wv = dma("sync", wst[:, sl:sl + 1024], w1T[oc, :, :], wk)
                wv = dma("sync", wst[:, sl + 1024:sl + 2048], w2T[oc, :, :], wk, serial=False)
                wait("tensor", wk, wv)
                wait("tensor", "ve", ve_done[par])
                wait("tensor", "sc", sc_done[par])
                for r in range(8):
                    op("tensor", lambda e, r=r, sl=sl, pa=pa: e.matmul(
                        pa[:, 0:512], wst[:, sl + r * 128:sl + (r + 1) * 128],
                        x1b[:, r * 512:(r + 1) * 512], start=(r == 0), stop=(r == 7)))
                for r in range(8):
                    mm = op("tensor", lambda e, r=r, sl=sl, pbb=pbb: e.matmul(
                        pbb[:, 0:512], wst[:, sl + 1024 + r * 128:sl + 1024 + (r + 1) * 128],
                        x1b[:, r * 512:(r + 1) * 512], start=(r == 0), stop=(r == 7)),
                        "mm" if r == 7 else None)
                slot_mm[par] = mm
                wait("scalar", "mm", mm)
                hv = op("scalar", lambda e, oc=oc, pa=pa, hb=hb: e.activation(hb[:, :], pa[:, 0:512], AF.Silu,
                                                                              scale=fsc[:, oc:oc + 1]), "sc")
                sc_done[par] = hv
                wait("vector", "sc", hv)
                ve_done[par] = op("vector", lambda e, oc=oc, pbb=pbb, hb=hb: e.scalar_tensor_tensor(
                    out=ht[:, oc * 512:(oc + 1) * 512], in0=pbb[:, 0:512],
                    scalar=fsc[:, 32 + oc:33 + oc], in1=hb[:, :], op0=OP.mult, op1=OP.mult), "ve")
            for stage, (wmat, scoff) in enumerate(((h1T, 64), (h2T, 96))):
                slot_mm = [cnt["mm"], cnt["mm"]]
                sc_done = [cnt["sc"], cnt["sc"]]
                gp_done = [cnt["gp"], cnt["gp"]]
                wait("tensor", "ve", cnt["ve"])
                for oc in range(32):
                    par = oc % 2
                    sl = par * 4096
                    wk = "w" if par == 0 else "x"
                    pa = pb[par]
                    hb = hs0 if par == 0 else hs1
                    wait("sync", "mm", slot_mm[par])
                    wv = None
                    for piece in range(2):
                        wv = dma("sync", wst[:, sl + piece * 2048:sl + (piece + 1) * 2048],
                                 wmat[oc, :, piece * 2048:(piece + 1) * 2048], wk, serial=(piece == 0))
                    wait("tensor", wk, wv)
                    wait("tensor", "sc", sc_done[par])
                    for r in range(32):
                        mm = op("tensor", lambda e, r=r, sl=sl, pa=pa: e.matmul(
                            pa[:, 0:512], wst[:, sl + r * 128:sl + (r + 1) * 128],
                            ht[:, r * 512:(r + 1) * 512], start=(r == 0), stop=(r == 31)),
                            "mm" if r == 31 else None)
                    slot_mm[par] = mm
                    wait("scalar", "mm", mm)
                    wait("scalar", "gp", gp_done[par])
                    hv = op("scalar", lambda e, oc=oc, scoff=scoff, hb=hb, pa=pa: e.activation(
                        hb[:, :], pa[:, 0:512], AF.Gelu, scale=fsc[:, scoff + oc:scoff + oc + 1]), "sc")
                    sc_done[par] = hv
                    wait("gpsimd", "sc", hv)
                    gp_done[par] = dma("gpsimd", hd.ap()[:, oc * 512:(oc + 1) * 512], hb[:, :], "gp")
                wait("scalar", "gp", cnt["gp"])
                wait("scalar", "mm")
                hv = 0
                for piece in range(8):
                    hv = dma("scalar", ht[:, piece * 2048:(piece + 1) * 2048],
                             hd.ap()[:, piece * 2048:(piece + 1) * 2048], "x", serial=(piece == 0))
                wait("tensor", "x", hv)
            slot_mm = [cnt["mm"], cnt["mm"]]
            ve_done = [cnt["ve"], cnt["ve"]]
            gp_last = cnt["gp"]
            for oc in range(8):
                par = oc % 2
                sl = par * 4096
                wk = "w" if par == 0 else "x"
                pa = pb[par]
                wait("sync", "mm", slot_mm[par])
                wv = None
                for piece in range(2):
                    wv = dma("sync", wst[:, sl + piece * 2048:sl + (piece + 1) * 2048],
                             w3T[oc, :, piece * 2048:(piece + 1) * 2048], wk, serial=(piece == 0))
                wait("tensor", wk, wv)
                if oc < 2:
                    wait("tensor", "sc")
                wait("tensor", "ve", ve_done[par])
                for r in range(32):
                    mm = op("tensor", lambda e, r=r, sl=sl, pa=pa: e.matmul(
                        pa[:, 0:512], wst[:, sl + r * 128:sl + (r + 1) * 128],
                        ht[:, r * 512:(r + 1) * 512], start=(r == 0), stop=(r == 31)),
                        "mm" if r == 31 else None)
                slot_mm[par] = mm
                wait("scalar", "ld")
                xv = dma("scalar", x1f[:, :], x1d.ap()[:, oc * 512:(oc + 1) * 512], "x")
                wait("vector", "mm", mm)
                wait("vector", "x", xv)
                wait("vector", "gp", gp_last)
                rv1 = op("vector", lambda e, oc=oc, pa=pa: e.scalar_tensor_tensor(
                    out=r2f[:, :], in0=pa[:, 0:512], scalar=fsc[:, 128 + oc:129 + oc],
                    in1=x1f[:, :], op0=OP.mult, op1=OP.add), "ve")
                ve_done[par] = rv1
                wait("scalar", "ve", rv1)
                sv = op("scalar", lambda e: e.activation(sqs[:, :], r2f[:, :], AF.Square), "sc")
                wait("tensor", "sc", sv)
                op("tensor", lambda e, oc=oc: e.matmul(pb[4][0:1, 0:512], onc[:, :], r2f[:, :],
                                                       start=(oc == 0), stop=(oc == 7)))
                mm = op("tensor", lambda e, oc=oc: e.matmul(pb[3][0:1, 0:512], onc[:, :], sqs[:, :],
                                                            start=(oc == 0), stop=(oc == 7)), "mm")
                wait("gpsimd", "mm", mm)
                gp_last = dma("gpsimd", r2d.ap()[:, oc * 512:(oc + 1) * 512], r2f[:, :], "gp")
            wait("vector", "mm", mm)
            op("vector", lambda e: e.tensor_copy(rowa[:, :], pb[4][0:1, 0:512]))
            rv2 = op("vector", lambda e: e.tensor_copy(rowb[:, :], pb[3][0:1, 0:512]), "ve")
            emit_ln_rows(rv2)
            wait("tensor", "ve")
            bc = op("tensor", lambda e: e.matmul(pb[0][:, 0:512], onr[:, :], rowa[:, :], start=True, stop=True))
            bc = op("tensor", lambda e: e.matmul(pb[1][:, 0:512], onr[:, :], rowb[:, :], start=True, stop=True), "mm")
            wait("vector", "mm", bc)
            wait("vector", "gp", cnt["gp"])
            wait("scalar", "gp")
            for oc in range(8):
                wait("scalar", "ve")
                xv = dma("scalar", r2f[:, :], r2d.ap()[:, oc * 512:(oc + 1) * 512], "x")
                wait("vector", "x", xv)
                wait("vector", "st")
                op("vector", lambda e: e.tensor_tensor(outf[:, :], r2f[:, :], pb[0][:, 0:512], OP.subtract))
                ov = op("vector", lambda e: e.tensor_tensor(outf[:, :], outf[:, :], pb[1][:, 0:512], OP.mult), "ve")
                wait("sync", "ve", ov)
                dma("sync", outp[oc, :, e_ * 512:(e_ + 1) * 512], outf[:, :], "st")

        def emit_ln_rows(rv2):
            wait("scalar", "ve", rv2)
            op("scalar", lambda e: e.mul(rowa[:, :], rowa[:, :], 1.0 / 1024.0))
            op("scalar", lambda e: e.mul(rowb[:, :], rowb[:, :], 1.0 / 1024.0))
            sv = op("scalar", lambda e: e.activation(sqs[0:1, 0:512], rowa[:, :], AF.Square), "sc")
            wait("vector", "sc", sv)
            vv = op("vector", lambda e: e.tensor_tensor(rowb[:, :], rowb[:, :], sqs[0:1, 0:512], OP.subtract), "ve")
            wait("scalar", "ve", vv)
            sv = op("scalar", lambda e: e.activation(rowb[:, :], rowb[:, :], AF.Sqrt, bias=epsc[0:1, 0:1]), "sc")
            wait("vector", "sc", sv)
            op("vector", lambda e: e.reciprocal(rowb[:, :], rowb[:, :]), "ve")

        ntt = _DBG_TILES if _DBG_TILES is not None else NTT
        for tt in range(ntt):
            if tt >= 2:
                wait("vector", "sd", cnt["sd"])
            for b in range(B):
                emit_proj(tt, b)
            nch = _DBG_CHUNKS if _DBG_CHUNKS is not None else NCH
            for c in range(nch):
                if _DBG_STAGE >= 1:
                    emit_prep_all(tt, c)
                if _DBG_STAGE >= 2:
                    emit_chain(tt, c)
                if _DBG_STAGE >= 3:
                    for b in range(B):
                        emit_post(tt, b, c)
            if tt % 2 == 1 and _DBG_P2:
                emit_exchange(tt // 2)
                emit_p2(tt // 2)

        if _DBG_DUMP:
            wait("vector", "mm")
            op("vector", lambda e: e.tensor_copy(p3dump[:, :], pb[3][:, 0:256]), "ve")
            wait("sync", "ve")
            wait("sync", "sc")
            dma("sync", dbgo[:, :], send[:, :], "st")
            dma("sync", dbgs[:, :], st_f[:, :], "st")
            dma("sync", dbgq[:, :], qT_[:, :], "st")
            dma("sync", dbgk[:, :], kT_[:, :], "st")
            dma("sync", dbgv[:, :], vT_[:, :], "st")
            dma("sync", dbgsp[:, :], spT_[:, :], "st")
            dma("sync", dbgln[:, :], lnT_[:, :], "st")
            dma("sync", dbgbe[:, :], beta_[:, :], "st")
            dma("sync", dbgu[:, 0:256], u0[:, :], "st")
            dma("sync", dbgu[:, 256:512], u1[:, :], "st")
            dma("sync", dbgos[:, 0:128], oS[:, :], "st")
            dma("sync", dbgos[:, 128:256], oS2[:, :], "st")
            dma("sync", dbgoc[:, :], ocdump[:, :], "st")
            dma("sync", dbggq[:, 0:128], gq01[:, :], "st")
            dma("sync", dbggq[:, 128:256], gbd0[:, :], "st")
            dma("sync", dbgp3[:, :], p3dump[:, :], "st")
            dma("sync", dbgmk[:, 0:32], mks[:, :], "st")
            dma("sync", dbgmk[:, 32:96], mg0[:, :], "st")
            dma("sync", dbge[:, 0:512], e1t[:, :], "st")
            dma("sync", dbge[:, 512:1024], e2t[:, :], "st")
        wait("sync", "st", cnt["st"])

        blk = es.enter_context(nc.Block())

        @blk.sync
        def _(e):
            for f in prog["sync"]:
                f(e)

        @blk.tensor
        def _(e):
            for f in prog["tensor"]:
                f(e)

        @blk.vector
        def _(e):
            for f in prog["vector"]:
                f(e)

        @blk.scalar
        def _(e):
            for f in prog["scalar"]:
                f(e)

        @blk.gpsimd
        def _(e):
            for f in prog["gpsimd"]:
                f(e)

    return nc



_NC_CACHE = {}


def _consts():
    identb = np.eye(128, dtype=ml_dtypes.bfloat16)
    identf = np.eye(128, dtype=np.float32)
    onecf = np.ones((128, 1), np.float32)
    onerf = np.ones((1, 128), np.float32)
    r = np.arange(32)
    i32 = np.arange(32)
    m = -(r[:, None] < i32[None, :]).astype(np.float32)
    mk32q = np.tile(m, (4, 1))
    i64 = np.arange(64)
    mq0 = np.tile(-(r[:, None] <= i64[None, :]).astype(np.float32), (4, 1))
    mq1 = np.tile(-((r[:, None] + 32) <= i64[None, :]).astype(np.float32), (4, 1))
    return identb, identf, onecf, onerf, mk32q, mq0, mq1


def kernel(**inp):
    x = np.asarray(inp["L0_attn_x"], np.float32)
    xfm = np.ascontiguousarray(x.transpose(0, 2, 1)).reshape(B, 8, 128, L)

    def colslice(W, h):
        return np.ascontiguousarray(W[:, h * 128:(h + 1) * 128].reshape(8, 128, 128))

    Wq, Wk, Wv, Wa, Wg = (np.asarray(inp[f"L0_attn_W{n}"], np.float32) for n in "qkvag")
    Wb = np.asarray(inp["L0_attn_Wb"], np.float32)
    Wo = np.asarray(inp["L0_attn_Wo"], np.float32)
    ba = np.asarray(inp["L0_attn_ba"], np.float32)
    bb = np.asarray(inp["L0_attn_bb"], np.float32)
    cqw = np.asarray(inp["L0_attn_cqw"], np.float32)
    ckw = np.asarray(inp["L0_attn_ckw"], np.float32)
    cvw = np.asarray(inp["L0_attn_cvw"], np.float32)
    cqb = np.asarray(inp["L0_attn_cqb"], np.float32)
    ckb = np.asarray(inp["L0_attn_ckb"], np.float32)
    cvb = np.asarray(inp["L0_attn_cvb"], np.float32)
    lngv = np.asarray(inp["L0_attn_ln_g"], np.float32)
    lnbv = np.asarray(inp["L0_attn_ln_b"], np.float32)

    woTh = np.ascontiguousarray(Wo.reshape(8, 128, 8, 128).transpose(2, 1, 0, 3)).reshape(8, 128, 8 * 128)

    def ffn_lhsT(W, no, nk):
        Wr = W.reshape(no, 128, nk, 128)
        return np.ascontiguousarray(Wr.transpose(0, 3, 2, 1)).reshape(no, 128, nk * 128)

    w1w = np.asarray(inp["L0_ffn_w1_w"], np.float32)
    w2w = np.asarray(inp["L0_ffn_w2_w"], np.float32)
    h1w = np.asarray(inp["L0_ffn_h1_w"], np.float32)
    h2w = np.asarray(inp["L0_ffn_h2_w"], np.float32)
    w3w = np.asarray(inp["L0_ffn_w3_w"], np.float32)
    w1Tb = ffn_lhsT(w1w, 32, 8).astype(ml_dtypes.bfloat16)
    w2Tb = ffn_lhsT(w2w, 32, 8).astype(ml_dtypes.bfloat16)
    h1Tb = ffn_lhsT(h1w, 32, 32).astype(ml_dtypes.bfloat16)
    h2Tb = ffn_lhsT(h2w, 32, 32).astype(ml_dtypes.bfloat16)
    w3Tb = ffn_lhsT(w3w, 8, 32).astype(ml_dtypes.bfloat16)
    fsclv = np.zeros((128, 5 * 32), np.float32)
    for fi, sname in enumerate(("w1_s", "w2_s", "h1_s", "h2_s", "w3_s")):
        sv = np.asarray(inp["L0_ffn_" + sname], np.float32)
        nch = sv.shape[0] // 128
        fsclv[:, fi * 32:fi * 32 + nch] = sv.reshape(nch, 128).T

    identb, identf, onecf, onerf, mk32q, mq0, mq1 = _consts()

    if "nc" not in _NC_CACHE:
        _NC_CACHE["nc"] = build_nc()
    ncp = _NC_CACHE["nc"]

    in_maps = []
    for s in range(8):
        h = s
        mb, half = s // 2, s % 2
        xres = np.empty((8, 128, NRND * 512), np.float32)
        for e in range(NRND):
            xres[:, :, e * 512:(e + 1) * 512] = xfm[mb, :, :, e * 1024 + half * 512: e * 1024 + half * 512 + 512]
        w5v = np.concatenate([colslice(W, h) for W in (Wq, Wk, Wv, Wa, Wg)], 0)
        w5v = np.ascontiguousarray(w5v.transpose(1, 0, 2)).reshape(128, 40 * 128)
        convwv = np.stack([cqw[h * 128:(h + 1) * 128], ckw[h * 128:(h + 1) * 128],
                           cvw[h * 128:(h + 1) * 128]], 0)
        convwv = np.ascontiguousarray(convwv.transpose(1, 0, 2)).reshape(128, 12)
        convbv = np.stack([cqb[h * 128:(h + 1) * 128], ckb[h * 128:(h + 1) * 128],
                           cvb[h * 128:(h + 1) * 128]], 1)
        in_maps.append({
            "xfm": xfm, "xre": xres, "w5": w5v,
            "wbcol": np.ascontiguousarray(Wb[:, h].reshape(8, 128).T),
            "convw": convwv, "convb": np.ascontiguousarray(convbv),
            "bneg": ba[h * 128:(h + 1) * 128].reshape(128, 1).copy(),
            "bbv": bb[h].reshape(1, 1),
            "lngb": np.stack([lngv, lnbv], 1),
            "woT": woTh, "w1T": w1Tb, "w2T": w2Tb, "h1T": h1Tb, "h2T": h2Tb, "w3T": w3Tb,
            "fscl": fsclv, "identb": identb, "identf": identf,
            "onecf": onecf, "onerf": onerf, "mk32q": mk32q, "mq0": mq0, "mq1": mq1,
        })

    res = run_bass_kernel_spmd(ncp, in_maps, core_ids=list(range(8)))
    out = np.empty((B, L, D), np.float32)
    for s in range(8):
        mb, half = s // 2, s % 2
        o = res.results[s]["outp"]
        for e in range(NRND):
            seg = o[:, :, e * 512:(e + 1) * 512]
            t0 = e * 1024 + half * 512
            out[mb, t0:t0 + 512, :] = seg.transpose(2, 0, 1).reshape(512, D)
    return out


# version 10
# speedup vs baseline: 1.2714x; 1.0088x over previous
import contextlib
import math
import numpy as np
import ml_dtypes
import concourse.bass as bass
import concourse.mybir as mybir
from concourse.bass_utils import run_bass_kernel_spmd
from concourse.aot_env import allow_ici_all_to_all


def _cc_alltoall(e, iap, oap):
    with allow_ici_all_to_all():
        return e.collective_compute(
            "AllToAll", mybir.AluOpType.bypass, replica_groups=[list(range(8))],
            ins=[iap], outs=[oap])

F32 = mybir.dt.float32
BF16 = mybir.dt.bfloat16
AF = mybir.ActivationFunctionType
OP = mybir.AluOpType

B, L, D = 4, 4096, 1024
TT, CH = 512, 64
NTT, NCH = L // TT, TT // CH
NRND = 4
DKI = 1.0 / math.sqrt(128.0)
NIT = 31
_DBG_TILES = None
_DBG_P2 = True
_DBG_DUMP = False
_DBG_CHUNKS = None
_DBG_STAGE = 3
_DBG_PREP = 99
_DBG_BSET = (0, 1, 2, 3)


def build_nc():
    nc = bass.Bass("TRN2", target_bir_lowering=False)

    xfm = nc.dram_tensor("xfm", [B, 8, 128, L], F32, kind="ExternalInput").ap()
    xre = nc.dram_tensor("xre", [8, 128, NRND * 512], F32, kind="ExternalInput").ap()
    w5 = nc.dram_tensor("w5", [128, 40 * 128], F32, kind="ExternalInput").ap()
    wbcol = nc.dram_tensor("wbcol", [128, 8], F32, kind="ExternalInput").ap()
    convw = nc.dram_tensor("convw", [128, 12], F32, kind="ExternalInput").ap()
    convb = nc.dram_tensor("convb", [128, 3], F32, kind="ExternalInput").ap()
    bneg = nc.dram_tensor("bneg", [128, 1], F32, kind="ExternalInput").ap()
    bbv = nc.dram_tensor("bbv", [1, 1], F32, kind="ExternalInput").ap()
    lngb = nc.dram_tensor("lngb", [128, 2], F32, kind="ExternalInput").ap()
    woT = nc.dram_tensor("woT", [8, 128, 8 * 128], F32, kind="ExternalInput").ap()
    w1T = nc.dram_tensor("w1T", [32, 128, 8 * 128], BF16, kind="ExternalInput").ap()
    w2T = nc.dram_tensor("w2T", [32, 128, 8 * 128], BF16, kind="ExternalInput").ap()
    h1T = nc.dram_tensor("h1T", [32, 128, 32 * 128], BF16, kind="ExternalInput").ap()
    h2T = nc.dram_tensor("h2T", [32, 128, 32 * 128], BF16, kind="ExternalInput").ap()
    w3T = nc.dram_tensor("w3T", [8, 128, 32 * 128], BF16, kind="ExternalInput").ap()
    fscl = nc.dram_tensor("fscl", [128, 5 * 32], F32, kind="ExternalInput").ap()
    identb = nc.dram_tensor("identb", [128, 128], BF16, kind="ExternalInput").ap()
    identf = nc.dram_tensor("identf", [128, 128], F32, kind="ExternalInput").ap()
    onecf = nc.dram_tensor("onecf", [128, 1], F32, kind="ExternalInput").ap()
    onerf = nc.dram_tensor("onerf", [1, 128], F32, kind="ExternalInput").ap()
    mk32q = nc.dram_tensor("mk32q", [128, 32], F32, kind="ExternalInput").ap()
    mq0 = nc.dram_tensor("mq0", [128, 64], F32, kind="ExternalInput").ap()
    mq1 = nc.dram_tensor("mq1", [128, 64], F32, kind="ExternalInput").ap()
    ibs = [nc.dram_tensor(f"ib{e}", [8, 128, 512], BF16) for e in range(NRND)]
    obs = [nc.dram_tensor(f"ob{e}", [8, 128, 512], BF16) for e in range(NRND)]
    x1d = nc.dram_tensor("x1d", [128, 8 * 512], F32)
    r2d = nc.dram_tensor("r2d", [128, 8 * 512], F32)
    hd = nc.dram_tensor("hd", [128, 32 * 512], BF16)
    outp = nc.dram_tensor("outp", [8, 128, NRND * 512], F32, kind="ExternalOutput").ap()
    if _DBG_DUMP:
        dbgo = nc.dram_tensor("dbgo", [128, 8 * 512], BF16, kind="ExternalOutput").ap()
        dbgs = nc.dram_tensor("dbgs", [128, B * 128], F32, kind="ExternalOutput").ap()
        dbgq = nc.dram_tensor("dbgq", [128, B * 512], BF16, kind="ExternalOutput").ap()
        dbgk = nc.dram_tensor("dbgk", [128, B * 512], BF16, kind="ExternalOutput").ap()
        dbgv = nc.dram_tensor("dbgv", [128, B * 512], BF16, kind="ExternalOutput").ap()
        dbgsp = nc.dram_tensor("dbgsp", [128, B * 512], F32, kind="ExternalOutput").ap()
        dbgln = nc.dram_tensor("dbgln", [128, B * 512], F32, kind="ExternalOutput").ap()
        dbgbe = nc.dram_tensor("dbgbe", [1, B * 512], F32, kind="ExternalOutput").ap()
        dbgu = nc.dram_tensor("dbgu", [128, 512], BF16, kind="ExternalOutput").ap()
        dbgos = nc.dram_tensor("dbgos", [64, 256], F32, kind="ExternalOutput").ap()
        dbgoc = nc.dram_tensor("dbgoc", [64, 256], F32, kind="ExternalOutput").ap()
        dbggq = nc.dram_tensor("dbggq", [128, 256], BF16, kind="ExternalOutput").ap()
        dbgmk = nc.dram_tensor("dbgmk", [128, 128], F32, kind="ExternalOutput").ap()
        dbgp3 = nc.dram_tensor("dbgp3", [128, 256], F32, kind="ExternalOutput").ap()
        dbge = nc.dram_tensor("dbge", [64, 8 * 128], F32, kind="ExternalOutput").ap()

    es = contextlib.ExitStack()
    with es:
        T = lambda name, shape, dt=F32: es.enter_context(nc.sbuf_tensor(name, shape, dt))
        wq = T("wq", [128, 40 * 128 + 8], BF16)
        cdiag = T("cdiag", [128, 12 * 128], BF16)
        cbias = T("cbias", [128, 3], F32)
        bnegc = T("bnegc", [128, 1], F32)
        bbc = T("bbc", [1, 1], F32)
        lng = T("lng", [128, 2], F32)
        idb = T("idb", [128, 128], BF16)
        idf = T("idf", [128, 128], F32)
        onc = T("onc", [128, 1], F32)
        onr = T("onr", [1, 128], F32)
        mks = T("mks", [128, 32], F32)
        mg0 = T("mg0", [128, 64], F32)
        mg1 = T("mg1", [128, 64], F32)
        fsc = T("fsc", [128, 5 * 32], F32)
        xf = T("xf", [128, 8 * TT], F32)
        xb = T("xb", [128, 8 * TT], BF16)
        qT_ = T("qT_", [128, B * TT], BF16)
        kT_ = T("kT_", [128, B * TT], BF16)
        vT_ = T("vT_", [128, B * TT], BF16)
        gT_ = T("gT_", [128, B * TT], BF16)
        spT_ = T("spT_", [128, B * TT], F32)
        lnT_ = T("lnT_", [128, B * TT], F32)
        beta_ = T("beta_", [1, B * TT], F32)
        pre = T("pre", [128, TT + 3], F32)
        preb = T("preb", [128, TT + 3], BF16)
        halo = T("halo", [128, 9 * B * 2], F32)
        st_f = T("st_f", [128, B * 128], F32)
        st_b = T("st_b", [128, B * 128], BF16)
        sbf = T("sbf", [128, 128], BF16)
        kct = T("kct", [128, 256], BF16)
        gq01 = T("gq01", [128, 128], BF16)
        gbd0 = T("gbd0", [128, 128], BF16)
        gbd1 = T("gbd1", [128, 128], BF16)
        gbdx = T("gbdx", [128, 128], BF16)
        efm = T("efm", [128, 4 * 192], F32)
        rhsp = T("rhsp", [128, 256], F32)
        e1t = T("e1t", [64, B * 128], F32)
        e2t = T("e2t", [64, B * 128], F32)
        cp0 = T("cp0", [128, 256], F32)
        cp1 = T("cp1", [128, 256], F32)
        invs = T("invs", [128, 256], F32)
        bstk = T("bstk", [128, 2], F32)
        bstn = T("bstn", [128, 2], F32)
        rstk = T("rstk", [128, 512], BF16)
        delta3 = [T(f"delta{i}", [128, 256], BF16) for i in range(2)]
        u0 = T("u0", [128, 256], BF16)
        u1 = T("u1", [128, 256], BF16)
        oS = T("oS", [64, 128], F32)
        oS2 = T("oS2", [64, 128], F32)
        ocol = T("ocol", [64, 2], F32)
        mcol = T("mcol", [128, 8], F32)
        ecolP = T("ecolP", [128, B], F32)
        ecolN = T("ecolN", [128, 2 * B], F32)
        ognm = T("ognm", [128, 64], F32)
        send = T("send", [128, 8 * 512], BF16)
        recv = T("recv", [128, 8 * 512], BF16)
        x1f = T("x1f", [128, 512], F32)
        x1b = T("x1b", [128, 8 * 512], BF16)
        r2f = T("r2f", [128, 512], F32)
        sqs = T("sqs", [128, 512], F32)
        rowa = T("rowa", [1, 512], F32)
        rowb = T("rowb", [1, 512], F32)
        hs0 = T("hs0", [128, 512], BF16)
        hs1 = T("hs1", [128, 512], BF16)
        ht = T("ht", [128, 32 * 512], BF16)
        wst = T("wst", [128, 2 * 4096], BF16)
        outf = T("outf", [128, 512], F32)
        sscr_state = T("sscr_state", [128, 128], F32)
        epsc = T("epsc", [128, 1], F32)
        ocdump = T("ocdump", [64, 256], F32)
        p3dump = T("p3dump", [128, 256], F32)

        pb = [es.enter_context(nc.psum_tensor(f"pb{i}", [128, 512], F32)) for i in range(6)]
        pbar = es.enter_context(nc.psum_tensor("pbar", [128, 512], F32))
        ptb = es.enter_context(nc.psum_tensor("ptb", [128, 512], BF16))

        sems = {}
        for k in ("ld", "x", "w", "mm", "ve", "sc", "cc", "sd", "rv", "st", "gp"):
            sems[k] = es.enter_context(nc.semaphore("s_" + k))
        cnt = {k: 0 for k in sems}
        prog = {k: [] for k in ("sync", "tensor", "vector", "scalar", "gpsimd")}

        def emit(eng, fn):
            prog[eng].append(fn)

        def dma(eng, out, in_, key, serial=True):
            if serial:
                wait(eng, key, cnt[key])
            cnt[key] += 16
            emit(eng, lambda e, out=out, in_=in_, key=key: e.dma_start(out=out, in_=in_).then_inc(sems[key], 16))
            return cnt[key]

        def wait(eng, key, val=None):
            if val is None:
                val = cnt[key]
            emit(eng, lambda e, key=key, val=val: e.wait_ge(sems[key], val))

        def op(eng, fn, key=None, nodrain=False):
            if eng in ("vector", "scalar", "gpsimd") and not nodrain:
                emit(eng, lambda e: e.drain())
            if key:
                cnt[key] += 1
                emit(eng, lambda e, fn=fn, key=key: fn(e).then_inc(sems[key], 1))
                return cnt[key]
            emit(eng, lambda e, fn=fn: fn(e))
            return None

        TP = [(0, 0), (0, 32), (0, 64), (0, 96)]

        def bar_mm():
            return op("tensor", lambda e: e.matmul(pbar[:, 0:4], idb[:, :], idb[:, 0:4],
                                                   start=True, stop=True, skip_group_check=True), "mm")

        dma("sync", idb[:, :], identb[:, :], "ld")
        dma("sync", idf[:, :], identf[:, :], "ld")
        dma("sync", onc[:, :], onecf[:, :], "ld")
        dma("sync", onr[:, :], onerf[:, :], "ld")
        dma("sync", mks[:, :], mk32q[:, :], "ld")
        dma("sync", mg0[:, :], mq0[:, :], "ld")
        dma("sync", mg1[:, :], mq1[:, :], "ld")
        dma("sync", bnegc[:, :], bneg[:, :], "ld")
        dma("sync", bbc[:, :], bbv[:, :], "ld")
        dma("sync", lng[:, :], lngb[:, :], "ld")
        dma("sync", cbias[:, :], convb[:, :], "ld")
        dma("sync", fsc[:, :], fscl[:, :], "ld")
        for g in range(5):
            wait("sync", "ve")
            v = dma("sync", xf[:, 0:8 * 128], w5[:, g * 1024:(g + 1) * 1024], "x")
            wait("vector", "x", v)
            op("vector", lambda e, g=g: e.tensor_copy(wq[:, g * 1024:(g + 1) * 1024], xf[:, 0:1024]), "ve")
        wait("sync", "ve")
        v = dma("sync", xf[:, 0:8], wbcol[:, :], "x")
        wait("vector", "x", v)
        op("vector", lambda e: e.tensor_copy(wq[:, 5120:5128], xf[:, 0:8]), "ve")
        v = dma("sync", xf[:, 8:20], convw[:, :], "x")
        wait("vector", "x", v)
        for t in range(12):
            op("vector", lambda e, t=t: e.tensor_scalar(
                cdiag[:, t * 128:(t + 1) * 128], idb[:, :], xf[:, 8 + t:9 + t], 0.0, OP.mult, OP.add))
        op("vector", lambda e: e.memset(epsc[:, :], 1e-5))
        op("vector", lambda e: e.memset(send[:, :], 0.0))
        op("vector", lambda e: e.memset(recv[:, :], 0.0))
        op("vector", lambda e: e.memset(halo[:, :], 0.0))
        op("vector", lambda e: e.memset(st_f[:, :], 0.0))
        op("vector", lambda e: e.memset(st_b[:, :], 0.0))
        op("vector", lambda e: e.memset(gbd0[:, :], 0.0))
        op("vector", lambda e: e.memset(gbd1[:, :], 0.0))
        op("vector", lambda e: e.memset(gbdx[:, :], 0.0))
        op("vector", lambda e: e.memset(gq01[:, :], 0.0))
        op("vector", lambda e: e.memset(kct[:, :], 0.0))
        op("vector", lambda e: e.memset(rstk[:, :], 0.0))
        op("vector", lambda e: e.memset(invs[:, :], 1.0))
        op("vector", lambda e: e.memset(bstk[:, :], 0.0))
        op("vector", lambda e: e.memset(bstn[:, :], 0.0), "ve")
        wait("tensor", "ve")
        wait("scalar", "ve")
        xff_ve = [cnt["ve"]]
        xff_sc = [cnt["sc"]]
        post_sc = [cnt["sc"]]

        q_t = lambda b: qT_[:, b * TT:(b + 1) * TT]
        k_t = lambda b: kT_[:, b * TT:(b + 1) * TT]
        v_t = lambda b: vT_[:, b * TT:(b + 1) * TT]
        g_t = lambda b: gT_[:, b * TT:(b + 1) * TT]
        sp_t = lambda b: spT_[:, b * TT:(b + 1) * TT]
        ln_t = lambda b: lnT_[:, b * TT:(b + 1) * TT]
        be_t = lambda b: beta_[:, b * TT:(b + 1) * TT]
        st_fb = lambda b: st_f[:, b * 128:(b + 1) * 128]
        st_bb = lambda b: st_b[:, b * 128:(b + 1) * 128]

        def emit_proj(tt, b):
            t0 = tt * TT
            wait("sync", "ve", xff_ve[0])
            wait("sync", "sc", xff_sc[0])
            xv = 0
            for c in range(8):
                xv = dma("sync", xf[:, c * TT:(c + 1) * TT], xfm[b, c, :, t0:t0 + TT], "x", serial=(c == 0))
            wait("vector", "x", xv)
            wait("vector", "mm")
            for c in range(8):
                vx = op("vector", lambda e, c=c: e.tensor_copy(xb[:, c * TT:(c + 1) * TT], xf[:, c * TT:(c + 1) * TT]),
                        "ve" if c == 7 else None)
            xff_ve[0] = vx
            wait("tensor", "ve", vx)
            for ti in range(3):
                for c in range(8):
                    mm = op("tensor", lambda e, c=c, ti=ti: e.matmul(
                        pb[0][:, 0:TT], wq[:, (ti * 8 + c) * 128:(ti * 8 + c + 1) * 128],
                        xb[:, c * TT:(c + 1) * TT], start=(c == 0), stop=(c == 7)),
                        "mm" if c == 7 else None)
                wait("scalar", "mm", mm)
                wait("scalar", "ve")
                sc = op("scalar", lambda e: e.copy(pre[:, 3:3 + TT], pb[0][:, 0:TT]), "sc")
                wait("vector", "sc", sc)
                hi = (ti * B + b) * 3 + (tt % 2) * 36
                ho = (ti * B + b) * 3 + ((tt + 1) % 2) * 36
                op("vector", lambda e, hi=hi: e.tensor_copy(pre[:, 0:3], halo[:, hi:hi + 3]))
                op("vector", lambda e, ho=ho: e.tensor_copy(halo[:, ho:ho + 3], pre[:, TT:TT + 3]))
                cv = op("vector", lambda e: e.tensor_copy(preb[:, :], pre[:, :]), "ve")
                wait("tensor", "ve", cv)
                for j in range(4):
                    mm = op("tensor", lambda e, ti=ti, j=j: e.matmul(
                        pb[1][:, 0:TT], cdiag[:, (ti * 4 + j) * 128:(ti * 4 + j + 1) * 128],
                        preb[:, j:j + TT], start=(j == 0), stop=(j == 3)),
                        "mm" if j == 3 else None)
                wait("scalar", "mm", mm)
                dstt = (q_t(b), k_t(b), v_t(b))[ti]
                scv = op("scalar", lambda e, ti=ti, dstt=dstt: e.activation(
                    dstt, pb[1][:, 0:TT], AF.Silu, bias=cbias[:, ti:ti + 1], scale=1.0), "sc")
                if ti == 1:
                    wait("vector", "sc", scv)
                    op("vector", lambda e, dstt=dstt: e.tensor_scalar(dstt, dstt, DKI, 0.0, OP.mult, OP.add), "ve")
            for c in range(8):
                mm = op("tensor", lambda e, c=c: e.matmul(
                    pb[0][:, 0:TT], wq[:, (24 + c) * 128:(24 + c + 1) * 128],
                    xb[:, c * TT:(c + 1) * TT], start=(c == 0), stop=(c == 7)),
                    "mm" if c == 7 else None)
            wait("scalar", "mm", mm)
            op("scalar", lambda e: e.activation(xf[:, 0:TT], pb[0][:, 0:TT], AF.Sigmoid,
                                                bias=bnegc[:, 0:1], scale=1.0))
            xff_sc[0] = op("scalar", lambda e, b=b: e.activation(sp_t(b), xf[:, 0:TT], AF.Ln), "sc")
            wait("tensor", "sc")
            for c in range(8):
                mm = op("tensor", lambda e, c=c: e.matmul(
                    pb[1][:, 0:TT], wq[:, (32 + c) * 128:(32 + c + 1) * 128],
                    xb[:, c * TT:(c + 1) * TT], start=(c == 0), stop=(c == 7)),
                    "mm" if c == 7 else None)
            wait("scalar", "mm", mm)
            op("scalar", lambda e, b=b: e.activation(g_t(b), pb[1][:, 0:TT], AF.Sigmoid), "sc")
            wait("tensor", "sc")
            for c in range(8):
                mm = op("tensor", lambda e, c=c: e.matmul(
                    pb[0][0:1, 0:TT], wq[:, 5120 + c:5121 + c],
                    xb[:, c * TT:(c + 1) * TT], start=(c == 0), stop=(c == 7)),
                    "mm" if c == 7 else None)
            wait("scalar", "mm", mm)
            scb = op("scalar", lambda e, b=b: e.activation(be_t(b), pb[0][0:1, 0:TT], AF.Sigmoid,
                                                           bias=bbc[0:1, 0:1], scale=1.0), "sc")
            wait("vector", "sc", scb)
            for c in range(NCH):
                op("vector", lambda e, b=b, c=c: e.tensor_tensor_scan(
                    out=ln_t(b)[:, c * CH:(c + 1) * CH], data0=sp_t(b)[:, c * CH:(c + 1) * CH],
                    data1=sp_t(b)[:, c * CH:(c + 1) * CH], initial=0.0, op0=OP.add, op1=OP.bypass),
                   "ve" if c == NCH - 1 else None)

        def emit_prep_all(tt, c):
            c0 = c * CH
            lncs = [ln_t(b)[:, c0:c0 + CH] for b in range(B)]
            wait("vector", "sc", cnt["sc"])
            for b in range(B):
                op("vector", lambda e, b=b: e.tensor_scalar(mcol[:, 2 * b:2 * b + 1], lncs[b][:, CH - 1:CH], 0.5, 0.0, OP.mult, OP.add))
                mv = op("vector", lambda e, b=b: e.tensor_scalar(mcol[:, 2 * b + 1:2 * b + 2], lncs[b][:, CH - 1:CH], -0.5, 0.0, OP.mult, OP.add),
                        "ve" if b == B - 1 else None)
            wait("scalar", "ve", mv)
            wait("scalar", "mm")
            for b in range(B):
                eo = 192 * b
                op("scalar", lambda e, b=b, eo=eo: e.activation(efm[:, eo:eo + 64], lncs[b], AF.Exp, scale=1.0))
                e2v = op("scalar", lambda e, b=b, eo=eo: e.activation(efm[:, eo + 64:eo + 128], lncs[b], AF.Exp, bias=mcol[:, 2 * b + 1:2 * b + 2], scale=1.0),
                         "sc" if b == B - 1 else None)
            wait("vector", "sc", e2v)
            for b in range(B):
                tsub = op("vector", lambda e, b=b, c0=c0: e.tensor_tensor(
                    rhsp[:, 64 * b:64 * b + 64], lncs[b], sp_t(b)[:, c0:c0 + CH], OP.subtract), "ve" if b == B - 1 else None)
            wait("scalar", "ve", tsub)
            for b in range(B):
                eo = 192 * b
                e3 = op("scalar", lambda e, b=b, eo=eo: e.activation(efm[:, eo + 128:eo + 192], rhsp[:, 64 * b:64 * b + 64], AF.Exp, bias=mcol[:, 2 * b:2 * b + 1], scale=-1.0),
                        "sc" if b == B - 1 else None)
            wait("vector", "sc", e3)
            for b in range(B):
                eo, ro = 192 * b, 64 * b
                op("vector", lambda e, b=b, c0=c0, eo=eo, ro=ro: e.tensor_tensor(rhsp[:, ro:ro + 64], efm[:, eo + 128:eo + 192], v_t(b)[:, c0:c0 + CH], OP.mult))
                op("vector", lambda e, ro=ro: e.tensor_scalar(rhsp[:, ro:ro + 64], rhsp[:, ro:ro + 64], -1.0, 0.0, OP.mult, OP.add))
                op("vector", lambda e, b=b, eo=eo: e.tensor_copy(ecolP[:, b:b + 1], efm[:, eo + 63:eo + 64]))
                op("vector", lambda e, b=b, eo=eo: e.tensor_scalar(ecolN[:, 2 * b:2 * b + 1], efm[:, eo + 63:eo + 64], -1.0, 0.0, OP.mult, OP.add))
                rr = op("vector", lambda e, b=b, eo=eo: e.tensor_scalar(ecolN[:, 2 * b + 1:2 * b + 2], efm[:, eo + 127:eo + 128], -1.0, 0.0, OP.mult, OP.add),
                        "ve" if b == B - 1 else None)
            wait("tensor", "ve", rr)
            for b in range(B):
                tp = TP[b]
                r0, r1 = 32 * b, 32 * b + 32
                Kc = k_t(b)[:, c0:c0 + CH]
                Qc = q_t(b)[:, c0:c0 + CH]
                eo, ro = 192 * b, 64 * b
                pbt = pb[b % 2]
                op("tensor", lambda e, pbt=pbt, r0=r0, r1=r1, b=b, c0=c0, tp=tp: e.matmul(
                    pbt[r0:r1, 0:128], sp_t(b)[:, c0:c0 + 32], idf[:, :], start=True, stop=True, tile_position=tp, skip_group_check=True))
                op("tensor", lambda e, pbt=pbt, r0=r0, r1=r1, b=b, c0=c0, tp=tp: e.matmul(
                    pbt[r0:r1, 128:256], sp_t(b)[:, c0 + 32:c0 + 64], idf[:, :], start=True, stop=True, tile_position=tp, skip_group_check=True))
                op("tensor", lambda e, pbt=pbt, r0=r0, r1=r1, ro=ro, tp=tp: e.matmul(
                    pbt[r0:r1, 256:384], rhsp[:, ro:ro + 32], idf[:, :], start=True, stop=True, tile_position=tp, skip_group_check=True))
                op("tensor", lambda e, pbt=pbt, r0=r0, r1=r1, ro=ro, tp=tp: e.matmul(
                    pbt[r0:r1, 384:512], rhsp[:, ro + 32:ro + 64], idf[:, :], start=True, stop=True, tile_position=tp, skip_group_check=True))
                op("tensor", lambda e, b=b, c0=c0, tp=tp: e.matmul(
                    pb[4][32 * b:32 * b + 32, 0:1], be_t(b)[0:1, c0:c0 + 32], onc[0:1, 0:1], start=True, stop=True, tile_position=tp, skip_group_check=True))
                op("tensor", lambda e, b=b, c0=c0, tp=tp: e.matmul(
                    pb[4][32 * b:32 * b + 32, 1:2], be_t(b)[0:1, c0 + 32:c0 + 64], onc[0:1, 0:1], start=True, stop=True, tile_position=tp, skip_group_check=True))
                op("tensor", lambda e, b=b, eo=eo: e.matmul(pb[2][0:64, 128 * b:128 * b + 128], efm[:, eo:eo + 64], idf[:, :], start=True, stop=True, skip_group_check=True))
                op("tensor", lambda e, b=b, eo=eo: e.matmul(pb[5][0:64, 128 * b:128 * b + 128], efm[:, eo + 64:eo + 128], idf[:, :], start=True, stop=True, skip_group_check=True))
                op("tensor", lambda e, Kc=Kc, b=b: e.matmul(
                    pb[3][32 * b:32 * b + 32, 0:64], Kc[:, 0:32], Kc[:, :], start=True, stop=True, tile_position=(0, 32 * b), skip_group_check=True))
                op("tensor", lambda e, Kc=Kc, b=b: e.matmul(
                    pb[3][32 * b:32 * b + 32, 64:96], Kc[:, 32:64], Kc[:, 32:64], start=True, stop=True, tile_position=(0, 32 * b), skip_group_check=True))
                op("tensor", lambda e, Kc=Kc, Qc=Qc, b=b: e.matmul(
                    pb[3][32 * b:32 * b + 32, 128:192], Kc[:, 0:32], Qc[:, :], start=True, stop=True, tile_position=(0, 32 * b), skip_group_check=True))
                op("tensor", lambda e, Kc=Kc, Qc=Qc, b=b: e.matmul(
                    pb[3][32 * b:32 * b + 32, 192:256], Kc[:, 32:64], Qc[:, :], start=True, stop=True, tile_position=(0, 32 * b), skip_group_check=True))
                op("tensor", lambda e, Kc=Kc, b=b, tp=tp: e.matmul(
                    pb[4][32 * b:32 * b + 32, 128:256], Kc[:, 0:32], idb[:, :], start=True, stop=True, tile_position=tp, skip_group_check=True))
                op("tensor", lambda e, Kc=Kc, b=b, tp=tp: e.matmul(
                    pb[4][32 * b:32 * b + 32, 256:384], Kc[:, 32:64], idb[:, :], start=True, stop=True, tile_position=tp, skip_group_check=True))
            tv = bar_mm()
            wait("scalar", "mm", tv)
            for b in range(B):
                r0, r1 = 32 * b, 32 * b + 32
                pbt = pb[b % 2]
                op("scalar", lambda e, pbt=pbt, r0=r0, r1=r1: e.activation(invs[r0:r1, 0:128], pbt[r0:r1, 0:128], AF.Exp, scale=-1.0))
                op("scalar", lambda e, pbt=pbt, r0=r0, r1=r1: e.activation(invs[r0:r1, 128:256], pbt[r0:r1, 128:256], AF.Exp, scale=-1.0))
                op("scalar", lambda e, b=b: e.copy(kct[32 * b:32 * b + 32, :], pb[4][32 * b:32 * b + 32, 128:384]))
                op("scalar", lambda e, r0=r0, r1=r1: e.copy(bstk[r0:r1, 0:2], pb[4][r0:r1, 0:2]))
                op("scalar", lambda e, r0=r0, r1=r1: e.mul(bstn[r0:r1, 0:2], pb[4][r0:r1, 0:2], -1.0))
                op("scalar", lambda e, pbt=pbt, r0=r0, r1=r1: e.copy(rstk[r0:r1, 128:256], pbt[r0:r1, 256:384]))
                sv = op("scalar", lambda e, pbt=pbt, r0=r0, r1=r1: e.copy(rstk[r0:r1, 384:512], pbt[r0:r1, 384:512]),
                        "sc" if b == B - 1 else None)
            wait("vector", "mm", tv)
            for b in range(B):
                r0, r1 = 32 * b, 32 * b + 32
                op("vector", lambda e, b=b: e.tensor_copy(e1t[:, b * 128:(b + 1) * 128], pb[2][0:64, 128 * b:128 * b + 128]))
                op("vector", lambda e, b=b: e.tensor_copy(e2t[:, b * 128:(b + 1) * 128], pb[5][0:64, 128 * b:128 * b + 128]))
                op("vector", lambda e, r0=r0, r1=r1: e.tensor_tensor(gbd0[r0:r1, r0:r1], pb[3][r0:r1, 0:32], mks[r0:r1, :], OP.mult))
                op("vector", lambda e, r0=r0, r1=r1: e.tensor_scalar(gbdx[r0:r1, r0:r1], pb[3][r0:r1, 32:64], -1.0, 0.0, OP.mult, OP.add))
                op("vector", lambda e, r0=r0, r1=r1: e.tensor_tensor(gbd1[r0:r1, r0:r1], pb[3][r0:r1, 64:96], mks[r0:r1, :], OP.mult))
                op("vector", lambda e, r0=r0, r1=r1: e.tensor_tensor(gq01[r0:r1, 0:64], pb[3][r0:r1, 128:192], mg0[r0:r1, :], OP.mult))
                vv = op("vector", lambda e, r0=r0, r1=r1: e.tensor_tensor(gq01[r0:r1, 64:128], pb[3][r0:r1, 192:256], mg1[r0:r1, :], OP.mult),
                        "ve" if b == B - 1 else None)
            wait("tensor", "ve", vv)
            wait("tensor", "sc", sv)

        def emit_chain(tt, c):
            c0 = c * CH
            wait("vector", "sc")
            op("vector", lambda e: e.tensor_scalar(cp0[:, 0:128], invs[:, 0:128], bstk[:, 0:1], 0.0, OP.mult, OP.add))
            op("vector", lambda e: e.tensor_copy(cp0[:, 128:256], cp0[:, 0:128]))
            op("vector", lambda e: e.tensor_scalar(cp1[:, 0:128], invs[:, 128:256], bstk[:, 1:2], 0.0, OP.mult, OP.add))
            cpv = op("vector", lambda e: e.tensor_copy(cp1[:, 128:256], cp1[:, 0:128]), "ve")
            wait("tensor", "ve", cpv)
            op("tensor", lambda e: e.matmul(pb[2][:, 0:256], idb[:, :], rstk[:, 0:256], start=True, stop=True))
            for b in range(B):
                op("tensor", lambda e, b=b, c0=c0: e.matmul(
                    pb[2][32 * b:32 * b + 32, 0:128], k_t(b)[:, c0:c0 + 32], st_bb(b),
                    start=False, stop=False, tile_position=(0, 32 * b),
                    skip_group_check=True))
            mmv = bar_mm()
            wait("vector", "mm", mmv)
            dv = op("vector", lambda e: e.tensor_tensor(delta3[0][:, :], pb[2][:, 0:256], cp0[:, :], OP.mult), "ve")
            wait("tensor", "ve", dv)
            for j in range(NIT):
                d = delta3[j % 2]
                mmv = op("tensor", lambda e, d=d: e.matmul(pb[3][:, 0:256], gbd0[:, :], d[:, :], start=True, stop=True), "mm")
                op("tensor", lambda e, d=d, j=j: e.matmul(pb[4][:, 0:256], idb[:, :], d[:, :], start=(j == 0), stop=False))
                wait("vector", "mm", mmv)
                dn = delta3[(j + 1) % 2]
                dv = op("vector", lambda e, dn=dn: e.tensor_tensor(dn[:, :], pb[3][:, 0:256], cp0[:, :], OP.mult), "ve", nodrain=True)
                wait("tensor", "ve", dv)
            mmv = op("tensor", lambda e: e.matmul(pb[4][:, 0:256], idb[:, :], delta3[NIT % 2][:, :], start=False, stop=True), "mm")
            wait("scalar", "mm", mmv)
            uv = op("scalar", lambda e: e.copy(u0[:, :], pb[4][:, 0:256]), "sc")
            wait("tensor", "sc", uv)
            op("tensor", lambda e: e.matmul(pb[5][:, 0:256], idb[:, :], rstk[:, 256:512], start=True, stop=True))
            for b in range(B):
                op("tensor", lambda e, b=b, c0=c0: e.matmul(
                    pb[5][32 * b:32 * b + 32, 0:128], k_t(b)[:, c0 + 32:c0 + 64], st_bb(b),
                    start=False, stop=False, tile_position=(0, 32 * b),
                    skip_group_check=True))
            mmv = op("tensor", lambda e: e.matmul(pb[5][:, 0:256], gbdx[:, :], u0[:, :], start=False, stop=True,
                                                  skip_group_check=True), "mm")
            wait("vector", "mm", mmv)
            dv = op("vector", lambda e: e.tensor_tensor(delta3[0][:, :], pb[5][:, 0:256], cp1[:, :], OP.mult), "ve")
            wait("tensor", "ve", dv)
            for j in range(NIT):
                d = delta3[j % 2]
                mmv = op("tensor", lambda e, d=d: e.matmul(pb[3][:, 0:256], gbd1[:, :], d[:, :], start=True, stop=True), "mm")
                op("tensor", lambda e, d=d, j=j: e.matmul(pb[4][:, 0:256], idb[:, :], d[:, :], start=(j == 0), stop=False))
                wait("vector", "mm", mmv)
                dn = delta3[(j + 1) % 2]
                dv = op("vector", lambda e, dn=dn: e.tensor_tensor(dn[:, :], pb[3][:, 0:256], cp1[:, :], OP.mult), "ve", nodrain=True)
                wait("tensor", "ve", dv)
            mmv = op("tensor", lambda e: e.matmul(pb[4][:, 0:256], idb[:, :], delta3[NIT % 2][:, :], start=False, stop=True), "mm")
            wait("scalar", "mm", mmv)
            uv = op("scalar", lambda e: e.copy(u1[:, :], pb[4][:, 0:256]), "sc")
            wait("tensor", "sc", uv)
            wait("vector", "sc", uv)

        def emit_post(tt, b, c):
            tp = TP[b]
            r0, r1 = 32 * b, 32 * b + 32
            c0 = c * CH
            slot = 2 * b + (tt % 2)
            op("tensor", lambda e, b=b, c0=c0: e.matmul(
                pb[2][0:64, 0:128], q_t(b)[:, c0:c0 + CH], st_bb(b), start=True, stop=True, skip_group_check=True))
            op("tensor", lambda e, r0=r0, r1=r1: e.matmul(
                pb[2][0:64, 0:128], gq01[r0:r1, 0:64], u0[r0:r1, 0:128], start=False, stop=False, tile_position=(r0, 0), skip_group_check=True))
            op("tensor", lambda e, r0=r0, r1=r1: e.matmul(
                pb[2][0:64, 0:128], gq01[r0:r1, 64:128], u1[r0:r1, 0:128], start=False, stop=False, tile_position=(r0, 0), skip_group_check=True))
            op("tensor", lambda e, r0=r0, r1=r1: e.matmul(
                pb[2][0:64, 128:256], gq01[r0:r1, 0:64], u0[r0:r1, 128:256], start=True, stop=True, tile_position=(r0, 0), skip_group_check=True))
            op("tensor", lambda e, r0=r0, r1=r1: e.matmul(
                pb[2][0:64, 128:256], gq01[r0:r1, 64:128], u1[r0:r1, 128:256], start=False, stop=False, tile_position=(r0, 0), skip_group_check=True))
            om = bar_mm()
            wait("tensor", "sc", post_sc[0])
            op("tensor", lambda e, r0=r0, r1=r1: e.matmul(
                pb[1][:, 0:128], u0[r0:r1, 0:128], kct[r0:r1, 0:128], start=True, stop=True, tile_position=(r0, 0), skip_group_check=True))
            op("tensor", lambda e, r0=r0, r1=r1: e.matmul(
                pb[1][:, 0:128], u1[r0:r1, 0:128], kct[r0:r1, 128:256], start=False, stop=False, tile_position=(r0, 0), skip_group_check=True))
            op("tensor", lambda e, r0=r0, r1=r1: e.matmul(
                pb[1][:, 128:256], u0[r0:r1, 128:256], kct[r0:r1, 0:128], start=True, stop=True, tile_position=(r0, 0), skip_group_check=True))
            op("tensor", lambda e, r0=r0, r1=r1: e.matmul(
                pb[1][:, 128:256], u1[r0:r1, 128:256], kct[r0:r1, 128:256], start=False, stop=False, tile_position=(r0, 0), skip_group_check=True))
            sm = bar_mm()
            wait("vector", "mm", om)
            if _DBG_DUMP and b == 3:
                op("vector", lambda e: e.tensor_copy(ocdump[0:64, 0:128], pb[2][0:64, 0:128]))
                op("vector", lambda e: e.tensor_copy(ocdump[0:64, 128:256], pb[2][0:64, 128:256]))
            op("vector", lambda e, b=b: e.tensor_tensor(oS[:, :], pb[2][0:64, 0:128], e1t[:, b * 128:(b + 1) * 128], OP.mult))
            op("vector", lambda e, b=b: e.tensor_tensor(oS2[:, :], pb[2][0:64, 128:256], e2t[:, b * 128:(b + 1) * 128], OP.mult))
            op("vector", lambda e: e.tensor_tensor(oS[:, :], oS[:, :], oS2[:, :], OP.add))
            op("vector", lambda e: e.tensor_reduce(ocol[:, 0:1], oS[:, :], mybir.AxisListType.X, OP.add))
            op("vector", lambda e: e.tensor_scalar(ocol[:, 0:1], ocol[:, 0:1], 1.0 / 128.0, 0.0, OP.mult, OP.add))
            ov = op("vector", lambda e: e.tensor_scalar(oS[:, :], oS[:, :], ocol[:, 0:1], 0.0, OP.subtract, OP.add), "ve")
            wait("scalar", "ve", ov)
            sq = op("scalar", lambda e: e.activation(oS2[:, :], oS[:, :], AF.Square), "sc")
            wait("vector", "sc", sq)
            rv = op("vector", lambda e: e.tensor_reduce(ocol[:, 1:2], oS2[:, :], mybir.AxisListType.X, OP.add), "ve")
            wait("scalar", "ve", rv)
            sq = op("scalar", lambda e: e.activation(ocol[:, 1:2], ocol[:, 1:2], AF.Sqrt, bias=epsc[0:64, 0:1], scale=1.0 / 128.0), "sc")
            wait("vector", "sc", sq)
            op("vector", lambda e: e.reciprocal(ocol[:, 1:2], ocol[:, 1:2]))
            onv = op("vector", lambda e: e.tensor_scalar(oS[:, :], oS[:, :], ocol[:, 1:2], 0.0, OP.mult, OP.add), "ve")
            wait("tensor", "ve", onv)
            ot = op("tensor", lambda e: e.matmul(pb[2][0:128, 256:320], oS[:, :], idf[0:64, 0:64], start=True, stop=True, skip_group_check=True), "mm")
            wait("vector", "mm", ot)
            og = op("vector", lambda e: e.tensor_scalar(ognm[:, :], pb[2][0:128, 256:320],
                                                        lng[:, 0:1], lng[:, 1:2], OP.mult, OP.add), "ve")
            gv = op("vector", lambda e, b=b, c0=c0, slot=slot, c=c: e.tensor_tensor(
                send[:, slot * 512 + c * 64:slot * 512 + (c + 1) * 64], ognm[:, :],
                g_t(b)[:, c0:c0 + CH], OP.mult), "ve")
            wait("vector", "mm", sm)
            op("vector", lambda e, b=b: e.tensor_scalar(sscr_state[:, :], st_fb(b), ecolP[:, b:b + 1], 0.0, OP.mult, OP.add))
            op("vector", lambda e, b=b: e.scalar_tensor_tensor(
                out=sscr_state[:, :], in0=pb[1][:, 0:128], scalar=ecolN[:, 2 * b:2 * b + 1],
                in1=sscr_state[:, :], op0=OP.mult, op1=OP.add))
            sv = op("vector", lambda e, b=b: e.scalar_tensor_tensor(
                out=st_fb(b), in0=pb[1][:, 128:256], scalar=ecolN[:, 2 * b + 1:2 * b + 2],
                in1=sscr_state[:, :], op0=OP.mult, op1=OP.add), "ve")
            cb = op("vector", lambda e, b=b: e.tensor_copy(sbf[:, :], st_fb(b)), "ve")
            wait("tensor", "ve", cb)
            st_tr = op("tensor", lambda e: e.matmul(pb[1][:, 256:384], sbf[:, :], idb[:, :], start=True, stop=True, skip_group_check=True), "mm")
            wait("scalar", "mm", st_tr)
            sv2 = op("scalar", lambda e, b=b: e.copy(st_bb(b), pb[1][:, 256:384]), "sc")
            post_sc[0] = sv2


        def emit_exchange(e_):
            wait("gpsimd", "ve")
            for s in range(8):
                dma("gpsimd", ibs[e_].ap()[s, :, :], send[:, s * 512:(s + 1) * 512], "sd", serial=(s == 0))
            wait("gpsimd", "sd")
            op("gpsimd", lambda e, e_=e_: _cc_alltoall(e, ibs[e_].ap().opt(), obs[e_].ap().opt()), "cc")
            wait("sync", "cc", cnt["cc"])
            for r in range(8):
                dma("sync", recv[:, r * 512:(r + 1) * 512], obs[e_].ap()[r, :, :], "rv", serial=(r == 0))

        def emit_p2(e_):
            wait("tensor", "rv", cnt["rv"])
            wait("sync", "ve")
            wait("vector", "w")
            wait("vector", "x")
            wait("vector", "mm")
            xrv = 0
            for oc in range(8):
                wait("sync", "ve", xff_ve[0])
                if oc == 0:
                    wait("sync", "sc", xff_sc[0])
                wait("sync", "mm")
                wait("sync", "gp")
                wv = dma("sync", xf[:, 0:1024], woT[oc, :, :], "x")
                wait("vector", "x", wv)
                cv = op("vector", lambda e: e.tensor_copy(wst[:, 0:1024], xf[:, 0:1024]), "ve")
                xff_ve[0] = cv
                wait("tensor", "ve", cv)
                for r in range(8):
                    mm = op("tensor", lambda e, r=r, oc=oc: e.matmul(
                        pb[5][:, 0:512], wst[:, r * 128:(r + 1) * 128],
                        recv[:, r * 512:(r + 1) * 512], start=(r == 0), stop=(r == 7)),
                        "mm" if r == 7 else None)
                xrv = dma("sync", x1f[:, :], xre[oc, :, e_ * 512:(e_ + 1) * 512], "x")
                wait("vector", "mm", mm)
                wait("vector", "x", xrv)
                rv1 = op("vector", lambda e: e.tensor_tensor(x1f[:, :], pb[5][:, 0:512], x1f[:, :], OP.add), "ve")
                wait("scalar", "ve", rv1)
                sv = op("scalar", lambda e: e.activation(sqs[:, :], x1f[:, :], AF.Square), "sc")
                wait("tensor", "sc", sv)
                op("tensor", lambda e, oc=oc: e.matmul(pb[4][0:1, 0:512], onc[:, :], x1f[:, :],
                                                       start=(oc == 0), stop=(oc == 7)))
                mm = op("tensor", lambda e, oc=oc: e.matmul(pb[3][0:1, 0:512], onc[:, :], sqs[:, :],
                                                            start=(oc == 0), stop=(oc == 7)), "mm")
                wait("gpsimd", "mm", mm)
                dv = dma("gpsimd", x1d.ap()[:, oc * 512:(oc + 1) * 512], x1f[:, :], "gp")
            wait("vector", "mm", mm)
            op("vector", lambda e: e.tensor_copy(rowa[:, :], pb[4][0:1, 0:512]))
            rv2 = op("vector", lambda e: e.tensor_copy(rowb[:, :], pb[3][0:1, 0:512]), "ve")
            emit_ln_rows(rv2)
            wait("tensor", "ve")
            bc = op("tensor", lambda e: e.matmul(pb[0][:, 0:512], onr[:, :], rowa[:, :], start=True, stop=True))
            bc = op("tensor", lambda e: e.matmul(pb[1][:, 0:512], onr[:, :], rowb[:, :], start=True, stop=True), "mm")
            wait("vector", "mm", bc)
            wait("scalar", "gp", cnt["gp"])
            for oc in range(8):
                wait("scalar", "ld")
                xv = dma("scalar", x1f[:, :], x1d.ap()[:, oc * 512:(oc + 1) * 512], "x")
                wait("vector", "x", xv)
                op("vector", lambda e: e.tensor_tensor(x1f[:, :], x1f[:, :], pb[0][:, 0:512], OP.subtract))
                op("vector", lambda e: e.tensor_tensor(x1f[:, :], x1f[:, :], pb[1][:, 0:512], OP.mult))
                cv = op("vector", lambda e, oc=oc: e.tensor_copy(x1b[:, oc * 512:(oc + 1) * 512], x1f[:, :]), "ve")
                wait("scalar", "ve", cv)
                dma("scalar", x1d.ap()[:, oc * 512:(oc + 1) * 512], x1f[:, :], "ld")
            wait("tensor", "ve", cnt["ve"])
            slot_mm = [cnt["mm"], cnt["mm"]]
            sc_done = [cnt["sc"], cnt["sc"]]
            ve_done = [cnt["ve"], cnt["ve"]]
            bkA = [pb[0], pb[2]]
            bkB = [pb[1], pb[5]]
            for oc in range(32):
                par = oc % 2
                sl = par * 4096
                wk = "w" if par == 0 else "x"
                pa, pbb = bkA[par], bkB[par]
                hb = hs0 if par == 0 else hs1
                wait("sync", "mm", slot_mm[par])
                wv = dma("sync", wst[:, sl:sl + 1024], w1T[oc, :, :], wk)
                wv = dma("sync", wst[:, sl + 1024:sl + 2048], w2T[oc, :, :], wk, serial=False)
                wait("tensor", wk, wv)
                wait("tensor", "ve", ve_done[par])
                wait("tensor", "sc", sc_done[par])
                for r in range(8):
                    op("tensor", lambda e, r=r, sl=sl, pa=pa: e.matmul(
                        pa[:, 0:512], wst[:, sl + r * 128:sl + (r + 1) * 128],
                        x1b[:, r * 512:(r + 1) * 512], start=(r == 0), stop=(r == 7)))
                for r in range(8):
                    mm = op("tensor", lambda e, r=r, sl=sl, pbb=pbb: e.matmul(
                        pbb[:, 0:512], wst[:, sl + 1024 + r * 128:sl + 1024 + (r + 1) * 128],
                        x1b[:, r * 512:(r + 1) * 512], start=(r == 0), stop=(r == 7)),
                        "mm" if r == 7 else None)
                slot_mm[par] = mm
                wait("scalar", "mm", mm)
                hv = op("scalar", lambda e, oc=oc, pa=pa, hb=hb: e.activation(hb[:, :], pa[:, 0:512], AF.Silu,
                                                                              scale=fsc[:, oc:oc + 1]), "sc")
                sc_done[par] = hv
                wait("vector", "sc", hv)
                ve_done[par] = op("vector", lambda e, oc=oc, pbb=pbb, hb=hb: e.scalar_tensor_tensor(
                    out=ht[:, oc * 512:(oc + 1) * 512], in0=pbb[:, 0:512],
                    scalar=fsc[:, 32 + oc:33 + oc], in1=hb[:, :], op0=OP.mult, op1=OP.mult), "ve")
            for stage, (wmat, scoff) in enumerate(((h1T, 64), (h2T, 96))):
                slot_mm = [cnt["mm"], cnt["mm"]]
                sc_done = [cnt["sc"], cnt["sc"]]
                gp_done = [cnt["gp"], cnt["gp"]]
                wait("tensor", "ve", cnt["ve"])
                for oc in range(32):
                    par = oc % 2
                    sl = par * 4096
                    wk = "w" if par == 0 else "x"
                    pa = pb[par]
                    hb = hs0 if par == 0 else hs1
                    wait("sync", "mm", slot_mm[par])
                    wv = None
                    for piece in range(2):
                        wv = dma("sync", wst[:, sl + piece * 2048:sl + (piece + 1) * 2048],
                                 wmat[oc, :, piece * 2048:(piece + 1) * 2048], wk, serial=(piece == 0))
                    wait("tensor", wk, wv)
                    wait("tensor", "sc", sc_done[par])
                    for r in range(32):
                        mm = op("tensor", lambda e, r=r, sl=sl, pa=pa: e.matmul(
                            pa[:, 0:512], wst[:, sl + r * 128:sl + (r + 1) * 128],
                            ht[:, r * 512:(r + 1) * 512], start=(r == 0), stop=(r == 31)),
                            "mm" if r == 31 else None)
                    slot_mm[par] = mm
                    wait("scalar", "mm", mm)
                    wait("scalar", "gp", gp_done[par])
                    hv = op("scalar", lambda e, oc=oc, scoff=scoff, hb=hb, pa=pa: e.activation(
                        hb[:, :], pa[:, 0:512], AF.Gelu, scale=fsc[:, scoff + oc:scoff + oc + 1]), "sc")
                    sc_done[par] = hv
                    wait("gpsimd", "sc", hv)
                    gp_done[par] = dma("gpsimd", hd.ap()[:, oc * 512:(oc + 1) * 512], hb[:, :], "gp")
                wait("scalar", "gp", cnt["gp"])
                wait("scalar", "mm")
                hv = 0
                for piece in range(8):
                    hv = dma("scalar", ht[:, piece * 2048:(piece + 1) * 2048],
                             hd.ap()[:, piece * 2048:(piece + 1) * 2048], "x", serial=(piece == 0))
                wait("tensor", "x", hv)
            slot_mm = [cnt["mm"], cnt["mm"]]
            ve_done = [cnt["ve"], cnt["ve"]]
            gp_last = cnt["gp"]
            for oc in range(8):
                par = oc % 2
                sl = par * 4096
                wk = "w" if par == 0 else "x"
                pa = pb[par]
                wait("sync", "mm", slot_mm[par])
                wv = None
                for piece in range(2):
                    wv = dma("sync", wst[:, sl + piece * 2048:sl + (piece + 1) * 2048],
                             w3T[oc, :, piece * 2048:(piece + 1) * 2048], wk, serial=(piece == 0))
                wait("tensor", wk, wv)
                if oc < 2:
                    wait("tensor", "sc")
                wait("tensor", "ve", ve_done[par])
                for r in range(32):
                    mm = op("tensor", lambda e, r=r, sl=sl, pa=pa: e.matmul(
                        pa[:, 0:512], wst[:, sl + r * 128:sl + (r + 1) * 128],
                        ht[:, r * 512:(r + 1) * 512], start=(r == 0), stop=(r == 31)),
                        "mm" if r == 31 else None)
                slot_mm[par] = mm
                wait("scalar", "ld")
                xv = dma("scalar", x1f[:, :], x1d.ap()[:, oc * 512:(oc + 1) * 512], "x")
                wait("vector", "mm", mm)
                wait("vector", "x", xv)
                wait("vector", "gp", gp_last)
                rv1 = op("vector", lambda e, oc=oc, pa=pa: e.scalar_tensor_tensor(
                    out=r2f[:, :], in0=pa[:, 0:512], scalar=fsc[:, 128 + oc:129 + oc],
                    in1=x1f[:, :], op0=OP.mult, op1=OP.add), "ve")
                ve_done[par] = rv1
                wait("scalar", "ve", rv1)
                sv = op("scalar", lambda e: e.activation(sqs[:, :], r2f[:, :], AF.Square), "sc")
                wait("tensor", "sc", sv)
                op("tensor", lambda e, oc=oc: e.matmul(pb[4][0:1, 0:512], onc[:, :], r2f[:, :],
                                                       start=(oc == 0), stop=(oc == 7)))
                mm = op("tensor", lambda e, oc=oc: e.matmul(pb[3][0:1, 0:512], onc[:, :], sqs[:, :],
                                                            start=(oc == 0), stop=(oc == 7)), "mm")
                wait("gpsimd", "mm", mm)
                gp_last = dma("gpsimd", r2d.ap()[:, oc * 512:(oc + 1) * 512], r2f[:, :], "gp")
            wait("vector", "mm", mm)
            op("vector", lambda e: e.tensor_copy(rowa[:, :], pb[4][0:1, 0:512]))
            rv2 = op("vector", lambda e: e.tensor_copy(rowb[:, :], pb[3][0:1, 0:512]), "ve")
            emit_ln_rows(rv2)
            wait("tensor", "ve")
            bc = op("tensor", lambda e: e.matmul(pb[0][:, 0:512], onr[:, :], rowa[:, :], start=True, stop=True))
            bc = op("tensor", lambda e: e.matmul(pb[1][:, 0:512], onr[:, :], rowb[:, :], start=True, stop=True), "mm")
            wait("vector", "mm", bc)
            wait("vector", "gp", cnt["gp"])
            wait("scalar", "gp")
            for oc in range(8):
                wait("scalar", "ve")
                xv = dma("scalar", r2f[:, :], r2d.ap()[:, oc * 512:(oc + 1) * 512], "x")
                wait("vector", "x", xv)
                wait("vector", "st")
                op("vector", lambda e: e.tensor_tensor(outf[:, :], r2f[:, :], pb[0][:, 0:512], OP.subtract))
                ov = op("vector", lambda e: e.tensor_tensor(outf[:, :], outf[:, :], pb[1][:, 0:512], OP.mult), "ve")
                wait("sync", "ve", ov)
                dma("sync", outp[oc, :, e_ * 512:(e_ + 1) * 512], outf[:, :], "st")

        def emit_ln_rows(rv2):
            wait("scalar", "ve", rv2)
            op("scalar", lambda e: e.mul(rowa[:, :], rowa[:, :], 1.0 / 1024.0))
            op("scalar", lambda e: e.mul(rowb[:, :], rowb[:, :], 1.0 / 1024.0))
            sv = op("scalar", lambda e: e.activation(sqs[0:1, 0:512], rowa[:, :], AF.Square), "sc")
            wait("vector", "sc", sv)
            vv = op("vector", lambda e: e.tensor_tensor(rowb[:, :], rowb[:, :], sqs[0:1, 0:512], OP.subtract), "ve")
            wait("scalar", "ve", vv)
            sv = op("scalar", lambda e: e.activation(rowb[:, :], rowb[:, :], AF.Sqrt, bias=epsc[0:1, 0:1]), "sc")
            wait("vector", "sc", sv)
            op("vector", lambda e: e.reciprocal(rowb[:, :], rowb[:, :]), "ve")

        ntt = _DBG_TILES if _DBG_TILES is not None else NTT
        for tt in range(ntt):
            if tt >= 2:
                wait("vector", "sd", cnt["sd"])
            for b in range(B):
                emit_proj(tt, b)
            nch = _DBG_CHUNKS if _DBG_CHUNKS is not None else NCH
            for c in range(nch):
                if _DBG_STAGE >= 1:
                    emit_prep_all(tt, c)
                if _DBG_STAGE >= 2:
                    emit_chain(tt, c)
                if _DBG_STAGE >= 3:
                    for b in range(B):
                        emit_post(tt, b, c)
            if tt % 2 == 1 and _DBG_P2:
                emit_exchange(tt // 2)
                emit_p2(tt // 2)

        if _DBG_DUMP:
            wait("vector", "mm")
            op("vector", lambda e: e.tensor_copy(p3dump[:, :], pb[3][:, 0:256]), "ve")
            wait("sync", "ve")
            wait("sync", "sc")
            dma("sync", dbgo[:, :], send[:, :], "st")
            dma("sync", dbgs[:, :], st_f[:, :], "st")
            dma("sync", dbgq[:, :], qT_[:, :], "st")
            dma("sync", dbgk[:, :], kT_[:, :], "st")
            dma("sync", dbgv[:, :], vT_[:, :], "st")
            dma("sync", dbgsp[:, :], spT_[:, :], "st")
            dma("sync", dbgln[:, :], lnT_[:, :], "st")
            dma("sync", dbgbe[:, :], beta_[:, :], "st")
            dma("sync", dbgu[:, 0:256], u0[:, :], "st")
            dma("sync", dbgu[:, 256:512], u1[:, :], "st")
            dma("sync", dbgos[:, 0:128], oS[:, :], "st")
            dma("sync", dbgos[:, 128:256], oS2[:, :], "st")
            dma("sync", dbgoc[:, :], ocdump[:, :], "st")
            dma("sync", dbggq[:, 0:128], gq01[:, :], "st")
            dma("sync", dbggq[:, 128:256], gbd0[:, :], "st")
            dma("sync", dbgp3[:, :], p3dump[:, :], "st")
            dma("sync", dbgmk[:, 0:32], mks[:, :], "st")
            dma("sync", dbgmk[:, 32:96], mg0[:, :], "st")
            dma("sync", dbge[:, 0:512], e1t[:, :], "st")
            dma("sync", dbge[:, 512:1024], e2t[:, :], "st")
        wait("sync", "st", cnt["st"])

        blk = es.enter_context(nc.Block())

        @blk.sync
        def _(e):
            for f in prog["sync"]:
                f(e)

        @blk.tensor
        def _(e):
            for f in prog["tensor"]:
                f(e)

        @blk.vector
        def _(e):
            for f in prog["vector"]:
                f(e)

        @blk.scalar
        def _(e):
            for f in prog["scalar"]:
                f(e)

        @blk.gpsimd
        def _(e):
            for f in prog["gpsimd"]:
                f(e)

    return nc



_NC_CACHE = {}


def _consts():
    identb = np.eye(128, dtype=ml_dtypes.bfloat16)
    identf = np.eye(128, dtype=np.float32)
    onecf = np.ones((128, 1), np.float32)
    onerf = np.ones((1, 128), np.float32)
    r = np.arange(32)
    i32 = np.arange(32)
    m = -(r[:, None] < i32[None, :]).astype(np.float32)
    mk32q = np.tile(m, (4, 1))
    i64 = np.arange(64)
    mq0 = np.tile(-(r[:, None] <= i64[None, :]).astype(np.float32), (4, 1))
    mq1 = np.tile(-((r[:, None] + 32) <= i64[None, :]).astype(np.float32), (4, 1))
    return identb, identf, onecf, onerf, mk32q, mq0, mq1


def kernel(**inp):
    x = np.asarray(inp["L0_attn_x"], np.float32)
    xfm = np.ascontiguousarray(x.transpose(0, 2, 1)).reshape(B, 8, 128, L)

    def colslice(W, h):
        return np.ascontiguousarray(W[:, h * 128:(h + 1) * 128].reshape(8, 128, 128))

    Wq, Wk, Wv, Wa, Wg = (np.asarray(inp[f"L0_attn_W{n}"], np.float32) for n in "qkvag")
    Wb = np.asarray(inp["L0_attn_Wb"], np.float32)
    Wo = np.asarray(inp["L0_attn_Wo"], np.float32)
    ba = np.asarray(inp["L0_attn_ba"], np.float32)
    bb = np.asarray(inp["L0_attn_bb"], np.float32)
    cqw = np.asarray(inp["L0_attn_cqw"], np.float32)
    ckw = np.asarray(inp["L0_attn_ckw"], np.float32)
    cvw = np.asarray(inp["L0_attn_cvw"], np.float32)
    cqb = np.asarray(inp["L0_attn_cqb"], np.float32)
    ckb = np.asarray(inp["L0_attn_ckb"], np.float32)
    cvb = np.asarray(inp["L0_attn_cvb"], np.float32)
    lngv = np.asarray(inp["L0_attn_ln_g"], np.float32)
    lnbv = np.asarray(inp["L0_attn_ln_b"], np.float32)

    woTh = np.ascontiguousarray(Wo.reshape(8, 128, 8, 128).transpose(2, 1, 0, 3)).reshape(8, 128, 8 * 128)

    def ffn_lhsT(W, no, nk):
        Wr = W.reshape(no, 128, nk, 128)
        return np.ascontiguousarray(Wr.transpose(0, 3, 2, 1)).reshape(no, 128, nk * 128)

    w1w = np.asarray(inp["L0_ffn_w1_w"], np.float32)
    w2w = np.asarray(inp["L0_ffn_w2_w"], np.float32)
    h1w = np.asarray(inp["L0_ffn_h1_w"], np.float32)
    h2w = np.asarray(inp["L0_ffn_h2_w"], np.float32)
    w3w = np.asarray(inp["L0_ffn_w3_w"], np.float32)
    w1Tb = ffn_lhsT(w1w, 32, 8).astype(ml_dtypes.bfloat16)
    w2Tb = ffn_lhsT(w2w, 32, 8).astype(ml_dtypes.bfloat16)
    h1Tb = ffn_lhsT(h1w, 32, 32).astype(ml_dtypes.bfloat16)
    h2Tb = ffn_lhsT(h2w, 32, 32).astype(ml_dtypes.bfloat16)
    w3Tb = ffn_lhsT(w3w, 8, 32).astype(ml_dtypes.bfloat16)
    fsclv = np.zeros((128, 5 * 32), np.float32)
    for fi, sname in enumerate(("w1_s", "w2_s", "h1_s", "h2_s", "w3_s")):
        sv = np.asarray(inp["L0_ffn_" + sname], np.float32)
        nch = sv.shape[0] // 128
        fsclv[:, fi * 32:fi * 32 + nch] = sv.reshape(nch, 128).T

    identb, identf, onecf, onerf, mk32q, mq0, mq1 = _consts()

    if "nc" not in _NC_CACHE:
        _NC_CACHE["nc"] = build_nc()
    ncp = _NC_CACHE["nc"]

    in_maps = []
    for s in range(8):
        h = s
        mb, half = s // 2, s % 2
        xres = np.empty((8, 128, NRND * 512), np.float32)
        for e in range(NRND):
            xres[:, :, e * 512:(e + 1) * 512] = xfm[mb, :, :, e * 1024 + half * 512: e * 1024 + half * 512 + 512]
        w5v = np.concatenate([colslice(W, h) for W in (Wq, Wk, Wv, Wa, Wg)], 0)
        w5v = np.ascontiguousarray(w5v.transpose(1, 0, 2)).reshape(128, 40 * 128)
        convwv = np.stack([cqw[h * 128:(h + 1) * 128], ckw[h * 128:(h + 1) * 128],
                           cvw[h * 128:(h + 1) * 128]], 0)
        convwv = np.ascontiguousarray(convwv.transpose(1, 0, 2)).reshape(128, 12)
        convbv = np.stack([cqb[h * 128:(h + 1) * 128], ckb[h * 128:(h + 1) * 128],
                           cvb[h * 128:(h + 1) * 128]], 1)
        in_maps.append({
            "xfm": xfm, "xre": xres, "w5": w5v,
            "wbcol": np.ascontiguousarray(Wb[:, h].reshape(8, 128).T),
            "convw": convwv, "convb": np.ascontiguousarray(convbv),
            "bneg": ba[h * 128:(h + 1) * 128].reshape(128, 1).copy(),
            "bbv": bb[h].reshape(1, 1),
            "lngb": np.stack([lngv, lnbv], 1),
            "woT": woTh, "w1T": w1Tb, "w2T": w2Tb, "h1T": h1Tb, "h2T": h2Tb, "w3T": w3Tb,
            "fscl": fsclv, "identb": identb, "identf": identf,
            "onecf": onecf, "onerf": onerf, "mk32q": mk32q, "mq0": mq0, "mq1": mq1,
        })

    res = run_bass_kernel_spmd(ncp, in_maps, core_ids=list(range(8)))
    out = np.empty((B, L, D), np.float32)
    for s in range(8):
        mb, half = s // 2, s % 2
        o = res.results[s]["outp"]
        for e in range(NRND):
            seg = o[:, :, e * 512:(e + 1) * 512]
            t0 = e * 1024 + half * 512
            out[mb, t0:t0 + 512, :] = seg.transpose(2, 0, 1).reshape(512, D)
    return out
